# version 17
# baseline (speedup 1.0000x reference)
import numpy as np
from contextlib import ExitStack
import ml_dtypes
import concourse.bass as bass
import concourse.mybir as mybir
from concourse.bass_utils import run_bass_kernel_spmd

F32 = mybir.dt.float32
BF16 = mybir.dt.bfloat16
AF = mybir.ActivationFunctionType
ALU = mybir.AluOpType
AX = mybir.AxisListType
BF = ml_dtypes.bfloat16

T = 2048
CH = 512
NCH = T // CH
CT = 8
DM = 1024

ENGINES = ("sync", "tensor", "vector", "scalar", "gpsimd")


class P:
    def __init__(self, nc, ctx):
        self.nc = nc
        self.ctx = ctx
        self.q = {e: [] for e in ENGINES}
        self.sems = {}
        self.cnt = {}
        self.lastw = {e: {} for e in ENGINES}
        for s in ("pe", "ve", "se", "ge"):
            self.sems[s] = ctx.enter_context(nc.semaphore(s))
            self.cnt[s] = 0
        self.pending_barrier = {e: None for e in ENGINES}

    def dsem(self, name):
        if name not in self.sems:
            self.sems[name] = self.ctx.enter_context(self.nc.semaphore(name))
            self.cnt[name] = 0
        return name

    def barrier(self):
        snap = [(s, v) for s, v in self.cnt.items() if v > 0]
        for e in ENGINES:
            self.pending_barrier[e] = snap

    def on(self, eng, fn, deps=(), inc=None, n=1):
        if self.pending_barrier[eng]:
            deps = list(self.pending_barrier[eng]) + list(deps)
            self.pending_barrier[eng] = None
        w = []
        for d in deps:
            if d is None:
                continue
            s, v = d
            if self.lastw[eng].get(s, 0) >= v:
                continue
            self.lastw[eng][s] = v
            w.append((s, v))
        h = None
        if inc is not None:
            self.cnt[inc] += n
            h = (inc, self.cnt[inc])
        sems = self.sems

        def run(eo, w=tuple(w), fn=fn, inc=inc, n=n):
            for (s, v) in w:
                eo.wait_ge(sems[s], v)
            r = fn(eo)
            if inc is not None:
                r.then_inc(sems[inc], n)

        self.q[eng].append(run)
        return h

    def load(self, dst, src, deps=(), stream="ld"):
        return self.on("sync", lambda e, d=dst, s=src: e.dma_start(out=d, in_=s),
                       deps=deps, inc=self.dsem("L_" + stream), n=16)

    def store(self, dst, src, deps=(), stream="st"):
        return self.on("gpsimd", lambda e, d=dst, s=src: e.dma_start(out=d, in_=s),
                       deps=deps, inc=self.dsem("S_" + stream), n=16)

    def mm(self, out, lhsT, rhs, start, stop, deps=(), inc=False):
        return self.on(
            "tensor",
            lambda e, o=out, l=lhsT, r=rhs, st=start, sp=stop:
                e.matmul(o, l, r, start=st, stop=sp),
            deps=deps, inc="pe" if inc else None)

    def mmgrp(self, out, steps, deps=()):
        n = len(steps)
        h = None
        for i, (l, r) in enumerate(steps):
            h = self.mm(out, l, r, i == 0, i == n - 1,
                        deps=deps if i == 0 else (), inc=(i == n - 1))
        return h

    def v(self, fn, deps=()):
        return self.on("vector", fn, deps=deps, inc="ve")

    def s(self, fn, deps=()):
        return self.on("scalar", fn, deps=deps, inc="se")


class Buf:
    def __init__(self, ap):
        self.ap = ap
        self.writer = None
        self.users = []

    def wd(self):
        d = list(self.users)
        if self.writer is not None:
            d.append(self.writer)
        self.users = []
        return d

    def rd(self):
        return [self.writer] if self.writer is not None else []

    def w(self, h):
        self.writer = h
        return h

    def r(self, h):
        if h is not None:
            self.users.append(h)
        return h


class Banks:
    def __init__(self, tensors):
        self.t = tensors
        self.st = [{"w": None, "r": []} for _ in tensors]

    def wdeps(self, i):
        d = list(self.st[i]["r"])
        if self.st[i]["w"] is not None:
            d.append(self.st[i]["w"])
        self.st[i]["r"] = []
        return d

    def set_w(self, i, h):
        self.st[i]["w"] = h

    def rdeps(self, i):
        return [self.st[i]["w"]] if self.st[i]["w"] is not None else []

    def add_r(self, i, h):
        if h is not None:
            self.st[i]["r"].append(h)


class Arena:
    def __init__(self, tens, cols):
        self.t = tens
        self.cols = cols
        self.off = 0

    def reset(self):
        self.off = 0

    def bf16(self, n):
        o = self.off
        assert o + n <= self.cols, f"arena overflow {o}+{n}>{self.cols}"
        self.off += n
        return Buf(self.t[:, o:o + n])

    def f32(self, n):
        if self.off % 2:
            self.off += 1
        o = self.off
        assert o + 2 * n <= self.cols, "arena overflow f32"
        self.off += 2 * n
        return Buf(self.t[:, o:o + 2 * n].bitcast(F32))

    def bf16_at(self, o, n):
        assert o + n <= self.cols
        return Buf(self.t[:, o:o + n])

    def f32_at(self, o, n):
        assert o % 2 == 0 and o + 2 * n <= self.cols
        return Buf(self.t[:, o:o + 2 * n].bitcast(F32))


def fetch_w(env, dst_buf, src_ap, dst_off=0, cols=None):
    p = env["p"]
    i = env["WST_i"][0]
    env["WST_i"][0] = 1 - i
    wst = env["WSTb"][i]
    if cols is None:
        n = 1
        for d in src_ap.shape:
            n *= d
        cols = n // 128
    assert cols <= 1024, cols
    stg = wst.ap[:, 0:cols]
    if len(src_ap.shape) == 3:
        stg = stg.rearrange("p (a n) -> p a n", a=src_ap.shape[1])
    ld = p.load(stg, src_ap, deps=wst.wd(), stream=f"wst{i}")
    wst.w(ld)
    dst = dst_buf.ap[:, dst_off:dst_off + cols]
    cv = p.on("gpsimd", lambda e, d=dst, s=wst.ap[:, 0:cols]:
              e.tensor_copy(d, s), deps=wst.rd() + dst_buf.wd(), inc="ge")
    wst.r(cv)
    dst_buf.w(cv)
    return cv


class Pool:
    def __init__(self, bufs):
        self.bufs = bufs
        self.i = 0

    def next(self):
        b = self.bufs[self.i]
        self.i = (self.i + 1) % len(self.bufs)
        return b


def cross(tos, froms):
    pend = []
    for f in froms:
        pend += f.wd()
    for t in tos:
        t.users += pend


def build_program(phases, arrays):
    nc = bass.Bass("TRN2", target_bir_lowering=False)
    ctx = ExitStack()
    used = []
    dcache = {}

    def din(name):
        if name in dcache:
            return dcache[name]
        a = arrays[name]
        dt = BF16 if a.dtype == BF else F32
        used.append(name)
        ap = nc.dram_tensor(name, list(a.shape), dt, kind="ExternalInput").ap()
        dcache[name] = ap
        return ap

    x_in = nc.dram_tensor("x_in", [DM, T], F32, kind="ExternalInput").ap()
    used.append("x_in")
    x_out = nc.dram_tensor("x_out", [DM, T], F32, kind="ExternalOutput").ap()

    dsc = {}

    def dram_scratch(name, shape, dt):
        if name not in dsc:
            dsc[name] = nc.dram_tensor(name, list(shape), dt, kind="Internal").ap()
        return dsc[name]

    with ctx:
        p = P(nc, ctx)
        E = ctx.enter_context
        R_t = E(nc.sbuf_tensor("Rt", [128, DM * T // 128], F32))
        XL_t = E(nc.sbuf_tensor("XLt", [128, DM * T // 128], BF16))
        AR_t = E(nc.sbuf_tensor("ARt", [128, 26752], BF16))
        WST_t = E(nc.sbuf_tensor("WSTt", [128, 2048], F32))
        WB_t = E(nc.sbuf_tensor("WBt", [128, 4096], BF16))
        SC1_t = E(nc.sbuf_tensor("SC1t", [128, 512], F32))
        SC2_t = E(nc.sbuf_tensor("SC2t", [128, 512], F32))
        SCB_t = E(nc.sbuf_tensor("SCBt", [128, 2048], BF16))
        SLOT_t = E(nc.sbuf_tensor("SLOTt", [128, 3072], BF16))
        ROWA_t = E(nc.sbuf_tensor("ROWAt", [1, 512], F32))
        ROWB_t = E(nc.sbuf_tensor("ROWBt", [1, 512], F32))
        ROWC_t = E(nc.sbuf_tensor("ROWCt", [1, 512], F32))
        COLS_t = E(nc.sbuf_tensor("COLSt", [128, 192], F32))
        E16_t = E(nc.sbuf_tensor("E16t", [16, 2048], BF16))
        IDT_t = E(nc.sbuf_tensor("IDTt", [128, 128], F32))
        HC_t = E(nc.sbuf_tensor("HCt", [128, 256], F32))
        AS_t = E(nc.sbuf_tensor("ASt", [128, 256], F32))
        ONEC_t = E(nc.sbuf_tensor("ONECt", [128, 1], F32))
        ONER_t = E(nc.sbuf_tensor("ONERt", [1, 128], F32))
        EPS_t = E(nc.sbuf_tensor("EPSt", [1, 2], F32))
        N10_t = E(nc.sbuf_tensor("N10t", [128, 1], F32))
        CM1_t = E(nc.sbuf_tensor("CM1t", [128, 1], F32))
        pb = [E(nc.psum_tensor(f"pb{i}", [128, 512], F32)) for i in range(8)]

        env = dict(nc=nc, p=p, bk=Banks(pb), pb=pb, arena=Arena(AR_t, 26752),
                   R_t=R_t,
                   RB=[Buf(R_t[:, c * T:(c + 1) * T]) for c in range(CT)],
                   XLB=[Buf(XL_t[:, c * T:(c + 1) * T]) for c in range(CT)],
                   WSTb=[Buf(WST_t[:, 0:1024]), Buf(WST_t[:, 1024:2048])],
                   WST_i=[0], WB=WB_t,
                   SC1=Buf(SC1_t[:]), SC2=Buf(SC2_t[:]), SCB=Buf(SCB_t[:]),
                   SLOT=[Buf(SLOT_t[:, i * 512:(i + 1) * 512]) for i in range(6)],
                   ROWA=Buf(ROWA_t[:]), ROWB=Buf(ROWB_t[:]), ROWC=Buf(ROWC_t[:]),
                   COLS=Buf(COLS_t[:]), E16=Buf(E16_t[:]), IDT=Buf(IDT_t[:]),
                   HC=Buf(HC_t[:]), AS=Buf(AS_t[:]),
                   ONEC=ONEC_t, ONER=ONER_t,
                   din=din, dram_scratch=dram_scratch)
        bk, arena = env["bk"], env["arena"]

        h1 = p.v(lambda e: e.memset(ONEC_t[:], 1.0))
        h2 = p.v(lambda e: e.memset(ONER_t[:], 1.0))
        h3 = p.v(lambda e: e.memset(EPS_t[0:1, 0:1], 1e-5))
        h4 = p.v(lambda e: e.memset(EPS_t[0:1, 1:2], 1e-12))
        h5 = p.v(lambda e: e.memset(N10_t[:], -10.0))
        h6 = p.v(lambda e: e.memset(CM1_t[:], -1.0))
        nc.const_aps.register(F32, -1.0, CM1_t[:])
        env["ones_h"] = [h1, h2, h3, h4, h5, h6]
        env["N10"] = N10_t[:]
        env["EPS5"] = EPS_t[0:1, 0:1]
        env["EPS12"] = EPS_t[0:1, 1:2]
        env["E16"].w(p.load(E16_t[:], din("c_e16"), stream="e16"))
        env["IDT"].w(p.load(IDT_t[:], din("c_id128"), stream="idt"))

        rl = []
        for c in range(CT):
            rl.append(p.load(env["RB"][c].ap, x_in[c * 128:(c + 1) * 128, :],
                             stream="rb"))
        for c in range(CT):
            env["RB"][c].w(rl[-1])

        for ph in phases:
            p.barrier()
            arena.reset()
            {"attn": emit_attn, "moe": emit_moe,
             "l1ffn": emit_l1ffn, "mamba": emit_mamba}[ph["kind"]](env, ph)

        p.barrier()
        sh = []
        for c in range(CT):
            sh.append(p.store(x_out[c * 128:(c + 1) * 128, :], env["RB"][c].ap,
                              deps=env["RB"][c].rd(), stream="xout"))
        p.on("gpsimd", lambda e: e.nop(), deps=[sh[-1]])

        with nc.Block() as block:
            @block.sync
            def _(eo):
                for f in p.q["sync"]:
                    f(eo)

            @block.tensor
            def _(eo):
                for f in p.q["tensor"]:
                    f(eo)

            @block.vector
            def _(eo):
                for f in p.q["vector"]:
                    f(eo)

            @block.scalar
            def _(eo):
                for f in p.q["scalar"]:
                    f(eo)

            @block.gpsimd
            def _(eo):
                for f in p.q["gpsimd"]:
                    f(eo)

    return nc, used


def emit_ln(env):
    p, bk, pb = env["p"], env["bk"], env["pb"]
    RB, XLB = env["RB"], env["XLB"]
    SC1, SC2 = env["SC1"], env["SC2"]
    ROWA, ROWB, ROWC = env["ROWA"], env["ROWB"], env["ROWC"]
    ONEC, ONER = env["ONEC"], env["ONER"]
    base = list(env["ones_h"])
    for ch in range(NCH):
        w6 = bk.wdeps(6)
        w5 = bk.wdeps(5)
        hsum = hsq = None
        for c in range(CT):
            rt = RB[c].ap[:, ch * CH:(ch + 1) * CH]
            hsum = p.mm(pb[6][0:1, :], ONEC[:], rt, c == 0, c == CT - 1,
                        deps=(base + w6 + RB[c].rd()) if c == 0 else RB[c].rd(),
                        inc=True)
            RB[c].r(hsum)
            sq_b = SC1 if c % 2 == 0 else SC2
            hsqa = p.s(lambda e, o=sq_b.ap, i=rt: e.activation(o, i, AF.Square),
                       deps=sq_b.wd() + RB[c].rd() + (w5 if c == 0 else []))
            sq_b.w(hsqa)
            RB[c].r(hsqa)
            hsq = p.mm(pb[5][0:1, :], ONEC[:], sq_b.ap, c == 0, c == CT - 1,
                       deps=sq_b.rd(), inc=True)
            sq_b.r(hsq)
        bk.set_w(6, hsum)
        bk.set_w(5, hsq)
        SC1.r(hsq)
        SC2.r(hsq)
        for c in range(CT):
            RB[c].r(hsq)
        ha = p.v(lambda e: e.tensor_scalar(ROWA.ap, pb[6][0:1, :], 1.0 / DM, None, ALU.mult),
                 deps=bk.rdeps(6) + ROWA.wd())
        ROWA.w(ha)
        bk.add_r(6, ha)
        hb = p.v(lambda e: e.tensor_scalar(ROWB.ap, pb[5][0:1, :], 1.0 / DM, None, ALU.mult),
                 deps=bk.rdeps(5) + ROWB.wd())
        ROWB.w(hb)
        bk.add_r(5, hb)
        hc = p.v(lambda e: e.tensor_tensor(ROWC.ap, ROWA.ap, ROWA.ap, ALU.mult),
                 deps=ROWA.rd() + ROWC.wd())
        ROWA.r(hc)
        ROWC.w(hc)
        hd = p.v(lambda e: e.tensor_tensor(ROWC.ap, ROWB.ap, ROWC.ap, ALU.subtract),
                 deps=ROWB.rd() + ROWC.rd())
        ROWB.r(hd)
        ROWC.w(hd)
        he = p.s(lambda e: e.activation(ROWC.ap, ROWC.ap, AF.Sqrt, bias=env["EPS5"]),
                 deps=ROWC.rd() + env["ones_h"])
        ROWC.w(he)
        hf = p.v(lambda e: e.reciprocal(ROWC.ap, ROWC.ap), deps=ROWC.rd())
        ROWC.w(hf)
        w7 = bk.wdeps(7)
        hm = p.mm(pb[7][:, :], ONER[:], ROWA.ap, True, True,
                  deps=w7 + ROWA.rd(), inc=True)
        ROWA.r(hm)
        bk.set_w(7, hm)
        w4 = bk.wdeps(4)
        hr = p.mm(pb[4][:, :], ONER[:], ROWC.ap, True, True,
                  deps=w4 + ROWC.rd(), inc=True)
        ROWC.r(hr)
        bk.set_w(4, hr)
        for c in range(CT):
            rt = RB[c].ap[:, ch * CH:(ch + 1) * CH]
            xt = XLB[c].ap[:, ch * CH:(ch + 1) * CH]
            sb = SC1 if c % 2 == 0 else SC2
            h1_ = p.v(lambda e, o=sb.ap, i=rt: e.tensor_tensor(o, i, pb[7][:, :], ALU.subtract),
                      deps=bk.rdeps(7) + sb.wd() + RB[c].rd())
            sb.w(h1_)
            RB[c].r(h1_)
            bk.add_r(7, h1_)
            h2_ = p.v(lambda e, o=xt, i=sb.ap: e.tensor_tensor(o, i, pb[4][:, :], ALU.mult),
                      deps=bk.rdeps(4) + sb.rd() + XLB[c].wd())
            sb.r(h2_)
            bk.add_r(4, h2_)
            XLB[c].w(h2_)


def _xl_rd(env):
    return [h for c in range(CT) for h in env["XLB"][c].rd()]


def emit_attn(env, ph):
    p, bk, pb, arena = env["p"], env["bk"], env["pb"], env["arena"]
    RB, XLB = env["RB"], env["XLB"]
    SC1, SC2, SCB = env["SC1"], env["SC2"], env["SCB"]
    ROWA, ROWB, ONER = env["ROWA"], env["ROWB"], env["ONER"]
    din, dram_scratch = env["din"], env["dram_scratch"]
    WB_t = env["WB"]
    COLS = env["COLS"]

    emit_ln(env)

    tag = ph["tag"]
    wq = din(tag + "_wq"); wk = din(tag + "_wk")
    wv = din(tag + "_wv"); wo = din(tag + "_wo")
    rope = ph.get("rope", False)
    scale = ph["scale"]
    has_bias = ph.get("bias", False)
    if has_bias:
        COLS.w(p.load(COLS.ap[:, 0:CT], din(tag + "_bo"), deps=COLS.wd(), stream="cols"))

    QT = arena.bf16(T)
    KT = arena.bf16(T)
    VB = arena.bf16(16 * 132)
    PRB2 = [arena.bf16(16 * CH), arena.bf16(16 * CH)]
    AOS = Buf(arena.t[:, 0:4096])
    WP = Pool([Buf(WB_t[:, i * 1024:(i + 1) * 1024]) for i in range(4)])
    if rope:
        COST = arena.bf16(T)
        SINT = arena.bf16(T)
        PERM = Buf(env["COLS"].ap[:, 64:128].bitcast(BF16))
        COST.w(p.load(COST.ap, din(tag + "_cos"), stream="cost"))
        SINT.w(p.load(SINT.ap, din(tag + "_sin"), stream="sint"))
        PERM.w(p.load(PERM.ap, din(tag + "_perm"), deps=PERM.wd(), stream="perm"))

    ao = dram_scratch(tag + "_ao", [DM, T], BF16)
    vview = VB.ap.rearrange("p (t h c) -> p t h c", t=16, h=2)
    VB.w(p.v(lambda e: e.memset(vview[:, :, :, 64:65], 1.0), deps=VB.wd()))

    ao_done = []
    for hg in range(CT):
        for (W, DST) in ((wq, QT), (wk, KT)):
            WQB = WP.next()
            fetch_w(env, WQB, W[:, hg * 1024:(hg + 1) * 1024])
            for ch in range(NCH):
                b = ch % 2
                wd = bk.wdeps(b)
                steps = [(WQB.ap[:, c * 128:(c + 1) * 128],
                          XLB[c].ap[:, ch * CH:(ch + 1) * CH]) for c in range(CT)]
                hmm = p.mmgrp(pb[b][:, :], steps,
                              deps=wd + WQB.rd() + _xl_rd(env) + DST.wd())
                WQB.r(hmm)
                for c in range(CT):
                    XLB[c].r(hmm)
                bk.set_w(b, hmm)
                hev = p.v(lambda e, o=DST.ap[:, ch * CH:(ch + 1) * CH], b_=b:
                          e.tensor_copy(o, pb[b_][:, :]), deps=bk.rdeps(b))
                bk.add_r(b, hev)
                DST.w(hev)
        if rope:
            SLOT = env["SLOT"]
            ru_ = 0
            for DST in (QT, KT):
                for ch in range(NCH):
                    b = 2 + ch % 2
                    wd = bk.wdeps(b)
                    sl = DST.ap[:, ch * CH:(ch + 1) * CH]
                    s1 = SLOT[(2 * ru_) % 6]
                    s2 = SLOT[(2 * ru_ + 1) % 6]
                    ru_ += 1
                    hrot = p.mm(pb[b][:, :], PERM.ap, sl, True, True,
                                deps=wd + PERM.rd() + DST.rd(), inc=True)
                    PERM.r(hrot)
                    DST.r(hrot)
                    bk.set_w(b, hrot)
                    ha = p.v(lambda e, i=sl, c=COST.ap[:, ch * CH:(ch + 1) * CH],
                             o=s1.ap: e.tensor_tensor(o, i, c, ALU.mult),
                             deps=COST.rd() + DST.rd() + s1.wd())
                    COST.r(ha)
                    DST.r(ha)
                    s1.w(ha)
                    hb = p.v(lambda e, s=SINT.ap[:, ch * CH:(ch + 1) * CH], b_=b,
                             o=s2.ap: e.tensor_tensor(o, pb[b_][:, :], s, ALU.mult),
                             deps=bk.rdeps(b) + SINT.rd() + s2.wd())
                    SINT.r(hb)
                    bk.add_r(b, hb)
                    s2.w(hb)
                    hc = p.v(lambda e, o=sl, a_=s1.ap, b2=s2.ap:
                             e.tensor_tensor(o, a_, b2, ALU.add),
                             deps=s1.rd() + s2.rd() + DST.wd())
                    s1.r(hc)
                    s2.r(hc)
                    DST.w(hc)
        WVB = WP.next()
        fetch_w(env, WVB, wv[:, hg * 1024:(hg + 1) * 1024])
        for tb in range(16):
            b = 4 + tb % 2
            wd = bk.wdeps(b)
            steps = [(XLB[c].ap[:, tb * 128:(tb + 1) * 128],
                      WVB.ap[:, c * 128:(c + 1) * 128]) for c in range(CT)]
            hv = p.mmgrp(pb[b][:, 0:128], steps,
                         deps=wd + WVB.rd() + _xl_rd(env) + VB.wd())
            WVB.r(hv)
            for c in range(CT):
                XLB[c].r(hv)
            bk.set_w(b, hv)
            vdst = VB.ap[:, tb * 132: tb * 132 + 132].rearrange(
                "p (h c) -> p h c", h=2)[:, :, 0:64]
            vsrc = pb[b][:, 0:128].rearrange("p (h c) -> p h c", h=2)
            he = p.v(lambda e, o=vdst, i=vsrc: e.tensor_copy(o, i),
                     deps=bk.rdeps(b) + VB.wd())
            bk.add_r(b, he)
            VB.w(he)
        for qc in range(NCH):
            prb_h = [[], []]
            for kt in range(16):
                for a in range(2):
                    hr = slice(64 * a, 64 * a + 64)
                    PRB = PRB2[a]
                    b = 2 * a + kt % 2
                    wd = bk.wdeps(b)
                    hs = p.mm(pb[b][:, :], KT.ap[hr, kt * 128:(kt + 1) * 128],
                              QT.ap[hr, qc * CH:(qc + 1) * CH], True, True,
                              deps=wd + KT.rd() + QT.rd(), inc=True)
                    KT.r(hs)
                    QT.r(hs)
                    bk.set_w(b, hs)
                    he = p.s(lambda e, o=PRB.ap[:, kt * CH:(kt + 1) * CH], b_=b:
                             e.activation(o, pb[b_][:, :], AF.Exp, bias=env["N10"],
                                          scale=scale),
                             deps=bk.rdeps(b) + env["ones_h"] +
                             (PRB.wd() if kt == 0 else []))
                    bk.add_r(b, he)
                    PRB.w(he)
                    prb_h[a].append(he)
            for a in range(2):
                PRB = PRB2[a]
                bav = 4 + a
                wd = bk.wdeps(bav)
                steps = [(VB.ap[:, kt * 132 + a * 66: kt * 132 + a * 66 + 65],
                          PRB.ap[:, kt * CH:(kt + 1) * CH]) for kt in range(16)]
                hav = p.mmgrp(pb[bav][0:65, :], steps, deps=wd + prb_h[a] + VB.rd())
                VB.r(hav)
                PRB.r(hav)
                bk.set_w(bav, hav)
                rw = ROWA if a == 0 else ROWB
                hl1 = p.s(lambda e, b_=bav, o=rw.ap: e.activation(o, pb[b_][64:65, :], AF.Ln),
                          deps=bk.rdeps(bav) + rw.wd())
                rw.w(hl1)
                bk.add_r(bav, hl1)
                hrc = p.s(lambda e, o=rw.ap: e.activation(o, o, AF.Exp, scale=-1.0),
                          deps=rw.rd() + rw.wd())
                rw.w(hrc)
                bbc = 6 + a
                wd3 = bk.wdeps(bbc)
                hbc = p.mm(pb[bbc][0:64, :], ONER[0:1, 0:64], rw.ap, True, True,
                           deps=wd3 + rw.rd(), inc=True)
                rw.r(hbc)
                bk.set_w(bbc, hbc)
                sx = SC1 if a == 0 else SC2
                hcp = p.v(lambda e, b_=bbc, o=sx.ap: e.tensor_copy(o[0:64, :], pb[b_][0:64, :]),
                          deps=bk.rdeps(bbc) + sx.wd())
                sx.w(hcp)
                bk.add_r(bbc, hcp)
                hnm = p.v(lambda e, b_=bav, i=sx.ap, o=SCB.ap[0:64, a * CH:a * CH + CH]:
                          e.tensor_tensor(o, pb[b_][0:64, :], i[0:64, :], ALU.mult),
                          deps=bk.rdeps(bav) + sx.rd() + SCB.wd())
                sx.r(hnm)
                SCB.w(hnm)
                bk.add_r(bav, hnm)
                hst = p.store(ao[hg * 128 + a * 64: hg * 128 + a * 64 + 64,
                                 qc * CH:(qc + 1) * CH], SCB.ap[0:64, a * CH:a * CH + CH],
                              deps=SCB.rd(), stream="ao")
                SCB.r(hst)
                ao_done.append(hst)

    WOF = Buf(PRB2[0].ap)
    WOF.users += PRB2[0].wd()
    AOS.users += QT.wd() + KT.wd()
    for c in range(CT):
        fetch_w(env, WOF, wo[:, c * 1024:(c + 1) * 1024], dst_off=c * 1024)
    for ch in range(NCH):
        lds = []
        for i in range(CT):
            h = p.load(AOS.ap[:, i * CH:(i + 1) * CH],
                       ao[i * 128:(i + 1) * 128, ch * CH:(ch + 1) * CH],
                       deps=ao_done + AOS.wd(), stream="aos")
            AOS.w(h)
            lds.append(h)
        for c in range(CT):
            WOB = WOF
            b = 6 + c % 2
            wd = bk.wdeps(b)
            steps = [(WOB.ap[:, c * 1024 + i * 128:c * 1024 + (i + 1) * 128],
                      AOS.ap[:, i * CH:(i + 1) * CH]) for i in range(CT)]
            hmm = p.mmgrp(pb[b][:, :], steps, deps=wd + WOB.rd() + lds)
            WOB.r(hmm)
            AOS.r(hmm)
            bk.set_w(b, hmm)
            rt = RB[c].ap[:, ch * CH:(ch + 1) * CH]
            if has_bias:
                hev = p.v(lambda e, o=rt, b_=b, cc=c:
                          e.scalar_tensor_tensor(o, pb[b_][:, :], COLS.ap[:, cc:cc + 1], o,
                                                 ALU.add, ALU.add),
                          deps=bk.rdeps(b) + COLS.rd() + RB[c].rd() + RB[c].wd())
                COLS.r(hev)
            else:
                hev = p.v(lambda e, o=rt, b_=b: e.tensor_tensor(o, o, pb[b_][:, :], ALU.add),
                          deps=bk.rdeps(b) + RB[c].rd() + RB[c].wd())
            bk.add_r(b, hev)
            RB[c].w(hev)


def emit_moe(env, ph):
    p, bk, pb, arena = env["p"], env["bk"], env["pb"], env["arena"]
    RB, XLB = env["RB"], env["XLB"]
    SC1, SC2 = env["SC1"], env["SC2"]
    E16, IDT = env["E16"], env["IDT"]
    din, dram_scratch = env["din"], env["dram_scratch"]
    WB_t = env["WB"]
    COLS = env["COLS"]
    HT = 11

    emit_ln(env)

    tag = ph["tag"]
    sg = din(tag + "_sg"); su = din(tag + "_su"); sd = din(tag + "_sd")
    rg = din(tag + "_rg"); ru = din(tag + "_ru"); rd_ = din(tag + "_rd")
    wr = din(tag + "_wr"); rb = din(tag + "_rb")

    AH = arena.bf16(HT * T)
    W7 = arena.bf16(T)
    LGT = arena.f32(CH)
    WTK = arena.f32(16)
    WRP = arena.f32(CH)
    WRB = arena.bf16(64)
    GU = Pool([Buf(WB_t[:, i * 1024:(i + 1) * 1024]) for i in range(4)])
    DBs = [Buf(WB_t[:, 0:1408]), Buf(WB_t[:, 1408:2816])]
    DBp = Pool(DBs)

    fetch_w(env, WRB, wr[:, 0:56], cols=56)
    COLS.w(p.load(COLS.ap[0:7, 8:9], rb, deps=COLS.wd(), stream="cols"))

    resid_spill = dram_scratch("moe_resid", [DM, T], F32)

    for ch in range(NCH):
        wd = bk.wdeps(6)
        steps = [(WRB.ap[:, c * 7:(c + 1) * 7],
                  XLB[c].ap[:, ch * CH:(ch + 1) * CH]) for c in range(CT)]
        hl = p.mmgrp(pb[6][0:7, :], steps, deps=wd + WRB.rd() + _xl_rd(env))
        WRB.r(hl)
        for c in range(CT):
            XLB[c].r(hl)
        bk.set_w(6, hl)
        hev = p.v(lambda e: e.tensor_scalar(LGT.ap[0:7, :], pb[6][0:7, :],
                                            COLS.ap[0:7, 8:9], None, ALU.add),
                  deps=bk.rdeps(6) + COLS.rd() + LGT.wd())
        COLS.r(hev)
        LGT.w(hev)
        bk.add_r(6, hev)
        for tb in range(4):
            t0 = ch * 4 + tb
            wd7 = bk.wdeps(7)
            htr = p.on("tensor",
                       lambda e, s=LGT.ap[0:7, tb * 128:(tb + 1) * 128]:
                       e.transpose(pb[7][:, 0:7], s, IDT.ap[0:7, 0:7]),
                       deps=wd7 + LGT.rd() + IDT.rd(), inc="pe")
            LGT.r(htr)
            IDT.r(htr)
            bk.set_w(7, htr)
            hm = p.v(lambda e: e.reduce_max(WTK.ap[:, 0:1], pb[7][:, 0:7], axis=AX.X),
                     deps=bk.rdeps(7) + WTK.wd())
            WTK.w(hm)
            bk.add_r(7, hm)
            hng = p.v(lambda e: e.tensor_scalar(WTK.ap[:, 1:2], WTK.ap[:, 0:1],
                                                -1.0, None, ALU.mult), deps=WTK.rd())
            WTK.w(hng)
            hex_ = p.s(lambda e: e.activation(SC1.ap[:, 0:7], pb[7][:, 0:7], AF.Exp,
                                              bias=WTK.ap[:, 1:2]),
                       deps=bk.rdeps(7) + WTK.rd() + SC1.wd())
            WTK.r(hex_)
            SC1.w(hex_)
            bk.add_r(7, hex_)
            hs1 = p.s(lambda e: e.activation(SC2.ap[:, 0:7], SC1.ap[:, 0:7], AF.Sign,
                                             bias=-1.0),
                      deps=SC1.rd() + SC2.wd() + env["ones_h"])
            SC1.r(hs1)
            SC2.w(hs1)
            hm1 = p.v(lambda e: e.tensor_scalar(SC2.ap[:, 0:7], SC2.ap[:, 0:7],
                                                1.0, None, ALU.add), deps=SC2.rd())
            SC2.w(hm1)
            hpm = p.v(lambda e: e.scalar_tensor_tensor(SC1.ap[:, 8:15], SC2.ap[:, 0:7],
                                                       -1e30, SC1.ap[:, 0:7],
                                                       ALU.mult, ALU.add),
                      deps=SC2.rd() + SC1.rd())
            SC2.r(hpm)
            SC1.w(hpm)
            hm2 = p.v(lambda e: e.reduce_max(WTK.ap[:, 2:3], SC1.ap[:, 8:15], axis=AX.X),
                      deps=SC1.rd() + WTK.wd())
            WTK.w(hm2)
            hng2 = p.v(lambda e: e.tensor_scalar(WTK.ap[:, 3:4], WTK.ap[:, 2:3],
                                                 -1.0, None, ALU.mult), deps=WTK.rd())
            WTK.w(hng2)
            hs2 = p.s(lambda e: e.activation(SC1.ap[:, 8:15], SC1.ap[:, 8:15], AF.Sign,
                                             bias=WTK.ap[:, 3:4]),
                      deps=WTK.rd() + SC1.rd())
            WTK.r(hs2)
            SC1.w(hs2)
            hm2b = p.v(lambda e: e.tensor_scalar(SC1.ap[:, 8:15], SC1.ap[:, 8:15],
                                                 1.0, None, ALU.add), deps=SC1.rd())
            SC1.w(hm2b)
            hdn = p.v(lambda e: e.tensor_scalar(WTK.ap[:, 4:5], WTK.ap[:, 2:3],
                                                1.0, None, ALU.add), deps=WTK.rd())
            WTK.w(hdn)
            hrdn = p.v(lambda e: e.reciprocal(WTK.ap[:, 4:5], WTK.ap[:, 4:5]),
                       deps=WTK.rd())
            WTK.w(hrdn)
            hw1 = p.v(lambda e: e.scalar_tensor_tensor(SC2.ap[:, 0:7], SC1.ap[:, 8:15],
                                                       WTK.ap[:, 2:3], SC2.ap[:, 0:7],
                                                       ALU.mult, ALU.add),
                      deps=SC1.rd() + WTK.rd() + SC2.rd())
            SC1.r(hw1)
            WTK.r(hw1)
            SC2.w(hw1)
            hw2 = p.v(lambda e: e.tensor_scalar(SC2.ap[:, 0:7], SC2.ap[:, 0:7],
                                                WTK.ap[:, 4:5], None, ALU.mult),
                      deps=SC2.rd() + WTK.rd())
            WTK.r(hw2)
            SC2.w(hw2)
            wd7b = bk.wdeps(7)
            htb = p.on("tensor", lambda e: e.transpose(pb[7][0:7, 0:128], SC2.ap[:, 0:7],
                                                       IDT.ap[:, :]),
                       deps=wd7b + SC2.rd() + IDT.rd(), inc="pe")
            SC2.r(htb)
            IDT.r(htb)
            bk.set_w(7, htb)
            hwv = p.v(lambda e, o=W7.ap[0:7, t0 * 128:(t0 + 1) * 128]:
                      e.tensor_copy(o, pb[7][0:7, 0:128]),
                      deps=bk.rdeps(7) + (W7.wd() if t0 == 0 else []))
            bk.add_r(7, hwv)
            W7.w(hwv)

    if ph.get("dbg_xl"):
        dbx = env["nc"].dram_tensor("dbg_xl", [DM, T], BF16, kind="ExternalOutput").ap()
        for c in range(CT):
            h = p.store(dbx[c * 128:(c + 1) * 128, :], XLB[c].ap, deps=XLB[c].rd(),
                        stream="dbg")
            XLB[c].r(h)
    if ph.get("dbg"):
        dbg = env["nc"].dram_tensor("dbg_w", [16, T], F32, kind="ExternalOutput").ap()
        dbw = p.v(lambda e: e.tensor_copy(SC1.ap[0:16, 0:7], SC1.ap[0:16, 0:7]),
                  deps=[])
        hst = p.store(dbg[0:7, :], W7.ap[0:7, :], deps=[W7.writer], stream="dbg")
        W7.r(hst)

    sp_hs = []
    for c in range(CT):
        sp_hs.append(p.store(resid_spill[c * 128:(c + 1) * 128, :], RB[c].ap,
                             deps=RB[c].rd(), stream="res"))
    for c in range(CT):
        RB[c].r(sp_hs[-1])

    n_ex = 1 if ph.get("only_shared") else 8
    for ex in range(n_ex):
        if ex == 0:
            gT, uT, dT = sg, su, sd
        else:
            gT, uT, dT = rg[ex - 1], ru[ex - 1], rd_[ex - 1]
        for half in range(2):
            cross(GU.bufs, DBs)
            a_done = []
            for ib in range(HT):
                i0 = half * HT + ib
                WGB = GU.next()
                WUB = GU.next()
                fetch_w(env, WGB, gT[:, i0 * 1024:(i0 + 1) * 1024])
                fetch_w(env, WUB, uT[:, i0 * 1024:(i0 + 1) * 1024])
                for ch in range(NCH):
                    bg = ch % 2
                    bu = 2 + ch % 2
                    wdg = bk.wdeps(bg)
                    steps = [(WGB.ap[:, c * 128:(c + 1) * 128],
                              XLB[c].ap[:, ch * CH:(ch + 1) * CH]) for c in range(CT)]
                    hgm = p.mmgrp(pb[bg][:, :], steps,
                                  deps=wdg + WGB.rd() + _xl_rd(env))
                    WGB.r(hgm)
                    for c in range(CT):
                        XLB[c].r(hgm)
                    bk.set_w(bg, hgm)
                    wdu = bk.wdeps(bu)
                    steps = [(WUB.ap[:, c * 128:(c + 1) * 128],
                              XLB[c].ap[:, ch * CH:(ch + 1) * CH]) for c in range(CT)]
                    hum = p.mmgrp(pb[bu][:, :], steps, deps=wdu + WUB.rd())
                    WUB.r(hum)
                    for c in range(CT):
                        XLB[c].r(hum)
                    bk.set_w(bu, hum)
                    hsi = p.s(lambda e, b_=bg: e.activation(SC1.ap, pb[b_][:, :], AF.Silu),
                              deps=bk.rdeps(bg) + SC1.wd())
                    SC1.w(hsi)
                    bk.add_r(bg, hsi)
                    ha = p.v(lambda e, o=AH.ap[:, ib * T + ch * CH: ib * T + (ch + 1) * CH],
                             b_=bu: e.tensor_tensor(o, SC1.ap, pb[b_][:, :], ALU.mult),
                             deps=bk.rdeps(bu) + SC1.rd() +
                             (AH.wd() if (ib == 0 and ch == 0) else []))
                    SC1.r(ha)
                    bk.add_r(bu, ha)
                    AH.w(ha)
                    a_done.append(ha)
            cross(DBs, GU.bufs)
            for c in range(CT):
                DB = DBp.next()
                base_ = (half * 8 + c) * 1408
                fetch_w(env, DB, dT[:, base_:base_ + 768], dst_off=0, cols=768)
                fetch_w(env, DB, dT[:, base_ + 768:base_ + 1408],
                        dst_off=768, cols=640)
                for ch in range(NCH):
                    b = 4 + (ch % 2)
                    wd = bk.wdeps(b)
                    steps = [(DB.ap[:, ib * 128:(ib + 1) * 128],
                              AH.ap[:, ib * T + ch * CH: ib * T + (ch + 1) * CH])
                             for ib in range(HT)]
                    hmm = p.mmgrp(pb[b][:, :], steps, deps=wd + DB.rd() + a_done)
                    DB.r(hmm)
                    AH.r(hmm)
                    bk.set_w(b, hmm)
                    rt = RB[c].ap[:, ch * CH:(ch + 1) * CH]
                    if ex == 0:
                        if half == 0:
                            hev = p.s(lambda e, o=rt, b_=b:
                                      e.activation(o, pb[b_][:, :], AF.Copy),
                                      deps=bk.rdeps(b) + RB[c].wd())
                        else:
                            hev = p.v(lambda e, o=rt, b_=b:
                                      e.tensor_tensor(o, o, pb[b_][:, :], ALU.add),
                                      deps=bk.rdeps(b) + RB[c].rd())
                        bk.add_r(b, hev)
                        RB[c].w(hev)
                    else:
                        wd6 = bk.wdeps(6)
                        hwre = p.mm(pb[6][:, :], E16.ap[0:7, (ex - 1) * 128:ex * 128],
                                    W7.ap[0:7, ch * CH:(ch + 1) * CH], True, True,
                                    deps=wd6 + E16.rd() + W7.rd(), inc=True)
                        E16.r(hwre)
                        W7.r(hwre)
                        bk.set_w(6, hwre)
                        hcp = p.s(lambda e: e.activation(WRP.ap, pb[6][:, :], AF.Copy),
                                  deps=bk.rdeps(6) + WRP.wd())
                        WRP.w(hcp)
                        bk.add_r(6, hcp)
                        hmu = p.v(lambda e, b_=b: e.tensor_tensor(SC2.ap, pb[b_][:, :],
                                                                  WRP.ap, ALU.mult),
                                  deps=bk.rdeps(b) + WRP.rd() + SC2.wd())
                        WRP.r(hmu)
                        SC2.w(hmu)
                        bk.add_r(b, hmu)
                        hev = p.v(lambda e, o=rt: e.tensor_tensor(o, o, SC2.ap, ALU.add),
                                  deps=SC2.rd() + RB[c].rd())
                        SC2.r(hev)
                        RB[c].w(hev)

    if ph.get("only_shared"):
        return
    for c in range(CT):
        for ch in range(NCH):
            sb = SC1 if ch % 2 == 0 else SC2
            ld = p.load(sb.ap, resid_spill[c * 128:(c + 1) * 128, ch * CH:(ch + 1) * CH],
                        deps=sb.wd(), stream="sc1" if ch % 2 == 0 else "sc2")
            sb.w(ld)
            rt = RB[c].ap[:, ch * CH:(ch + 1) * CH]
            ha = p.v(lambda e, o=rt, i=sb.ap: e.tensor_tensor(o, o, i, ALU.add),
                     deps=sb.rd() + RB[c].rd())
            sb.r(ha)
            RB[c].w(ha)


def emit_l1ffn(env, ph):
    p, bk, pb, arena = env["p"], env["bk"], env["pb"], env["arena"]
    RB, XLB = env["RB"], env["XLB"]
    SC1, SC2 = env["SC1"], env["SC2"]
    din, dram_scratch = env["din"], env["dram_scratch"]
    WB_t = env["WB"]
    COLS = env["COLS"]

    emit_ln(env)

    tag = ph["tag"]
    wr_ = din(tag + "_wr"); wk_ = din(tag + "_wk"); wv_ = din(tag + "_wv")
    hmu = COLS.w(p.load(COLS.ap[:, 0:16], din(tag + "_cols"), deps=COLS.wd(), stream="cols"))
    homu = COLS.w(p.v(lambda e: e.tensor_scalar(COLS.ap[:, 16:32], COLS.ap[:, 0:16],
                                                -1.0, 1.0, ALU.mult, ALU.add),
                      deps=COLS.rd() + COLS.wd()))

    XRK = arena.bf16(T * CT)
    TMP = arena.bf16(T)
    WP = Pool([Buf(WB_t[:, i * 1024:(i + 1) * 1024]) for i in range(4)])
    r_spill = dram_scratch(tag + "_r", [DM, T], BF16)

    def mix(dst_tile, mu_off, omu_off, c, dst_buf):
        xt = XLB[c].ap
        h1 = p.v(lambda e: e.tensor_scalar(TMP.ap[:, 1:T], xt[:, 0:T - 1],
                                           COLS.ap[:, omu_off + c:omu_off + c + 1],
                                           None, ALU.mult),
                 deps=COLS.rd() + XLB[c].rd() + TMP.wd())
        COLS.r(h1)
        XLB[c].r(h1)
        TMP.w(h1)
        h2 = p.v(lambda e: e.memset(TMP.ap[:, 0:1], 0.0), deps=[])
        TMP.w(h2)
        h3 = p.v(lambda e, o=dst_tile: e.scalar_tensor_tensor(
            o, xt, COLS.ap[:, mu_off + c:mu_off + c + 1], TMP.ap, ALU.mult, ALU.add),
            deps=TMP.rd() + COLS.rd() + XLB[c].rd() + dst_buf.wd())
        COLS.r(h3)
        XLB[c].r(h3)
        TMP.r(h3)
        dst_buf.w(h3)
        return h3

    xr_h = []
    for c in range(CT):
        xr_h.append(mix(XRK.ap[:, c * T:(c + 1) * T], 0, 16, c, XRK))
    r_done = []
    sb = env["SCB"]
    for et in range(CT):
        WBu = WP.next()
        fetch_w(env, WBu, wr_[:, et * 1024:(et + 1) * 1024])
        for ch in range(NCH):
            b = ch % 2
            wd = bk.wdeps(b)
            steps = [(WBu.ap[:, c * 128:(c + 1) * 128],
                      XRK.ap[:, c * T + ch * CH: c * T + (ch + 1) * CH]) for c in range(CT)]
            hmm = p.mmgrp(pb[b][:, :], steps, deps=wd + WBu.rd() + xr_h)
            WBu.r(hmm)
            XRK.r(hmm)
            bk.set_w(b, hmm)
            hev = p.s(lambda e, o=sb.ap[:, 0:CH], b_=b:
                      e.activation(o, pb[b_][:, :], AF.Sigmoid),
                      deps=bk.rdeps(b) + sb.wd())
            sb.w(hev)
            bk.add_r(b, hev)
            hst = p.store(r_spill[et * 128:(et + 1) * 128, ch * CH:(ch + 1) * CH],
                          sb.ap[:, 0:CH], deps=sb.rd(), stream="rsp")
            sb.r(hst)
            r_done.append(hst)
    xk_h = []
    for c in range(CT):
        xk_h.append(mix(XLB[c].ap, 8, 24, c, XLB[c]))
    ka_h = []
    for et in range(CT):
        WBu = WP.next()
        fetch_w(env, WBu, wk_[:, et * 1024:(et + 1) * 1024])
        for ch in range(NCH):
            b = ch % 2
            wd = bk.wdeps(b)
            steps = [(WBu.ap[:, c * 128:(c + 1) * 128],
                      XLB[c].ap[:, ch * CH:(ch + 1) * CH]) for c in range(CT)]
            hmm = p.mmgrp(pb[b][:, :], steps, deps=wd + WBu.rd() + xk_h + XRK.wd())
            WBu.r(hmm)
            for c in range(CT):
                XLB[c].r(hmm)
            bk.set_w(b, hmm)
            h1 = p.s(lambda e, b_=b: e.activation(SC1.ap, pb[b_][:, :], AF.Relu),
                     deps=bk.rdeps(b) + SC1.wd())
            SC1.w(h1)
            bk.add_r(b, h1)
            h2 = p.s(lambda e, o=XRK.ap[:, et * T + ch * CH: et * T + (ch + 1) * CH]:
                     e.activation(o, SC1.ap, AF.Square), deps=SC1.rd())
            SC1.r(h2)
            XRK.w(h2)
            ka_h.append(h2)
    for c in range(CT):
        WBu = WP.next()
        fetch_w(env, WBu, wv_[:, c * 1024:(c + 1) * 1024])
        for ch in range(NCH):
            b = 2 + ch % 2
            wd = bk.wdeps(b)
            steps = [(WBu.ap[:, e_ * 128:(e_ + 1) * 128],
                      XRK.ap[:, e_ * T + ch * CH: e_ * T + (ch + 1) * CH])
                     for e_ in range(CT)]
            hmm = p.mmgrp(pb[b][:, :], steps, deps=wd + WBu.rd() + ka_h)
            WBu.r(hmm)
            XRK.r(hmm)
            bk.set_w(b, hmm)
            ldr = p.load(sb.ap[:, 0:CH], r_spill[c * 128:(c + 1) * 128,
                                                 ch * CH:(ch + 1) * CH],
                         deps=r_done + sb.wd(), stream="scb")
            sb.w(ldr)
            hmu2 = p.v(lambda e, b_=b: e.tensor_tensor(SC1.ap, pb[b_][:, :],
                                                       sb.ap[:, 0:CH], ALU.mult),
                       deps=bk.rdeps(b) + sb.rd() + SC1.wd())
            sb.r(hmu2)
            SC1.w(hmu2)
            bk.add_r(b, hmu2)
            rt = RB[c].ap[:, ch * CH:(ch + 1) * CH]
            hev = p.v(lambda e, o=rt: e.tensor_tensor(o, o, SC1.ap, ALU.add),
                      deps=SC1.rd() + RB[c].rd() + RB[c].wd())
            SC1.r(hev)
            RB[c].w(hev)


def emit_mamba(env, ph):
    p, bk, pb, arena = env["p"], env["bk"], env["pb"], env["arena"]
    RB, XLB = env["RB"], env["XLB"]
    SC1, SC2, SCB = env["SC1"], env["SC2"], env["SCB"]
    SLOT = env["SLOT"]
    ROWA, ROWB, ROWC = env["ROWA"], env["ROWB"], env["ROWC"]
    ONEC, ONER = env["ONEC"], env["ONER"]
    E16 = env["E16"]
    HC, AS = env["HC"], env["AS"]
    din, dram_scratch = env["din"], env["dram_scratch"]
    WB_t = env["WB"]
    COLS = env["COLS"]
    R_t = env["R_t"]
    DT_ = 16

    emit_ln(env)

    tag = ph["tag"]
    ipw = din(tag + "_ipw")
    xpw = din(tag + "_xpw")
    dpw = din(tag + "_dpw")
    opw = din(tag + "_opw")
    f1w = din(tag + "_f1w")
    f2w = din(tag + "_f2w")
    alog = din(tag + "_alog")
    mcols = din(tag + "_cols")
    COLS.w(p.load(COLS.ap[:], mcols, deps=COLS.wd(), stream="cols"))

    AS.w(p.load(AS.ap, alog, deps=AS.wd(), stream="as"))
    HCB = [Buf(HC.ap[:, i:i + 1]) for i in range(256)]
    AS.w(p.s(lambda e: e.activation(AS.ap, AS.ap, AF.Exp), deps=AS.rd() + AS.wd()))
    AS.w(p.v(lambda e: e.tensor_scalar(AS.ap, AS.ap, -1.0, None, ALU.mult),
             deps=AS.rd() + AS.wd()))
    hmz = p.v(lambda e: e.memset(HC.ap, 0.0), deps=HC.wd())
    HC.w(hmz)
    for b_ in HCB:
        b_.w(hmz)

    xs_spill = dram_scratch(tag + "_xs", [2048, T], BF16)
    z_spill = dram_scratch(tag + "_z", [2048, T], BF16)
    mo_spill = dram_scratch(tag + "_mo", [DM, T], BF16)
    resid_spill = dram_scratch(tag + "_res", [DM, T], F32)

    XSC = arena.bf16_at(0, 8192)
    DTC = arena.bf16_at(8192, 8192)
    BRP = arena.bf16_at(25088, 512)
    CRP = arena.bf16_at(25600, 512)
    DTW = arena.bf16_at(16384, 2048)
    DTRB = arena.bf16_at(18432, 2048)
    BT = arena.bf16_at(20480, 2048)
    CTb = arena.bf16_at(22528, 2048)
    MOS = arena.bf16_at(24576, 512)
    XPW = arena.bf16_at(8192, 1536)
    FC = arena.bf16_at(0, 8192)
    S2 = arena.f32_at(8192, 4096)
    WP4 = Pool([Buf(WB_t[:, i * 1024:(i + 1) * 1024]) for i in range(4)])
    WP2 = Pool([Buf(WB_t[:, 0:2048]), Buf(WB_t[:, 2048:4096])])

    CONVT = Buf(R_t[:, 0:2052])
    CACC = Buf(R_t[:, 2052:4100])
    Y = Buf(R_t[:, 4104:12296])

    wst = env["WSTb"][0]
    for hx in range(2):
        ld = p.load(wst.ap[0:64, 0:1024], dpw[:, hx * 1024:(hx + 1) * 1024],
                    deps=wst.wd(), stream="wst")
        wst.w(ld)
        cv = p.v(lambda e, o=DTW.ap[0:64, hx * 1024:(hx + 1) * 1024]:
                 e.tensor_copy(o, wst.ap[0:64, 0:1024]),
                 deps=wst.rd() + (DTW.wd() if hx == 0 else []))
        wst.r(cv)
        DTW.w(cv)

    sp_h = []
    for c in range(CT):
        h = p.store(resid_spill[c * 128:(c + 1) * 128, :], RB[c].ap,
                    deps=RB[c].rd(), stream="res")
        sp_h.append(h)
    for c in range(CT):
        RB[c].r(sp_h[-1])
    CONVT.w(sp_h[-1])
    CACC.w(sp_h[-1])
    Y.w(sp_h[-1])

    hz3 = CONVT.w(p.v(lambda e: e.memset(CONVT.ap[:, 0:3], 0.0), deps=CONVT.wd()))
    xs_done = []
    z_done = []
    for et in range(32):
        IPWb = WP4.next()
        fetch_w(env, IPWb, ipw[:, et * 1024:(et + 1) * 1024])
        is_xs = et < 16
        for ch in range(NCH):
            b = ch % 2
            wd = bk.wdeps(b)
            steps = [(IPWb.ap[:, c * 128:(c + 1) * 128],
                      XLB[c].ap[:, ch * CH:(ch + 1) * CH]) for c in range(CT)]
            hmm = p.mmgrp(pb[b][:, :], steps, deps=wd + IPWb.rd() + _xl_rd(env))
            IPWb.r(hmm)
            for c in range(CT):
                XLB[c].r(hmm)
            bk.set_w(b, hmm)
            if is_xs:
                hev = p.s(lambda e, o=CONVT.ap[:, 3 + ch * CH: 3 + (ch + 1) * CH], b_=b:
                          e.activation(o, pb[b_][:, :], AF.Copy),
                          deps=bk.rdeps(b) + (CONVT.wd() if ch == 0 else []))
                CONVT.w(hev)
                bk.add_r(b, hev)
            else:
                hev = p.s(lambda e, o=SCB.ap[:, ch * CH:(ch + 1) * CH], b_=b:
                          e.activation(o, pb[b_][:, :], AF.Copy),
                          deps=bk.rdeps(b) + (SCB.wd() if ch == 0 else []))
                SCB.w(hev)
                bk.add_r(b, hev)
        if is_xs:
            h1 = p.v(lambda e, w=COLS.ap[:, et * 4:et * 4 + 1]:
                     e.tensor_scalar(CACC.ap, CONVT.ap[:, 0:T], w, None, ALU.mult),
                     deps=COLS.rd() + CONVT.rd() + CACC.wd())
            COLS.r(h1)
            CONVT.r(h1)
            CACC.w(h1)
            for j in (1, 2, 3):
                hj = p.v(lambda e, j_=j, w=COLS.ap[:, et * 4 + j:et * 4 + j + 1]:
                         e.scalar_tensor_tensor(CACC.ap, CONVT.ap[:, j_:j_ + T], w,
                                                CACC.ap, ALU.mult, ALU.add),
                         deps=COLS.rd() + CONVT.rd() + CACC.rd())
                COLS.r(hj)
                CONVT.r(hj)
                CACC.w(hj)
            hsl = p.s(lambda e, bc=COLS.ap[:, 128 + et:129 + et]:
                      e.activation(SCB.ap, CACC.ap, AF.Silu, bias=bc),
                      deps=CACC.rd() + SCB.wd() + COLS.rd())
            COLS.r(hsl)
            CACC.r(hsl)
            SCB.w(hsl)
            hst = p.store(xs_spill[et * 128:(et + 1) * 128, :], SCB.ap, deps=SCB.rd(),
                          stream="xs")
            SCB.r(hst)
            xs_done.append(hst)
        else:
            hst = p.store(z_spill[(et - 16) * 128:(et - 15) * 128, :], SCB.ap,
                          deps=SCB.rd(), stream="z")
            SCB.r(hst)
            z_done.append(hst)

    fetch_w(env, XPW, xpw[:, 0:768], dst_off=0, cols=768)
    fetch_w(env, XPW, xpw[:, 768:1536], dst_off=768, cols=768)
    for ch in range(NCH):
        w0 = bk.wdeps(0)
        w1 = bk.wdeps(1)
        w2 = bk.wdeps(2)
        hdtr = hB = hC = None
        for dt in range(DT_):
            sl = SLOT[dt % 2]
            ld = p.load(sl.ap, xs_spill[dt * 128:(dt + 1) * 128, ch * CH:(ch + 1) * CH],
                        deps=xs_done + sl.wd(), stream=f"slot{dt % 2}")
            sl.w(ld)
            hdtr = p.mm(pb[0][0:64, :], XPW.ap[:, dt * 96:dt * 96 + 64], sl.ap,
                        dt == 0, dt == DT_ - 1,
                        deps=(w0 + XPW.rd() + sl.rd()) if dt == 0 else sl.rd(),
                        inc=True)
            sl.r(hdtr)
            XPW.r(hdtr)
            hB = p.mm(pb[1][0:16, :], XPW.ap[:, dt * 96 + 64:dt * 96 + 80], sl.ap,
                      dt == 0, dt == DT_ - 1,
                      deps=(w1 + sl.rd()) if dt == 0 else sl.rd(),
                      inc=True)
            sl.r(hB)
            hC = p.mm(pb[2][0:16, :], XPW.ap[:, dt * 96 + 80:dt * 96 + 96], sl.ap,
                      dt == 0, dt == DT_ - 1,
                      deps=(w2 + sl.rd()) if dt == 0 else sl.rd(),
                      inc=True)
            sl.r(hC)
        bk.set_w(0, hdtr)
        bk.set_w(1, hB)
        bk.set_w(2, hC)
        SLOT[0].r(hC)
        SLOT[1].r(hC)
        h1 = p.s(lambda e, o=DTRB.ap[0:64, ch * CH:(ch + 1) * CH]:
                 e.activation(o, pb[0][0:64, :], AF.Copy),
                 deps=bk.rdeps(0) + (DTRB.wd() if ch == 0 else []))
        DTRB.w(h1)
        bk.add_r(0, h1)
        h2 = p.v(lambda e, o=BT.ap[0:16, ch * CH:(ch + 1) * CH]:
                 e.tensor_copy(o, pb[1][0:16, :]),
                 deps=bk.rdeps(1) + (BT.wd() if ch == 0 else []))
        BT.w(h2)
        bk.add_r(1, h2)
        h3 = p.v(lambda e, o=CTb.ap[0:16, ch * CH:(ch + 1) * CH]:
                 e.tensor_copy(o, pb[2][0:16, :]),
                 deps=bk.rdeps(2) + (CTb.wd() if ch == 0 else []))
        CTb.w(h3)
        bk.add_r(2, h3)
    DTC.r(bk.st[0]["w"])
    DTC.r(bk.st[1]["w"])
    DTC.r(bk.st[2]["w"])
    DTC.users += XPW.wd()

    cross(WP2.bufs, WP4.bufs)
    mo_done = []
    for ch in range(NCH):
        for dt in range(DT_):
            h = p.load(XSC.ap[:, dt * CH:(dt + 1) * CH],
                       xs_spill[dt * 128:(dt + 1) * 128, ch * CH:(ch + 1) * CH],
                       deps=xs_done + (XSC.wd() if dt == 0 else []), stream="xsc")
            XSC.w(h)
        for dt in range(DT_):
            b = dt % 2
            wd = bk.wdeps(b)
            hdt = p.mm(pb[b][:, :], DTW.ap[0:64, dt * 128:(dt + 1) * 128],
                       DTRB.ap[0:64, ch * CH:(ch + 1) * CH], True, True,
                       deps=wd + DTW.rd() + DTRB.rd(), inc=True)
            DTW.r(hdt)
            DTRB.r(hdt)
            bk.set_w(b, hdt)
            bc = COLS.ap[:, 80 + dt:81 + dt]
            ha = p.s(lambda e, b_=b, bc_=bc: e.activation(SC1.ap, pb[b_][:, :], AF.Abs,
                                                          bias=bc_),
                     deps=bk.rdeps(b) + SC1.wd() + COLS.rd())
            COLS.r(ha)
            SC1.w(ha)
            bk.add_r(b, ha)
            hb = p.s(lambda e: e.activation(SC1.ap, SC1.ap, AF.Exp, scale=-1.0),
                     deps=SC1.rd())
            SC1.w(hb)
            hc = p.s(lambda e: e.activation(SC1.ap, SC1.ap, AF.Ln, bias=1.0),
                     deps=SC1.rd())
            SC1.w(hc)
            hd = p.s(lambda e, b_=b, bc_=bc: e.activation(SC2.ap, pb[b_][:, :], AF.Relu,
                                                          bias=bc_),
                     deps=bk.rdeps(b) + SC2.wd() + COLS.rd())
            COLS.r(hd)
            SC2.w(hd)
            bk.add_r(b, hd)
            he = p.v(lambda e, o=DTC.ap[:, dt * CH:(dt + 1) * CH]:
                     e.tensor_tensor(o, SC1.ap, SC2.ap, ALU.add),
                     deps=SC1.rd() + SC2.rd() + (DTC.wd() if dt == 0 else []))
            SC1.r(he)
            SC2.r(he)
            DTC.w(he)
            hf = p.v(lambda e, o=XSC.ap[:, dt * CH:(dt + 1) * CH],
                     i=DTC.ap[:, dt * CH:(dt + 1) * CH]:
                     e.tensor_tensor(o, i, o, ALU.mult),
                     deps=DTC.rd() + XSC.rd() + XSC.wd())
            DTC.r(hf)
            XSC.w(hf)
        hy0 = Y.w(p.v(lambda e: e.memset(Y.ap, 0.0), deps=Y.wd()))
        for n in range(16):
            w6 = bk.wdeps(6)
            hbr = p.mm(pb[6][:, :], E16.ap[0:16, n * 128:(n + 1) * 128],
                       BT.ap[0:16, ch * CH:(ch + 1) * CH], True, True,
                       deps=w6 + E16.rd() + BT.rd(), inc=True)
            E16.r(hbr)
            BT.r(hbr)
            bk.set_w(6, hbr)
            hbe = p.s(lambda e: e.activation(BRP.ap, pb[6][:, :], AF.Copy),
                      deps=bk.rdeps(6) + BRP.wd())
            BRP.w(hbe)
            bk.add_r(6, hbe)
            w7 = bk.wdeps(7)
            hcr = p.mm(pb[7][:, :], E16.ap[0:16, n * 128:(n + 1) * 128],
                       CTb.ap[0:16, ch * CH:(ch + 1) * CH], True, True,
                       deps=w7 + E16.rd() + CTb.rd(), inc=True)
            E16.r(hcr)
            CTb.r(hcr)
            bk.set_w(7, hcr)
            hce = p.s(lambda e: e.activation(CRP.ap, pb[7][:, :], AF.Copy),
                      deps=bk.rdeps(7) + CRP.wd())
            CRP.w(hce)
            bk.add_r(7, hce)
            for dt in range(DT_):
                it = n * DT_ + dt
                _set = 3 * (it % 2)
                sA, sB, sC = SLOT[_set], SLOT[_set + 1], SLOT[_set + 2]
                scan_v = (it % 2 == 0)
                hda = p.s(lambda e, i=DTC.ap[:, dt * CH:(dt + 1) * CH],
                          sc=AS.ap[:, dt * 16 + n:dt * 16 + n + 1], o=sA.ap:
                          e.activation(o, i, AF.Exp, scale=sc),
                          deps=DTC.rd() + AS.rd() + sA.wd())
                DTC.r(hda)
                AS.r(hda)
                sA.w(hda)
                hdb = p.v(lambda e, i=XSC.ap[:, dt * CH:(dt + 1) * CH], o=sB.ap:
                          e.tensor_tensor(o, i, BRP.ap, ALU.mult),
                          deps=XSC.rd() + BRP.rd() + sB.wd())
                XSC.r(hdb)
                BRP.r(hdb)
                sB.w(hdb)
                hcb = HCB[dt * 16 + n]
                hsc = p.v(lambda e, ini=hcb.ap,
                          o=sC.ap, a=sA.ap, b2=sB.ap:
                          e.tensor_tensor_scan(o, a, b2, ini, ALU.mult, ALU.add),
                          deps=sA.rd() + sB.rd() + hcb.rd() + sC.wd())
                sA.r(hsc)
                sB.r(hsc)
                hcb.r(hsc)
                sC.w(hsc)
                hcy = p.on("gpsimd", lambda e, o=hcb.ap,
                           i=sC.ap[:, 511:512]: e.tensor_copy(o, i),
                           deps=sC.rd() + hcb.wd(), inc="ge")
                sC.r(hcy)
                hcb.w(hcy)
                hym = p.v(lambda e, i=sC.ap, o=sB.ap:
                          e.tensor_tensor(o, i, CRP.ap, ALU.mult),
                          deps=sC.rd() + CRP.rd() + sB.wd())
                sC.r(hym)
                CRP.r(hym)
                sB.w(hym)
                hya = p.on("gpsimd",
                           lambda e, o=Y.ap[:, dt * CH:(dt + 1) * CH], i=sB.ap:
                           e.tensor_tensor(o, o, i, ALU.add),
                           deps=sB.rd() + Y.rd() + Y.wd(), inc="ge")
                sB.r(hya)
                Y.w(hya)
        for dt in range(DT_):
            h = p.load(XSC.ap[:, dt * CH:(dt + 1) * CH],
                       xs_spill[dt * 128:(dt + 1) * 128, ch * CH:(ch + 1) * CH],
                       deps=XSC.wd() if dt == 0 else (), stream="xsc")
            XSC.w(h)
            h = p.load(DTC.ap[:, dt * CH:(dt + 1) * CH],
                       z_spill[dt * 128:(dt + 1) * 128, ch * CH:(ch + 1) * CH],
                       deps=(z_done + DTC.wd()) if dt == 0 else (), stream="dtc")
            DTC.w(h)
        for dt in range(DT_):
            hya = p.v(lambda e, o=Y.ap[:, dt * CH:(dt + 1) * CH],
                      i=XSC.ap[:, dt * CH:(dt + 1) * CH],
                      d=COLS.ap[:, 64 + dt:65 + dt]:
                      e.scalar_tensor_tensor(o, i, d, o, ALU.mult, ALU.add),
                      deps=XSC.rd() + COLS.rd() + Y.rd() + Y.wd())
            XSC.r(hya)
            COLS.r(hya)
            Y.w(hya)
            sZ = SLOT[5]
            hsz = p.s(lambda e, i=DTC.ap[:, dt * CH:(dt + 1) * CH], o=sZ.ap:
                      e.activation(o, i, AF.Silu), deps=DTC.rd() + sZ.wd())
            DTC.r(hsz)
            sZ.w(hsz)
            hmx = p.v(lambda e, o=XSC.ap[:, dt * CH:(dt + 1) * CH],
                      y=Y.ap[:, dt * CH:(dt + 1) * CH], s=sZ.ap:
                      e.tensor_tensor(o, y, s, ALU.mult),
                      deps=Y.rd() + sZ.rd() + XSC.rd() + XSC.wd())
            sZ.r(hmx)
            Y.r(hmx)
            XSC.w(hmx)
        for c in range(CT):
            OPW = WP2.next()
            fetch_w(env, OPW, opw[:, c * 2048:c * 2048 + 1024], dst_off=0, cols=1024)
            fetch_w(env, OPW, opw[:, c * 2048 + 1024:(c + 1) * 2048],
                    dst_off=1024, cols=1024)
            b = 4 + c % 2
            wd = bk.wdeps(b)
            steps = [(OPW.ap[:, dt * 128:(dt + 1) * 128],
                      XSC.ap[:, dt * CH:(dt + 1) * CH]) for dt in range(DT_)]
            hmm = p.mmgrp(pb[b][:, :], steps, deps=wd + OPW.rd() + XSC.rd())
            OPW.r(hmm)
            XSC.r(hmm)
            bk.set_w(b, hmm)
            hev = p.s(lambda e, b_=b: e.activation(MOS.ap[:, 0:CH], pb[b_][:, :], AF.Copy),
                      deps=bk.rdeps(b) + MOS.wd())
            MOS.w(hev)
            bk.add_r(b, hev)
            hst = p.store(mo_spill[c * 128:(c + 1) * 128, ch * CH:(ch + 1) * CH],
                          MOS.ap[:, 0:CH], deps=MOS.rd(), stream="mo")
            MOS.r(hst)
            mo_done.append(hst)

    S2.users += DTC.wd()
    for ch in range(NCH):
        for c in range(CT):
            hld = p.load(MOS.ap[:, 0:CH],
                         mo_spill[c * 128:(c + 1) * 128, ch * CH:(ch + 1) * CH],
                         deps=mo_done + MOS.wd(), stream="mos")
            MOS.w(hld)
            hs = p.v(lambda e, o=S2.ap[:, c * CH:(c + 1) * CH],
                     x=XLB[c].ap[:, ch * CH:(ch + 1) * CH]:
                     e.tensor_tensor(o, x, MOS.ap[:, 0:CH], ALU.add),
                     deps=MOS.rd() + XLB[c].rd() + (S2.wd() if c == 0 else []))
            MOS.r(hs)
            XLB[c].r(hs)
            S2.w(hs)
        _affine_ln(env, ch, S2, 1e-12, 96, 104, dst="XL")
    r_free = Y.wd() + CONVT.wd() + CACC.wd()
    FC.users += XSC.wd()
    env["_RESID_SPILL"] = resid_spill
    env["_RFREE"] = r_free
    F1C = arena.bf16_at(16384, 8192)
    F1C.users += DTW.wd() + DTRB.wd() + BT.wd() + CTb.wd()
    for et8 in range(8):
        fetch_w(env, F1C, f1w[:, et8 * 1024:(et8 + 1) * 1024], dst_off=et8 * 1024)
    for ch in range(NCH):
        for et in range(16):
            if et < 8:
                F1W = F1C
                f1off = et * 1024
            else:
                F1W = WP2.next()
                f1off = 0
                fetch_w(env, F1W, f1w[:, et * 1024:(et + 1) * 1024])
            b = et % 2
            wd = bk.wdeps(b)
            steps = [(F1W.ap[:, f1off + c * 128:f1off + (c + 1) * 128],
                      XLB[c].ap[:, ch * CH:(ch + 1) * CH]) for c in range(CT)]
            hmm = p.mmgrp(pb[b][:, :], steps, deps=wd + F1W.rd() + _xl_rd(env) +
                          (FC.wd() if et == 0 else []))
            F1W.r(hmm)
            for c in range(CT):
                XLB[c].r(hmm)
            bk.set_w(b, hmm)
            hb1 = p.v(lambda e, b_=b, bc=COLS.ap[:, 144 + et:145 + et]:
                      e.tensor_scalar(SC1.ap, pb[b_][:, :], bc, None, ALU.add),
                      deps=bk.rdeps(b) + COLS.rd() + SC1.wd())
            COLS.r(hb1)
            SC1.w(hb1)
            bk.add_r(b, hb1)
            hlr = p.v(lambda e, o=FC.ap[:, et * CH:(et + 1) * CH]:
                      e.scalar_tensor_tensor(o, SC1.ap, 0.01, SC1.ap, ALU.mult, ALU.max),
                      deps=SC1.rd())
            SC1.r(hlr)
            FC.w(hlr)
        for c in range(CT):
            F2W = WP2.next()
            fetch_w(env, F2W, f2w[:, c * 2048:c * 2048 + 1024], dst_off=0, cols=1024)
            fetch_w(env, F2W, f2w[:, c * 2048 + 1024:(c + 1) * 2048],
                    dst_off=1024, cols=1024)
            b = 4 + c % 2
            wd = bk.wdeps(b)
            steps = [(F2W.ap[:, et * 128:(et + 1) * 128],
                      FC.ap[:, et * CH:(et + 1) * CH]) for et in range(16)]
            hmm = p.mmgrp(pb[b][:, :], steps, deps=wd + F2W.rd() + FC.rd())
            F2W.r(hmm)
            FC.r(hmm)
            bk.set_w(b, hmm)
            hb2 = p.v(lambda e, b_=b, bc=COLS.ap[:, 160 + c:161 + c]:
                      e.tensor_scalar(SC1.ap, pb[b_][:, :], bc, None, ALU.add),
                      deps=bk.rdeps(b) + SC1.wd() + COLS.rd())
            COLS.r(hb2)
            SC1.w(hb2)
            bk.add_r(b, hb2)
            hs2 = p.v(lambda e, o=S2.ap[:, c * CH:(c + 1) * CH],
                      x=XLB[c].ap[:, ch * CH:(ch + 1) * CH]:
                      e.tensor_tensor(o, SC1.ap, x, ALU.add),
                      deps=SC1.rd() + XLB[c].rd() + (S2.wd() if c == 0 else []))
            SC1.r(hs2)
            XLB[c].r(hs2)
            S2.w(hs2)
        _affine_ln(env, ch, S2, 1e-12, 112, 120, dst="RESID")


def _affine_ln(env, ch, S2, eps, w_off, b_off, dst):
    p, bk, pb = env["p"], env["bk"], env["pb"]
    XLB = env["XLB"]
    SC1, SC2 = env["SC1"], env["SC2"]
    ROWA, ROWB, ROWC = env["ROWA"], env["ROWB"], env["ROWC"]
    ONEC, ONER = env["ONEC"], env["ONER"]
    COLS = env["COLS"]
    w6 = bk.wdeps(6)
    w5 = bk.wdeps(5)
    hsum = hsq = None
    for c in range(CT):
        st = S2.ap[:, c * CH:(c + 1) * CH]
        hsum = p.mm(pb[6][0:1, :], ONEC[:], st, c == 0, c == CT - 1,
                    deps=(w6 + S2.rd()) if c == 0 else S2.rd(), inc=True)
        S2.r(hsum)
        sq_b = SC1 if c % 2 == 0 else SC2
        hsqa = p.s(lambda e, o=sq_b.ap, i=st: e.activation(o, i, AF.Square),
                   deps=sq_b.wd() + S2.rd() + (w5 if c == 0 else []))
        sq_b.w(hsqa)
        S2.r(hsqa)
        hsq = p.mm(pb[5][0:1, :], ONEC[:], sq_b.ap, c == 0, c == CT - 1,
                   deps=sq_b.rd(), inc=True)
        sq_b.r(hsq)
    bk.set_w(6, hsum)
    bk.set_w(5, hsq)
    SC1.r(hsq)
    SC2.r(hsq)
    S2.r(hsq)
    ha = p.v(lambda e: e.tensor_scalar(ROWA.ap, pb[6][0:1, :], 1.0 / DM, None, ALU.mult),
             deps=bk.rdeps(6) + ROWA.wd())
    ROWA.w(ha)
    bk.add_r(6, ha)
    hb = p.v(lambda e: e.tensor_scalar(ROWB.ap, pb[5][0:1, :], 1.0 / DM, None, ALU.mult),
             deps=bk.rdeps(5) + ROWB.wd())
    ROWB.w(hb)
    bk.add_r(5, hb)
    hc = p.v(lambda e: e.tensor_tensor(ROWC.ap, ROWA.ap, ROWA.ap, ALU.mult),
             deps=ROWA.rd() + ROWC.wd())
    ROWA.r(hc)
    ROWC.w(hc)
    hd = p.v(lambda e: e.tensor_tensor(ROWC.ap, ROWB.ap, ROWC.ap, ALU.subtract),
             deps=ROWB.rd() + ROWC.rd())
    ROWB.r(hd)
    ROWC.w(hd)
    eps_ap = env["EPS5"] if eps == 1e-5 else env["EPS12"]
    he = p.s(lambda e, e_=eps_ap: e.activation(ROWC.ap, ROWC.ap, AF.Sqrt, bias=e_),
             deps=ROWC.rd() + env["ones_h"])
    ROWC.w(he)
    hf = p.v(lambda e: e.reciprocal(ROWC.ap, ROWC.ap), deps=ROWC.rd())
    ROWC.w(hf)
    w7 = bk.wdeps(7)
    hm = p.mm(pb[7][:, :], ONER[:], ROWA.ap, True, True, deps=w7 + ROWA.rd(), inc=True)
    ROWA.r(hm)
    bk.set_w(7, hm)
    w4 = bk.wdeps(4)
    hr = p.mm(pb[4][:, :], ONER[:], ROWC.ap, True, True, deps=w4 + ROWC.rd(), inc=True)
    ROWC.r(hr)
    bk.set_w(4, hr)
    RB = env["RB"]
    for c in range(CT):
        st = S2.ap[:, c * CH:(c + 1) * CH]
        h1 = p.v(lambda e, i=st: e.tensor_tensor(SC1.ap, i, pb[7][:, :], ALU.subtract),
                 deps=bk.rdeps(7) + SC1.wd() + S2.rd())
        SC1.w(h1)
        S2.r(h1)
        bk.add_r(7, h1)
        h2 = p.v(lambda e: e.tensor_tensor(SC1.ap, SC1.ap, pb[4][:, :], ALU.mult),
                 deps=bk.rdeps(4) + SC1.rd())
        SC1.w(h2)
        bk.add_r(4, h2)
        if dst == "XL":
            out = XLB[c].ap[:, ch * CH:(ch + 1) * CH]
            h3 = p.v(lambda e, o=out, wc=COLS.ap[:, w_off + c:w_off + c + 1],
                     bc2=COLS.ap[:, b_off + c:b_off + c + 1]:
                     e.tensor_scalar(o, SC1.ap, wc, bc2, ALU.mult, ALU.add),
                     deps=SC1.rd() + COLS.rd() + XLB[c].wd())
            SC1.r(h3)
            COLS.r(h3)
            XLB[c].w(h3)
        else:
            h3 = p.v(lambda e, wc=COLS.ap[:, w_off + c:w_off + c + 1],
                     bc2=COLS.ap[:, b_off + c:b_off + c + 1]:
                     e.tensor_scalar(SC1.ap, SC1.ap, wc, bc2, ALU.mult, ALU.add),
                     deps=SC1.rd() + COLS.rd())
            COLS.r(h3)
            SC1.w(h3)
            ld = p.load(SC2.ap, env["_RESID_SPILL"][c * 128:(c + 1) * 128,
                                                    ch * CH:(ch + 1) * CH],
                        deps=SC2.wd(), stream="sc2")
            SC2.w(ld)
            rt = RB[c].ap[:, ch * CH:(ch + 1) * CH]
            h4 = p.v(lambda e, o=rt: e.tensor_tensor(o, SC1.ap, SC2.ap, ALU.add),
                     deps=SC1.rd() + SC2.rd() + env["_RFREE"] + RB[c].wd())
            SC1.r(h4)
            SC2.r(h4)
            RB[c].w(h4)


def _ct(a):
    return np.ascontiguousarray(a, dtype=np.float32)


def _tile_w(wt, blk=128):
    C, E = wt.shape
    A = C // 128
    I = E // blk
    t = wt.reshape(A, 128, I, blk).transpose(1, 2, 0, 3).reshape(128, I * A * blk)
    return np.ascontiguousarray(t, dtype=np.float32)


def _tile_w3(w3, blk=128):
    return np.stack([_tile_w(w3[e], blk) for e in range(w3.shape[0])])


def _tile_down(dt):
    t = dt.reshape(2, 11, 128, 8, 128).transpose(2, 0, 3, 1, 4).reshape(128, -1)
    return np.ascontiguousarray(t, dtype=np.float32)


def _colpack(v, k):
    return np.ascontiguousarray(np.asarray(v, np.float32).reshape(k, 128).T)


def _rope_tables_l0():
    inv = 1.0 / (10000.0 ** (np.arange(0, 64, 2, dtype=np.float64) / 64))
    t = np.arange(T, dtype=np.float64)
    fr = t[:, None] * inv[None, :]
    emb = np.concatenate([fr, fr], axis=1)
    cosT = np.cos(emb).T
    sinT = np.sin(emb).T
    cos2 = np.ascontiguousarray(np.vstack([cosT, cosT]).astype(BF))
    sin2 = np.ascontiguousarray(np.vstack([sinT, sinT]).astype(BF))
    Pm = np.zeros((64, 64), np.float32)
    for j in range(32):
        Pm[j, 2 * j + 1] = -1.0
    for j in range(32, 64):
        Pm[j, 2 * (j - 32)] = 1.0
    blk = np.zeros((128, 128), np.float32)
    blk[0:64, 0:64] = Pm
    blk[64:128, 64:128] = Pm
    perm = np.ascontiguousarray(blk.T.astype(BF))
    return cos2, sin2, perm


def _rope_tables_l1():
    inv = 1.0 / (10000.0 ** (np.arange(0, 64, 2, dtype=np.float64) / 64))
    t = np.arange(T, dtype=np.float64)
    fr = t[:, None] * inv[None, :]
    cosT = np.repeat(np.cos(fr).T, 2, axis=0)
    sinT = np.repeat(np.sin(fr).T, 2, axis=0)
    cos2 = np.ascontiguousarray(np.vstack([cosT, cosT]).astype(BF))
    sin2 = np.ascontiguousarray(np.vstack([sinT, sinT]).astype(BF))
    Pm = np.zeros((64, 64), np.float32)
    for i in range(32):
        Pm[2 * i, 2 * i + 1] = -1.0
        Pm[2 * i + 1, 2 * i] = 1.0
    blk = np.zeros((128, 128), np.float32)
    blk[0:64, 0:64] = Pm
    blk[64:128, 64:128] = Pm
    perm = np.ascontiguousarray(blk.T.astype(BF))
    return cos2, sin2, perm


def _prep_arrays(inp):
    a = {}
    e16 = np.zeros((16, 2048), np.float32)
    for n in range(16):
        e16[n, n * 128:(n + 1) * 128] = 1.0
    a["c_e16"] = e16.astype(BF)
    a["c_id128"] = np.eye(128, dtype=np.float32)

    c0, s0, p0 = _rope_tables_l0()
    a["a0a_wq"] = _tile_w(np.asarray(inp["L0_attn_a_Wq"]).T)
    a["a0a_wk"] = _tile_w(np.asarray(inp["L0_attn_a_Wk"]).T)
    a["a0a_wv"] = _tile_w(np.asarray(inp["L0_attn_a_Wv"]).T)
    a["a0a_wo"] = _tile_w(np.asarray(inp["L0_attn_a_Wo"]).T)
    a["a0a_cos"], a["a0a_sin"], a["a0a_perm"] = c0, s0, p0
    wqkv = np.asarray(inp["L0_attn_b_w_qkv"])
    a["a0b_wq"] = _tile_w(np.ascontiguousarray(wqkv[0:1024].T))
    a["a0b_wk"] = _tile_w(np.ascontiguousarray(wqkv[1024:2048].T))
    a["a0b_wv"] = _tile_w(np.ascontiguousarray(wqkv[2048:3072].T))
    a["a0b_wo"] = _tile_w(np.asarray(inp["L0_attn_b_w_out"]).T)
    a["a0b_bo"] = _colpack(inp["L0_attn_b_b_out"], 8)
    a["moe_sg"] = _tile_w(np.asarray(inp["L0_ffn_sg"]).T)
    a["moe_su"] = _tile_w(np.asarray(inp["L0_ffn_su"]).T)
    a["moe_sd"] = _tile_down(np.ascontiguousarray(np.asarray(inp["L0_ffn_sd"]).T))
    a["moe_rg"] = _tile_w3(np.asarray(inp["L0_ffn_rg"]).transpose(0, 2, 1))
    a["moe_ru"] = _tile_w3(np.asarray(inp["L0_ffn_ru"]).transpose(0, 2, 1))
    a["moe_rd"] = np.stack([_tile_down(np.ascontiguousarray(np.asarray(inp["L0_ffn_rd"])[e].T)) for e in range(7)])
    a["moe_wr"] = _tile_w(np.asarray(inp["L0_ffn_wr"]).T, blk=7)
    a["moe_rb"] = _ct(np.asarray(inp["L0_ffn_rb"]).reshape(7, 1))
    c1, s1, p1 = _rope_tables_l1()
    a["a1a_wq"] = _tile_w(np.asarray(inp["L1_attn_a_Wq"]).T)
    a["a1a_wk"] = _tile_w(np.asarray(inp["L1_attn_a_Wk"]).T)
    a["a1a_wv"] = _tile_w(np.asarray(inp["L1_attn_a_Wv"]).T)
    a["a1a_wo"] = _tile_w(np.asarray(inp["L1_attn_a_Wo"]).T)
    a["a1a_cos"], a["a1a_sin"], a["a1a_perm"] = c1, s1, p1
    a["lff_wr"] = _tile_w(np.asarray(inp["L1_ffn_W_r"]).T)
    a["lff_wk"] = _tile_w(np.asarray(inp["L1_ffn_W_k"]).T)
    a["lff_wv"] = _tile_w(np.asarray(inp["L1_ffn_W_v"]).T)
    cols = np.zeros((128, 16), np.float32)
    cols[:, 0:8] = _colpack(inp["L1_ffn_mu_r"], 8)
    cols[:, 8:16] = _colpack(inp["L1_ffn_mu_k"], 8)
    a["lff_cols"] = cols
    a["mb_ipw"] = _tile_w(np.asarray(inp["L1_attn_b_in_proj_w"]).T)
    a["mb_xpw"] = _tile_w(np.asarray(inp["L1_attn_b_x_proj_w"]).T, blk=96)
    a["mb_dpw"] = _ct(np.asarray(inp["L1_attn_b_dt_proj_w"]).T)
    a["mb_opw"] = _tile_w(np.asarray(inp["L1_attn_b_out_proj_w"]).T)
    a["mb_f1w"] = _tile_w(np.asarray(inp["L1_attn_b_ffn_w1"]).T)
    a["mb_f2w"] = _tile_w(np.asarray(inp["L1_attn_b_ffn_w2"]).T)
    alog = np.asarray(inp["L1_attn_b_A_log"], np.float32)
    a["mb_alog"] = _ct(alog.reshape(16, 128, 16).transpose(1, 0, 2).reshape(128, 256))
    mc = np.zeros((128, 192), np.float32)
    cw = np.asarray(inp["L1_attn_b_conv_w"], np.float32)
    mc[:, 0:64] = cw.reshape(16, 128, 4).transpose(1, 0, 2).reshape(128, 64)
    mc[:, 64:80] = _colpack(inp["L1_attn_b_D"], 16)
    mc[:, 80:96] = _colpack(inp["L1_attn_b_dt_proj_b"], 16)
    mc[:, 96:104] = _colpack(inp["L1_attn_b_ln1_w"], 8)
    mc[:, 104:112] = _colpack(inp["L1_attn_b_ln1_b"], 8)
    mc[:, 112:120] = _colpack(inp["L1_attn_b_ln2_w"], 8)
    mc[:, 120:128] = _colpack(inp["L1_attn_b_ln2_b"], 8)
    mc[:, 128:144] = _colpack(inp["L1_attn_b_conv_b"], 16)
    mc[:, 144:160] = _colpack(inp["L1_attn_b_ffn_b1"], 16)
    mc[:, 160:168] = _colpack(inp["L1_attn_b_ffn_b2"], 8)
    a["mb_cols"] = mc
    return a


PH_A0A = dict(kind="attn", tag="a0a", rope=True, scale=0.125)
PH_MOE = dict(kind="moe", tag="moe")
PH_A0B = dict(kind="attn", tag="a0b", rope=False, scale=1.0 / 32.0, bias=True)
PH_A1A = dict(kind="attn", tag="a1a", rope=True, scale=0.125)
PH_LFF = dict(kind="l1ffn", tag="lff")
PH_MB = dict(kind="mamba", tag="mb")
ALL_PHASES = [PH_A0A, PH_MOE, PH_A0B, PH_MOE, PH_A1A, PH_LFF, PH_MB, PH_LFF]

FUSED = True


def kernel(**inputs):
    arrays = _prep_arrays(inputs)
    x = np.asarray(inputs["L0_attn_a_x"], np.float32)
    ncore = 8
    x_cur = [np.ascontiguousarray(x[i].T) for i in range(ncore)]

    if FUSED:
        groups = [ALL_PHASES]
    else:
        groups = [[ph] for ph in ALL_PHASES]

    prog_cache = {}
    for grp in groups:
        key = tuple(id(ph) for ph in grp)
        if key not in prog_cache:
            prog_cache[key] = build_program(grp, arrays)
        nc, used = prog_cache[key]
        in_maps = []
        for i in range(ncore):
            m = {n: arrays[n] for n in used if n != "x_in"}
            m["x_in"] = x_cur[i]
            in_maps.append(m)
        res = run_bass_kernel_spmd(nc, in_maps, core_ids=list(range(ncore)))
        x_cur = [np.asarray(res.results[i]["x_out"]) for i in range(ncore)]

    out = np.stack([xc.T for xc in x_cur]).astype(np.float32)
    return np.ascontiguousarray(out)


# version 18 (final)
# speedup vs baseline: 1.0275x; 1.0275x over previous
import numpy as np
from contextlib import ExitStack
import ml_dtypes
import concourse.bass as bass
import concourse.mybir as mybir
from concourse.bass_utils import run_bass_kernel_spmd

F32 = mybir.dt.float32
BF16 = mybir.dt.bfloat16
AF = mybir.ActivationFunctionType
ALU = mybir.AluOpType
AX = mybir.AxisListType
BF = ml_dtypes.bfloat16

T = 2048
CH = 512
NCH = T // CH
CT = 8
DM = 1024

ENGINES = ("sync", "tensor", "vector", "scalar", "gpsimd")


class P:
    def __init__(self, nc, ctx):
        self.nc = nc
        self.ctx = ctx
        self.q = {e: [] for e in ENGINES}
        self.sems = {}
        self.cnt = {}
        self.lastw = {e: {} for e in ENGINES}
        for s in ("pe", "ve", "se", "ge"):
            self.sems[s] = ctx.enter_context(nc.semaphore(s))
            self.cnt[s] = 0
        self.pending_barrier = {e: None for e in ENGINES}

    def dsem(self, name):
        if name not in self.sems:
            self.sems[name] = self.ctx.enter_context(self.nc.semaphore(name))
            self.cnt[name] = 0
        return name

    def barrier(self):
        snap = [(s, v) for s, v in self.cnt.items() if v > 0]
        for e in ENGINES:
            self.pending_barrier[e] = snap

    def on(self, eng, fn, deps=(), inc=None, n=1):
        if self.pending_barrier[eng]:
            deps = list(self.pending_barrier[eng]) + list(deps)
            self.pending_barrier[eng] = None
        w = []
        for d in deps:
            if d is None:
                continue
            s, v = d
            if self.lastw[eng].get(s, 0) >= v:
                continue
            self.lastw[eng][s] = v
            w.append((s, v))
        h = None
        if inc is not None:
            self.cnt[inc] += n
            h = (inc, self.cnt[inc])
        sems = self.sems

        def run(eo, w=tuple(w), fn=fn, inc=inc, n=n):
            for (s, v) in w:
                eo.wait_ge(sems[s], v)
            r = fn(eo)
            if inc is not None:
                r.then_inc(sems[inc], n)

        self.q[eng].append(run)
        return h

    def load(self, dst, src, deps=(), stream="ld"):
        return self.on("sync", lambda e, d=dst, s=src: e.dma_start(out=d, in_=s),
                       deps=deps, inc=self.dsem("L_" + stream), n=16)

    def store(self, dst, src, deps=(), stream="st"):
        return self.on("gpsimd", lambda e, d=dst, s=src: e.dma_start(out=d, in_=s),
                       deps=deps, inc=self.dsem("S_" + stream), n=16)

    def mm(self, out, lhsT, rhs, start, stop, deps=(), inc=False):
        return self.on(
            "tensor",
            lambda e, o=out, l=lhsT, r=rhs, st=start, sp=stop:
                e.matmul(o, l, r, start=st, stop=sp),
            deps=deps, inc="pe" if inc else None)

    def mmgrp(self, out, steps, deps=()):
        n = len(steps)
        h = None
        for i, (l, r) in enumerate(steps):
            h = self.mm(out, l, r, i == 0, i == n - 1,
                        deps=deps if i == 0 else (), inc=(i == n - 1))
        return h

    def v(self, fn, deps=()):
        return self.on("vector", fn, deps=deps, inc="ve")

    def s(self, fn, deps=()):
        return self.on("scalar", fn, deps=deps, inc="se")


class Buf:
    def __init__(self, ap):
        self.ap = ap
        self.writer = None
        self.users = []

    def wd(self):
        d = list(self.users)
        if self.writer is not None:
            d.append(self.writer)
        self.users = []
        return d

    def rd(self):
        return [self.writer] if self.writer is not None else []

    def w(self, h):
        self.writer = h
        return h

    def r(self, h):
        if h is not None:
            self.users.append(h)
        return h


class Banks:
    def __init__(self, tensors):
        self.t = tensors
        self.st = [{"w": None, "r": []} for _ in tensors]

    def wdeps(self, i):
        d = list(self.st[i]["r"])
        if self.st[i]["w"] is not None:
            d.append(self.st[i]["w"])
        self.st[i]["r"] = []
        return d

    def set_w(self, i, h):
        self.st[i]["w"] = h

    def rdeps(self, i):
        return [self.st[i]["w"]] if self.st[i]["w"] is not None else []

    def add_r(self, i, h):
        if h is not None:
            self.st[i]["r"].append(h)


class Arena:
    def __init__(self, tens, cols):
        self.t = tens
        self.cols = cols
        self.off = 0

    def reset(self):
        self.off = 0

    def bf16(self, n):
        o = self.off
        assert o + n <= self.cols, f"arena overflow {o}+{n}>{self.cols}"
        self.off += n
        return Buf(self.t[:, o:o + n])

    def f32(self, n):
        if self.off % 2:
            self.off += 1
        o = self.off
        assert o + 2 * n <= self.cols, "arena overflow f32"
        self.off += 2 * n
        return Buf(self.t[:, o:o + 2 * n].bitcast(F32))

    def bf16_at(self, o, n):
        assert o + n <= self.cols
        return Buf(self.t[:, o:o + n])

    def f32_at(self, o, n):
        assert o % 2 == 0 and o + 2 * n <= self.cols
        return Buf(self.t[:, o:o + 2 * n].bitcast(F32))


def fetch_w(env, dst_buf, src_ap, dst_off=0, cols=None):
    p = env["p"]
    i = env["WST_i"][0]
    env["WST_i"][0] = 1 - i
    wst = env["WSTb"][i]
    if cols is None:
        n = 1
        for d in src_ap.shape:
            n *= d
        cols = n // 128
    assert cols <= 1024, cols
    stg = wst.ap[:, 0:cols]
    if len(src_ap.shape) == 3:
        stg = stg.rearrange("p (a n) -> p a n", a=src_ap.shape[1])
    ld = p.load(stg, src_ap, deps=wst.wd(), stream=f"wst{i}")
    wst.w(ld)
    dst = dst_buf.ap[:, dst_off:dst_off + cols]
    cv = p.on("gpsimd", lambda e, d=dst, s=wst.ap[:, 0:cols]:
              e.tensor_copy(d, s), deps=wst.rd() + dst_buf.wd(), inc="ge")
    wst.r(cv)
    dst_buf.w(cv)
    return cv


class Pool:
    def __init__(self, bufs):
        self.bufs = bufs
        self.i = 0

    def next(self):
        b = self.bufs[self.i]
        self.i = (self.i + 1) % len(self.bufs)
        return b


def cross(tos, froms):
    pend = []
    for f in froms:
        pend += f.wd()
    for t in tos:
        t.users += pend


def build_program(phases, arrays):
    nc = bass.Bass("TRN2", target_bir_lowering=False)
    ctx = ExitStack()
    used = []
    dcache = {}

    def din(name):
        if name in dcache:
            return dcache[name]
        a = arrays[name]
        dt = BF16 if a.dtype == BF else F32
        used.append(name)
        ap = nc.dram_tensor(name, list(a.shape), dt, kind="ExternalInput").ap()
        dcache[name] = ap
        return ap

    x_in = nc.dram_tensor("x_in", [DM, T], F32, kind="ExternalInput").ap()
    used.append("x_in")
    x_out = nc.dram_tensor("x_out", [DM, T], F32, kind="ExternalOutput").ap()

    dsc = {}

    def dram_scratch(name, shape, dt):
        if name not in dsc:
            dsc[name] = nc.dram_tensor(name, list(shape), dt, kind="Internal").ap()
        return dsc[name]

    with ctx:
        p = P(nc, ctx)
        E = ctx.enter_context
        R_t = E(nc.sbuf_tensor("Rt", [128, DM * T // 128], F32))
        XL_t = E(nc.sbuf_tensor("XLt", [128, DM * T // 128], BF16))
        AR_t = E(nc.sbuf_tensor("ARt", [128, 26752], BF16))
        WST_t = E(nc.sbuf_tensor("WSTt", [128, 2048], F32))
        WB_t = E(nc.sbuf_tensor("WBt", [128, 4096], BF16))
        SC1_t = E(nc.sbuf_tensor("SC1t", [128, 512], F32))
        SC2_t = E(nc.sbuf_tensor("SC2t", [128, 512], F32))
        SCB_t = E(nc.sbuf_tensor("SCBt", [128, 2048], BF16))
        SLOT_t = E(nc.sbuf_tensor("SLOTt", [128, 3072], BF16))
        ROWA_t = E(nc.sbuf_tensor("ROWAt", [1, 512], F32))
        ROWB_t = E(nc.sbuf_tensor("ROWBt", [1, 512], F32))
        ROWC_t = E(nc.sbuf_tensor("ROWCt", [1, 512], F32))
        ROWD_t = E(nc.sbuf_tensor("ROWDt", [1, 512], F32))
        ROWE_t = E(nc.sbuf_tensor("ROWEt", [1, 512], F32))
        ROWF_t = E(nc.sbuf_tensor("ROWFt", [1, 512], F32))
        ONECB_t = E(nc.sbuf_tensor("ONECBt", [128, 1], BF16))
        COLS_t = E(nc.sbuf_tensor("COLSt", [128, 192], F32))
        E16_t = E(nc.sbuf_tensor("E16t", [16, 2048], BF16))
        IDT_t = E(nc.sbuf_tensor("IDTt", [128, 128], F32))
        HC_t = E(nc.sbuf_tensor("HCt", [128, 256], F32))
        AS_t = E(nc.sbuf_tensor("ASt", [128, 256], F32))
        ONEC_t = E(nc.sbuf_tensor("ONECt", [128, 1], F32))
        ONER_t = E(nc.sbuf_tensor("ONERt", [1, 128], F32))
        EPS_t = E(nc.sbuf_tensor("EPSt", [1, 2], F32))
        N10_t = E(nc.sbuf_tensor("N10t", [128, 1], F32))
        CM1_t = E(nc.sbuf_tensor("CM1t", [128, 1], F32))
        pb = [E(nc.psum_tensor(f"pb{i}", [128, 512], F32)) for i in range(8)]

        env = dict(nc=nc, p=p, bk=Banks(pb), pb=pb, arena=Arena(AR_t, 26752),
                   R_t=R_t,
                   RB=[Buf(R_t[:, c * T:(c + 1) * T]) for c in range(CT)],
                   XLB=[Buf(XL_t[:, c * T:(c + 1) * T]) for c in range(CT)],
                   WSTb=[Buf(WST_t[:, 0:1024]), Buf(WST_t[:, 1024:2048])],
                   WST_i=[0], WB=WB_t,
                   SC1=Buf(SC1_t[:]), SC2=Buf(SC2_t[:]), SCB=Buf(SCB_t[:]),
                   SLOT=[Buf(SLOT_t[:, i * 512:(i + 1) * 512]) for i in range(6)],
                   ROWA=Buf(ROWA_t[:]), ROWB=Buf(ROWB_t[:]), ROWC=Buf(ROWC_t[:]),
                   ROWD=Buf(ROWD_t[:]), ROWE=Buf(ROWE_t[:]), ROWF=Buf(ROWF_t[:]),
                   ONECB=ONECB_t,
                   COLS=Buf(COLS_t[:]), E16=Buf(E16_t[:]), IDT=Buf(IDT_t[:]),
                   HC=Buf(HC_t[:]), AS=Buf(AS_t[:]),
                   ONEC=ONEC_t, ONER=ONER_t,
                   din=din, dram_scratch=dram_scratch)
        bk, arena = env["bk"], env["arena"]

        h1 = p.v(lambda e: e.memset(ONEC_t[:], 1.0))
        h2 = p.v(lambda e: e.memset(ONER_t[:], 1.0))
        h3 = p.v(lambda e: e.memset(EPS_t[0:1, 0:1], 1e-5))
        h4 = p.v(lambda e: e.memset(EPS_t[0:1, 1:2], 1e-12))
        h5 = p.v(lambda e: e.memset(N10_t[:], -10.0))
        h6 = p.v(lambda e: e.memset(CM1_t[:], -1.0))
        h7 = p.v(lambda e: e.memset(ONECB_t[:], 1.0))
        nc.const_aps.register(F32, -1.0, CM1_t[:])
        env["ones_h"] = [h1, h2, h3, h4, h5, h6, h7]
        env["N10"] = N10_t[:]
        env["EPS5"] = EPS_t[0:1, 0:1]
        env["EPS12"] = EPS_t[0:1, 1:2]
        env["E16"].w(p.load(E16_t[:], din("c_e16"), stream="e16"))
        env["IDT"].w(p.load(IDT_t[:], din("c_id128"), stream="idt"))

        rl = []
        for c in range(CT):
            rl.append(p.load(env["RB"][c].ap, x_in[c * 128:(c + 1) * 128, :],
                             stream="rb"))
        for c in range(CT):
            env["RB"][c].w(rl[-1])

        for ph in phases:
            p.barrier()
            arena.reset()
            {"attn": emit_attn, "moe": emit_moe,
             "l1ffn": emit_l1ffn, "mamba": emit_mamba}[ph["kind"]](env, ph)

        p.barrier()
        sh = []
        for c in range(CT):
            sh.append(p.store(x_out[c * 128:(c + 1) * 128, :], env["RB"][c].ap,
                              deps=env["RB"][c].rd(), stream="xout"))
        p.on("gpsimd", lambda e: e.nop(), deps=[sh[-1]])

        with nc.Block() as block:
            @block.sync
            def _(eo):
                for f in p.q["sync"]:
                    f(eo)

            @block.tensor
            def _(eo):
                for f in p.q["tensor"]:
                    f(eo)

            @block.vector
            def _(eo):
                for f in p.q["vector"]:
                    f(eo)

            @block.scalar
            def _(eo):
                for f in p.q["scalar"]:
                    f(eo)

            @block.gpsimd
            def _(eo):
                for f in p.q["gpsimd"]:
                    f(eo)

    return nc, used


def emit_ln(env):
    p, bk, pb = env["p"], env["bk"], env["pb"]
    RB, XLB = env["RB"], env["XLB"]
    SC1, SC2 = env["SC1"], env["SC2"]
    SLOT = env["SLOT"]
    ROWS = [(env["ROWA"], env["ROWB"], env["ROWC"]),
            (env["ROWD"], env["ROWE"], env["ROWF"])]
    ONEC, ONER, ONECB = env["ONEC"], env["ONER"], env["ONECB"]
    base = list(env["ones_h"])
    for ch in range(NCH):
        par = ch % 2
        b_sum, b_sq, b_m, b_r = (6, 5, 7, 4) if par == 0 else (1, 0, 2, 3)
        rA, rB, rC = ROWS[par]
        w6 = bk.wdeps(b_sum)
        w5 = bk.wdeps(b_sq)
        hsum = hsq = None
        for c in range(CT):
            rt = RB[c].ap[:, ch * CH:(ch + 1) * CH]
            hsum = p.mm(pb[b_sum][0:1, :], ONEC[:], rt, c == 0, c == CT - 1,
                        deps=(base + w6 + RB[c].rd()) if c == 0 else RB[c].rd(),
                        inc=True)
            RB[c].r(hsum)
            sq_b = SLOT[(ch * 2 + c % 2) % 6]
            hsqa = p.s(lambda e, o=sq_b.ap, i=rt: e.activation(o, i, AF.Square),
                       deps=sq_b.wd() + RB[c].rd() + (w5 if c == 0 else []))
            sq_b.w(hsqa)
            RB[c].r(hsqa)
            hsq = p.mm(pb[b_sq][0:1, :], ONECB[:], sq_b.ap, c == 0, c == CT - 1,
                       deps=sq_b.rd(), inc=True)
            sq_b.r(hsq)
        bk.set_w(b_sum, hsum)
        bk.set_w(b_sq, hsq)
        for c in range(CT):
            RB[c].r(hsq)
        ha = p.v(lambda e, o=rA.ap, b_=b_sum:
                 e.tensor_scalar(o, pb[b_][0:1, :], 1.0 / DM, None, ALU.mult),
                 deps=bk.rdeps(b_sum) + rA.wd())
        rA.w(ha)
        bk.add_r(b_sum, ha)
        hb = p.v(lambda e, o=rB.ap, b_=b_sq:
                 e.tensor_scalar(o, pb[b_][0:1, :], 1.0 / DM, None, ALU.mult),
                 deps=bk.rdeps(b_sq) + rB.wd())
        rB.w(hb)
        bk.add_r(b_sq, hb)
        hc = p.v(lambda e, o=rC.ap, a=rA.ap: e.tensor_tensor(o, a, a, ALU.mult),
                 deps=rA.rd() + rC.wd())
        rA.r(hc)
        rC.w(hc)
        hd = p.v(lambda e, o=rC.ap, b2=rB.ap: e.tensor_tensor(o, b2, o, ALU.subtract),
                 deps=rB.rd() + rC.rd())
        rB.r(hd)
        rC.w(hd)
        he = p.s(lambda e, o=rC.ap: e.activation(o, o, AF.Sqrt, bias=env["EPS5"]),
                 deps=rC.rd() + env["ones_h"])
        rC.w(he)
        hf = p.v(lambda e, o=rC.ap: e.reciprocal(o, o), deps=rC.rd())
        rC.w(hf)
        w7 = bk.wdeps(b_m)
        hm = p.mm(pb[b_m][:, :], ONER[:], rA.ap, True, True,
                  deps=w7 + rA.rd(), inc=True)
        rA.r(hm)
        bk.set_w(b_m, hm)
        w4 = bk.wdeps(b_r)
        hr = p.mm(pb[b_r][:, :], ONER[:], rC.ap, True, True,
                  deps=w4 + rC.rd(), inc=True)
        rC.r(hr)
        bk.set_w(b_r, hr)
        for c in range(CT):
            rt = RB[c].ap[:, ch * CH:(ch + 1) * CH]
            xt = XLB[c].ap[:, ch * CH:(ch + 1) * CH]
            sb = SC1 if c % 2 == 0 else SC2
            h1_ = p.v(lambda e, o=sb.ap, i=rt, b_=b_m:
                      e.tensor_tensor(o, i, pb[b_][:, :], ALU.subtract),
                      deps=bk.rdeps(b_m) + sb.wd() + RB[c].rd())
            sb.w(h1_)
            RB[c].r(h1_)
            bk.add_r(b_m, h1_)
            h2_ = p.v(lambda e, o=xt, i=sb.ap, b_=b_r:
                      e.tensor_tensor(o, i, pb[b_][:, :], ALU.mult),
                      deps=bk.rdeps(b_r) + sb.rd() + XLB[c].wd())
            sb.r(h2_)
            bk.add_r(b_r, h2_)
            XLB[c].w(h2_)


def _xl_rd(env):
    return [h for c in range(CT) for h in env["XLB"][c].rd()]


def emit_attn(env, ph):
    p, bk, pb, arena = env["p"], env["bk"], env["pb"], env["arena"]
    RB, XLB = env["RB"], env["XLB"]
    SC1, SC2, SCB = env["SC1"], env["SC2"], env["SCB"]
    ROWA, ROWB, ONER = env["ROWA"], env["ROWB"], env["ONER"]
    din, dram_scratch = env["din"], env["dram_scratch"]
    WB_t = env["WB"]
    COLS = env["COLS"]

    emit_ln(env)

    tag = ph["tag"]
    wq = din(tag + "_wq"); wk = din(tag + "_wk")
    wv = din(tag + "_wv"); wo = din(tag + "_wo")
    rope = ph.get("rope", False)
    scale = ph["scale"]
    has_bias = ph.get("bias", False)
    if has_bias:
        COLS.w(p.load(COLS.ap[:, 0:CT], din(tag + "_bo"), deps=COLS.wd(), stream="cols"))

    QT = arena.bf16(T)
    KT = arena.bf16(T)
    VB = arena.bf16(16 * 132)
    PRB2 = [arena.bf16(16 * CH), arena.bf16(16 * CH)]
    AOS = Buf(arena.t[:, 0:4096])
    WP = Pool([Buf(WB_t[:, i * 1024:(i + 1) * 1024]) for i in range(4)])
    if rope:
        COST = arena.bf16(T)
        SINT = arena.bf16(T)
        PERM = Buf(env["COLS"].ap[:, 64:128].bitcast(BF16))
        COST.w(p.load(COST.ap, din(tag + "_cos"), stream="cost"))
        SINT.w(p.load(SINT.ap, din(tag + "_sin"), stream="sint"))
        PERM.w(p.load(PERM.ap, din(tag + "_perm"), deps=PERM.wd(), stream="perm"))

    ao = dram_scratch(tag + "_ao", [DM, T], BF16)
    vview = VB.ap.rearrange("p (t h c) -> p t h c", t=16, h=2)
    VB.w(p.v(lambda e: e.memset(vview[:, :, :, 64:65], 1.0), deps=VB.wd()))

    ao_done = []
    for hg in range(CT):
        for (W, DST) in ((wq, QT), (wk, KT)):
            WQB = WP.next()
            fetch_w(env, WQB, W[:, hg * 1024:(hg + 1) * 1024])
            for ch in range(NCH):
                b = ch % 2
                wd = bk.wdeps(b)
                steps = [(WQB.ap[:, c * 128:(c + 1) * 128],
                          XLB[c].ap[:, ch * CH:(ch + 1) * CH]) for c in range(CT)]
                hmm = p.mmgrp(pb[b][:, :], steps,
                              deps=wd + WQB.rd() + _xl_rd(env) + DST.wd())
                WQB.r(hmm)
                for c in range(CT):
                    XLB[c].r(hmm)
                bk.set_w(b, hmm)
                hev = p.v(lambda e, o=DST.ap[:, ch * CH:(ch + 1) * CH], b_=b:
                          e.tensor_copy(o, pb[b_][:, :]), deps=bk.rdeps(b))
                bk.add_r(b, hev)
                DST.w(hev)
        if rope:
            SLOT = env["SLOT"]
            ru_ = 0
            for DST in (QT, KT):
                for ch in range(NCH):
                    b = 2 + ch % 2
                    wd = bk.wdeps(b)
                    sl = DST.ap[:, ch * CH:(ch + 1) * CH]
                    s1 = SLOT[(2 * ru_) % 6]
                    s2 = SLOT[(2 * ru_ + 1) % 6]
                    ru_ += 1
                    hrot = p.mm(pb[b][:, :], PERM.ap, sl, True, True,
                                deps=wd + PERM.rd() + DST.rd(), inc=True)
                    PERM.r(hrot)
                    DST.r(hrot)
                    bk.set_w(b, hrot)
                    ha = p.v(lambda e, i=sl, c=COST.ap[:, ch * CH:(ch + 1) * CH],
                             o=s1.ap: e.tensor_tensor(o, i, c, ALU.mult),
                             deps=COST.rd() + DST.rd() + s1.wd())
                    COST.r(ha)
                    DST.r(ha)
                    s1.w(ha)
                    hb = p.v(lambda e, s=SINT.ap[:, ch * CH:(ch + 1) * CH], b_=b,
                             o=s2.ap: e.tensor_tensor(o, pb[b_][:, :], s, ALU.mult),
                             deps=bk.rdeps(b) + SINT.rd() + s2.wd())
                    SINT.r(hb)
                    bk.add_r(b, hb)
                    s2.w(hb)
                    hc = p.v(lambda e, o=sl, a_=s1.ap, b2=s2.ap:
                             e.tensor_tensor(o, a_, b2, ALU.add),
                             deps=s1.rd() + s2.rd() + DST.wd())
                    s1.r(hc)
                    s2.r(hc)
                    DST.w(hc)
        WVB = WP.next()
        fetch_w(env, WVB, wv[:, hg * 1024:(hg + 1) * 1024])
        for tb in range(16):
            b = 4 + tb % 2
            wd = bk.wdeps(b)
            steps = [(XLB[c].ap[:, tb * 128:(tb + 1) * 128],
                      WVB.ap[:, c * 128:(c + 1) * 128]) for c in range(CT)]
            hv = p.mmgrp(pb[b][:, 0:128], steps,
                         deps=wd + WVB.rd() + _xl_rd(env) + VB.wd())
            WVB.r(hv)
            for c in range(CT):
                XLB[c].r(hv)
            bk.set_w(b, hv)
            vdst = VB.ap[:, tb * 132: tb * 132 + 132].rearrange(
                "p (h c) -> p h c", h=2)[:, :, 0:64]
            vsrc = pb[b][:, 0:128].rearrange("p (h c) -> p h c", h=2)
            he = p.v(lambda e, o=vdst, i=vsrc: e.tensor_copy(o, i),
                     deps=bk.rdeps(b) + VB.wd())
            bk.add_r(b, he)
            VB.w(he)
        for qc in range(NCH):
            prb_h = [[], []]
            for kt in range(16):
                for a in range(2):
                    hr = slice(64 * a, 64 * a + 64)
                    PRB = PRB2[a]
                    b = 2 * a + kt % 2
                    wd = bk.wdeps(b)
                    hs = p.mm(pb[b][:, :], KT.ap[hr, kt * 128:(kt + 1) * 128],
                              QT.ap[hr, qc * CH:(qc + 1) * CH], True, True,
                              deps=wd + KT.rd() + QT.rd(), inc=True)
                    KT.r(hs)
                    QT.r(hs)
                    bk.set_w(b, hs)
                    he = p.s(lambda e, o=PRB.ap[:, kt * CH:(kt + 1) * CH], b_=b:
                             e.activation(o, pb[b_][:, :], AF.Exp, bias=env["N10"],
                                          scale=scale),
                             deps=bk.rdeps(b) + env["ones_h"] +
                             (PRB.wd() if kt == 0 else []))
                    bk.add_r(b, he)
                    PRB.w(he)
                    prb_h[a].append(he)
            for a in range(2):
                PRB = PRB2[a]
                bav = 4 + a
                wd = bk.wdeps(bav)
                steps = [(VB.ap[:, kt * 132 + a * 66: kt * 132 + a * 66 + 65],
                          PRB.ap[:, kt * CH:(kt + 1) * CH]) for kt in range(16)]
                hav = p.mmgrp(pb[bav][0:65, :], steps, deps=wd + prb_h[a] + VB.rd())
                VB.r(hav)
                PRB.r(hav)
                bk.set_w(bav, hav)
                rw = ROWA if a == 0 else ROWB
                hl1 = p.s(lambda e, b_=bav, o=rw.ap: e.activation(o, pb[b_][64:65, :], AF.Ln),
                          deps=bk.rdeps(bav) + rw.wd())
                rw.w(hl1)
                bk.add_r(bav, hl1)
                hrc = p.s(lambda e, o=rw.ap: e.activation(o, o, AF.Exp, scale=-1.0),
                          deps=rw.rd() + rw.wd())
                rw.w(hrc)
                bbc = 6 + a
                wd3 = bk.wdeps(bbc)
                hbc = p.mm(pb[bbc][0:64, :], ONER[0:1, 0:64], rw.ap, True, True,
                           deps=wd3 + rw.rd(), inc=True)
                rw.r(hbc)
                bk.set_w(bbc, hbc)
                sx = SC1 if a == 0 else SC2
                hcp = p.v(lambda e, b_=bbc, o=sx.ap: e.tensor_copy(o[0:64, :], pb[b_][0:64, :]),
                          deps=bk.rdeps(bbc) + sx.wd())
                sx.w(hcp)
                bk.add_r(bbc, hcp)
                hnm = p.v(lambda e, b_=bav, i=sx.ap, o=SCB.ap[0:64, a * CH:a * CH + CH]:
                          e.tensor_tensor(o, pb[b_][0:64, :], i[0:64, :], ALU.mult),
                          deps=bk.rdeps(bav) + sx.rd() + SCB.wd())
                sx.r(hnm)
                SCB.w(hnm)
                bk.add_r(bav, hnm)
                hst = p.store(ao[hg * 128 + a * 64: hg * 128 + a * 64 + 64,
                                 qc * CH:(qc + 1) * CH], SCB.ap[0:64, a * CH:a * CH + CH],
                              deps=SCB.rd(), stream="ao")
                SCB.r(hst)
                ao_done.append(hst)

    WOF = Buf(PRB2[0].ap)
    WOF.users += PRB2[0].wd()
    AOS.users += QT.wd() + KT.wd()
    for c in range(CT):
        fetch_w(env, WOF, wo[:, c * 1024:(c + 1) * 1024], dst_off=c * 1024)
    for ch in range(NCH):
        lds = []
        for i in range(CT):
            h = p.load(AOS.ap[:, i * CH:(i + 1) * CH],
                       ao[i * 128:(i + 1) * 128, ch * CH:(ch + 1) * CH],
                       deps=ao_done + AOS.wd(), stream="aos")
            AOS.w(h)
            lds.append(h)
        for c in range(CT):
            WOB = WOF
            b = 6 + c % 2
            wd = bk.wdeps(b)
            steps = [(WOB.ap[:, c * 1024 + i * 128:c * 1024 + (i + 1) * 128],
                      AOS.ap[:, i * CH:(i + 1) * CH]) for i in range(CT)]
            hmm = p.mmgrp(pb[b][:, :], steps, deps=wd + WOB.rd() + lds)
            WOB.r(hmm)
            AOS.r(hmm)
            bk.set_w(b, hmm)
            rt = RB[c].ap[:, ch * CH:(ch + 1) * CH]
            if has_bias:
                hev = p.v(lambda e, o=rt, b_=b, cc=c:
                          e.scalar_tensor_tensor(o, pb[b_][:, :], COLS.ap[:, cc:cc + 1], o,
                                                 ALU.add, ALU.add),
                          deps=bk.rdeps(b) + COLS.rd() + RB[c].rd() + RB[c].wd())
                COLS.r(hev)
            else:
                hev = p.v(lambda e, o=rt, b_=b: e.tensor_tensor(o, o, pb[b_][:, :], ALU.add),
                          deps=bk.rdeps(b) + RB[c].rd() + RB[c].wd())
            bk.add_r(b, hev)
            RB[c].w(hev)


def emit_moe(env, ph):
    p, bk, pb, arena = env["p"], env["bk"], env["pb"], env["arena"]
    RB, XLB = env["RB"], env["XLB"]
    SC1, SC2 = env["SC1"], env["SC2"]
    E16, IDT = env["E16"], env["IDT"]
    din, dram_scratch = env["din"], env["dram_scratch"]
    WB_t = env["WB"]
    COLS = env["COLS"]
    HT = 11

    emit_ln(env)

    tag = ph["tag"]
    sg = din(tag + "_sg"); su = din(tag + "_su"); sd = din(tag + "_sd")
    rg = din(tag + "_rg"); ru = din(tag + "_ru"); rd_ = din(tag + "_rd")
    wr = din(tag + "_wr"); rb = din(tag + "_rb")

    AH = arena.bf16(HT * T)
    W7 = arena.bf16(T)
    LGT = arena.f32(CH)
    WTK = arena.f32(16)
    WRP = arena.f32(CH)
    WRB = arena.bf16(64)
    GU = Pool([Buf(WB_t[:, i * 1024:(i + 1) * 1024]) for i in range(4)])
    DBs = [Buf(WB_t[:, 0:1408]), Buf(WB_t[:, 1408:2816])]
    DBp = Pool(DBs)

    fetch_w(env, WRB, wr[:, 0:56], cols=56)
    COLS.w(p.load(COLS.ap[0:7, 8:9], rb, deps=COLS.wd(), stream="cols"))

    resid_spill = dram_scratch("moe_resid", [DM, T], F32)

    for ch in range(NCH):
        wd = bk.wdeps(6)
        steps = [(WRB.ap[:, c * 7:(c + 1) * 7],
                  XLB[c].ap[:, ch * CH:(ch + 1) * CH]) for c in range(CT)]
        hl = p.mmgrp(pb[6][0:7, :], steps, deps=wd + WRB.rd() + _xl_rd(env))
        WRB.r(hl)
        for c in range(CT):
            XLB[c].r(hl)
        bk.set_w(6, hl)
        hev = p.v(lambda e: e.tensor_scalar(LGT.ap[0:7, :], pb[6][0:7, :],
                                            COLS.ap[0:7, 8:9], None, ALU.add),
                  deps=bk.rdeps(6) + COLS.rd() + LGT.wd())
        COLS.r(hev)
        LGT.w(hev)
        bk.add_r(6, hev)
        for tb in range(4):
            t0 = ch * 4 + tb
            wd7 = bk.wdeps(7)
            htr = p.on("tensor",
                       lambda e, s=LGT.ap[0:7, tb * 128:(tb + 1) * 128]:
                       e.transpose(pb[7][:, 0:7], s, IDT.ap[0:7, 0:7]),
                       deps=wd7 + LGT.rd() + IDT.rd(), inc="pe")
            LGT.r(htr)
            IDT.r(htr)
            bk.set_w(7, htr)
            hm = p.v(lambda e: e.reduce_max(WTK.ap[:, 0:1], pb[7][:, 0:7], axis=AX.X),
                     deps=bk.rdeps(7) + WTK.wd())
            WTK.w(hm)
            bk.add_r(7, hm)
            hng = p.v(lambda e: e.tensor_scalar(WTK.ap[:, 1:2], WTK.ap[:, 0:1],
                                                -1.0, None, ALU.mult), deps=WTK.rd())
            WTK.w(hng)
            hex_ = p.s(lambda e: e.activation(SC1.ap[:, 0:7], pb[7][:, 0:7], AF.Exp,
                                              bias=WTK.ap[:, 1:2]),
                       deps=bk.rdeps(7) + WTK.rd() + SC1.wd())
            WTK.r(hex_)
            SC1.w(hex_)
            bk.add_r(7, hex_)
            hs1 = p.s(lambda e: e.activation(SC2.ap[:, 0:7], SC1.ap[:, 0:7], AF.Sign,
                                             bias=-1.0),
                      deps=SC1.rd() + SC2.wd() + env["ones_h"])
            SC1.r(hs1)
            SC2.w(hs1)
            hm1 = p.v(lambda e: e.tensor_scalar(SC2.ap[:, 0:7], SC2.ap[:, 0:7],
                                                1.0, None, ALU.add), deps=SC2.rd())
            SC2.w(hm1)
            hpm = p.v(lambda e: e.scalar_tensor_tensor(SC1.ap[:, 8:15], SC2.ap[:, 0:7],
                                                       -1e30, SC1.ap[:, 0:7],
                                                       ALU.mult, ALU.add),
                      deps=SC2.rd() + SC1.rd())
            SC2.r(hpm)
            SC1.w(hpm)
            hm2 = p.v(lambda e: e.reduce_max(WTK.ap[:, 2:3], SC1.ap[:, 8:15], axis=AX.X),
                      deps=SC1.rd() + WTK.wd())
            WTK.w(hm2)
            hng2 = p.v(lambda e: e.tensor_scalar(WTK.ap[:, 3:4], WTK.ap[:, 2:3],
                                                 -1.0, None, ALU.mult), deps=WTK.rd())
            WTK.w(hng2)
            hs2 = p.s(lambda e: e.activation(SC1.ap[:, 8:15], SC1.ap[:, 8:15], AF.Sign,
                                             bias=WTK.ap[:, 3:4]),
                      deps=WTK.rd() + SC1.rd())
            WTK.r(hs2)
            SC1.w(hs2)
            hm2b = p.v(lambda e: e.tensor_scalar(SC1.ap[:, 8:15], SC1.ap[:, 8:15],
                                                 1.0, None, ALU.add), deps=SC1.rd())
            SC1.w(hm2b)
            hdn = p.v(lambda e: e.tensor_scalar(WTK.ap[:, 4:5], WTK.ap[:, 2:3],
                                                1.0, None, ALU.add), deps=WTK.rd())
            WTK.w(hdn)
            hrdn = p.v(lambda e: e.reciprocal(WTK.ap[:, 4:5], WTK.ap[:, 4:5]),
                       deps=WTK.rd())
            WTK.w(hrdn)
            hw1 = p.v(lambda e: e.scalar_tensor_tensor(SC2.ap[:, 0:7], SC1.ap[:, 8:15],
                                                       WTK.ap[:, 2:3], SC2.ap[:, 0:7],
                                                       ALU.mult, ALU.add),
                      deps=SC1.rd() + WTK.rd() + SC2.rd())
            SC1.r(hw1)
            WTK.r(hw1)
            SC2.w(hw1)
            hw2 = p.v(lambda e: e.tensor_scalar(SC2.ap[:, 0:7], SC2.ap[:, 0:7],
                                                WTK.ap[:, 4:5], None, ALU.mult),
                      deps=SC2.rd() + WTK.rd())
            WTK.r(hw2)
            SC2.w(hw2)
            wd7b = bk.wdeps(7)
            htb = p.on("tensor", lambda e: e.transpose(pb[7][0:7, 0:128], SC2.ap[:, 0:7],
                                                       IDT.ap[:, :]),
                       deps=wd7b + SC2.rd() + IDT.rd(), inc="pe")
            SC2.r(htb)
            IDT.r(htb)
            bk.set_w(7, htb)
            hwv = p.v(lambda e, o=W7.ap[0:7, t0 * 128:(t0 + 1) * 128]:
                      e.tensor_copy(o, pb[7][0:7, 0:128]),
                      deps=bk.rdeps(7) + (W7.wd() if t0 == 0 else []))
            bk.add_r(7, hwv)
            W7.w(hwv)

    if ph.get("dbg_xl"):
        dbx = env["nc"].dram_tensor("dbg_xl", [DM, T], BF16, kind="ExternalOutput").ap()
        for c in range(CT):
            h = p.store(dbx[c * 128:(c + 1) * 128, :], XLB[c].ap, deps=XLB[c].rd(),
                        stream="dbg")
            XLB[c].r(h)
    if ph.get("dbg"):
        dbg = env["nc"].dram_tensor("dbg_w", [16, T], F32, kind="ExternalOutput").ap()
        dbw = p.v(lambda e: e.tensor_copy(SC1.ap[0:16, 0:7], SC1.ap[0:16, 0:7]),
                  deps=[])
        hst = p.store(dbg[0:7, :], W7.ap[0:7, :], deps=[W7.writer], stream="dbg")
        W7.r(hst)

    sp_hs = []
    for c in range(CT):
        sp_hs.append(p.store(resid_spill[c * 128:(c + 1) * 128, :], RB[c].ap,
                             deps=RB[c].rd(), stream="res"))
    for c in range(CT):
        RB[c].r(sp_hs[-1])

    n_ex = 1 if ph.get("only_shared") else 8
    for ex in range(n_ex):
        if ex == 0:
            gT, uT, dT = sg, su, sd
        else:
            gT, uT, dT = rg[ex - 1], ru[ex - 1], rd_[ex - 1]
        for half in range(2):
            cross(GU.bufs, DBs)
            a_done = []
            for ib in range(HT):
                i0 = half * HT + ib
                WGB = GU.next()
                WUB = GU.next()
                fetch_w(env, WGB, gT[:, i0 * 1024:(i0 + 1) * 1024])
                fetch_w(env, WUB, uT[:, i0 * 1024:(i0 + 1) * 1024])
                for ch in range(NCH):
                    bg = ch % 2
                    bu = 2 + ch % 2
                    wdg = bk.wdeps(bg)
                    steps = [(WGB.ap[:, c * 128:(c + 1) * 128],
                              XLB[c].ap[:, ch * CH:(ch + 1) * CH]) for c in range(CT)]
                    hgm = p.mmgrp(pb[bg][:, :], steps,
                                  deps=wdg + WGB.rd() + _xl_rd(env))
                    WGB.r(hgm)
                    for c in range(CT):
                        XLB[c].r(hgm)
                    bk.set_w(bg, hgm)
                    wdu = bk.wdeps(bu)
                    steps = [(WUB.ap[:, c * 128:(c + 1) * 128],
                              XLB[c].ap[:, ch * CH:(ch + 1) * CH]) for c in range(CT)]
                    hum = p.mmgrp(pb[bu][:, :], steps, deps=wdu + WUB.rd())
                    WUB.r(hum)
                    for c in range(CT):
                        XLB[c].r(hum)
                    bk.set_w(bu, hum)
                    hsi = p.s(lambda e, b_=bg: e.activation(SC1.ap, pb[b_][:, :], AF.Silu),
                              deps=bk.rdeps(bg) + SC1.wd())
                    SC1.w(hsi)
                    bk.add_r(bg, hsi)
                    ha = p.v(lambda e, o=AH.ap[:, ib * T + ch * CH: ib * T + (ch + 1) * CH],
                             b_=bu: e.tensor_tensor(o, SC1.ap, pb[b_][:, :], ALU.mult),
                             deps=bk.rdeps(bu) + SC1.rd() +
                             (AH.wd() if (ib == 0 and ch == 0) else []))
                    SC1.r(ha)
                    bk.add_r(bu, ha)
                    AH.w(ha)
                    a_done.append(ha)
            cross(DBs, GU.bufs)
            for c in range(CT):
                DB = DBp.next()
                base_ = (half * 8 + c) * 1408
                fetch_w(env, DB, dT[:, base_:base_ + 768], dst_off=0, cols=768)
                fetch_w(env, DB, dT[:, base_ + 768:base_ + 1408],
                        dst_off=768, cols=640)
                for ch in range(NCH):
                    b = 4 + (ch % 2)
                    wd = bk.wdeps(b)
                    steps = [(DB.ap[:, ib * 128:(ib + 1) * 128],
                              AH.ap[:, ib * T + ch * CH: ib * T + (ch + 1) * CH])
                             for ib in range(HT)]
                    hmm = p.mmgrp(pb[b][:, :], steps, deps=wd + DB.rd() + a_done)
                    DB.r(hmm)
                    AH.r(hmm)
                    bk.set_w(b, hmm)
                    rt = RB[c].ap[:, ch * CH:(ch + 1) * CH]
                    if ex == 0:
                        if half == 0:
                            hev = p.s(lambda e, o=rt, b_=b:
                                      e.activation(o, pb[b_][:, :], AF.Copy),
                                      deps=bk.rdeps(b) + RB[c].wd())
                        else:
                            hev = p.v(lambda e, o=rt, b_=b:
                                      e.tensor_tensor(o, o, pb[b_][:, :], ALU.add),
                                      deps=bk.rdeps(b) + RB[c].rd())
                        bk.add_r(b, hev)
                        RB[c].w(hev)
                    else:
                        wd6 = bk.wdeps(6)
                        hwre = p.mm(pb[6][:, :], E16.ap[0:7, (ex - 1) * 128:ex * 128],
                                    W7.ap[0:7, ch * CH:(ch + 1) * CH], True, True,
                                    deps=wd6 + E16.rd() + W7.rd(), inc=True)
                        E16.r(hwre)
                        W7.r(hwre)
                        bk.set_w(6, hwre)
                        hcp = p.s(lambda e: e.activation(WRP.ap, pb[6][:, :], AF.Copy),
                                  deps=bk.rdeps(6) + WRP.wd())
                        WRP.w(hcp)
                        bk.add_r(6, hcp)
                        hmu = p.v(lambda e, b_=b: e.tensor_tensor(SC2.ap, pb[b_][:, :],
                                                                  WRP.ap, ALU.mult),
                                  deps=bk.rdeps(b) + WRP.rd() + SC2.wd())
                        WRP.r(hmu)
                        SC2.w(hmu)
                        bk.add_r(b, hmu)
                        hev = p.v(lambda e, o=rt: e.tensor_tensor(o, o, SC2.ap, ALU.add),
                                  deps=SC2.rd() + RB[c].rd())
                        SC2.r(hev)
                        RB[c].w(hev)

    if ph.get("only_shared"):
        return
    for c in range(CT):
        for ch in range(NCH):
            sb = SC1 if ch % 2 == 0 else SC2
            ld = p.load(sb.ap, resid_spill[c * 128:(c + 1) * 128, ch * CH:(ch + 1) * CH],
                        deps=sb.wd(), stream="sc1" if ch % 2 == 0 else "sc2")
            sb.w(ld)
            rt = RB[c].ap[:, ch * CH:(ch + 1) * CH]
            ha = p.v(lambda e, o=rt, i=sb.ap: e.tensor_tensor(o, o, i, ALU.add),
                     deps=sb.rd() + RB[c].rd())
            sb.r(ha)
            RB[c].w(ha)


def emit_l1ffn(env, ph):
    p, bk, pb, arena = env["p"], env["bk"], env["pb"], env["arena"]
    RB, XLB = env["RB"], env["XLB"]
    SC1, SC2 = env["SC1"], env["SC2"]
    din, dram_scratch = env["din"], env["dram_scratch"]
    WB_t = env["WB"]
    COLS = env["COLS"]

    emit_ln(env)

    tag = ph["tag"]
    wr_ = din(tag + "_wr"); wk_ = din(tag + "_wk"); wv_ = din(tag + "_wv")
    hmu = COLS.w(p.load(COLS.ap[:, 0:16], din(tag + "_cols"), deps=COLS.wd(), stream="cols"))
    homu = COLS.w(p.v(lambda e: e.tensor_scalar(COLS.ap[:, 16:32], COLS.ap[:, 0:16],
                                                -1.0, 1.0, ALU.mult, ALU.add),
                      deps=COLS.rd() + COLS.wd()))

    XRK = arena.bf16(T * CT)
    TMP = arena.bf16(T)
    WP = Pool([Buf(WB_t[:, i * 1024:(i + 1) * 1024]) for i in range(4)])
    r_spill = dram_scratch(tag + "_r", [DM, T], BF16)

    def mix(dst_tile, mu_off, omu_off, c, dst_buf):
        xt = XLB[c].ap
        h1 = p.v(lambda e: e.tensor_scalar(TMP.ap[:, 1:T], xt[:, 0:T - 1],
                                           COLS.ap[:, omu_off + c:omu_off + c + 1],
                                           None, ALU.mult),
                 deps=COLS.rd() + XLB[c].rd() + TMP.wd())
        COLS.r(h1)
        XLB[c].r(h1)
        TMP.w(h1)
        h2 = p.v(lambda e: e.memset(TMP.ap[:, 0:1], 0.0), deps=[])
        TMP.w(h2)
        h3 = p.v(lambda e, o=dst_tile: e.scalar_tensor_tensor(
            o, xt, COLS.ap[:, mu_off + c:mu_off + c + 1], TMP.ap, ALU.mult, ALU.add),
            deps=TMP.rd() + COLS.rd() + XLB[c].rd() + dst_buf.wd())
        COLS.r(h3)
        XLB[c].r(h3)
        TMP.r(h3)
        dst_buf.w(h3)
        return h3

    xr_h = []
    for c in range(CT):
        xr_h.append(mix(XRK.ap[:, c * T:(c + 1) * T], 0, 16, c, XRK))
    r_done = []
    sb = env["SCB"]
    for et in range(CT):
        WBu = WP.next()
        fetch_w(env, WBu, wr_[:, et * 1024:(et + 1) * 1024])
        for ch in range(NCH):
            b = ch % 2
            wd = bk.wdeps(b)
            steps = [(WBu.ap[:, c * 128:(c + 1) * 128],
                      XRK.ap[:, c * T + ch * CH: c * T + (ch + 1) * CH]) for c in range(CT)]
            hmm = p.mmgrp(pb[b][:, :], steps, deps=wd + WBu.rd() + xr_h)
            WBu.r(hmm)
            XRK.r(hmm)
            bk.set_w(b, hmm)
            hev = p.s(lambda e, o=sb.ap[:, 0:CH], b_=b:
                      e.activation(o, pb[b_][:, :], AF.Sigmoid),
                      deps=bk.rdeps(b) + sb.wd())
            sb.w(hev)
            bk.add_r(b, hev)
            hst = p.store(r_spill[et * 128:(et + 1) * 128, ch * CH:(ch + 1) * CH],
                          sb.ap[:, 0:CH], deps=sb.rd(), stream="rsp")
            sb.r(hst)
            r_done.append(hst)
    xk_h = []
    for c in range(CT):
        xk_h.append(mix(XLB[c].ap, 8, 24, c, XLB[c]))
    ka_h = []
    for et in range(CT):
        WBu = WP.next()
        fetch_w(env, WBu, wk_[:, et * 1024:(et + 1) * 1024])
        for ch in range(NCH):
            b = ch % 2
            wd = bk.wdeps(b)
            steps = [(WBu.ap[:, c * 128:(c + 1) * 128],
                      XLB[c].ap[:, ch * CH:(ch + 1) * CH]) for c in range(CT)]
            hmm = p.mmgrp(pb[b][:, :], steps, deps=wd + WBu.rd() + xk_h + XRK.wd())
            WBu.r(hmm)
            for c in range(CT):
                XLB[c].r(hmm)
            bk.set_w(b, hmm)
            h1 = p.s(lambda e, b_=b: e.activation(SC1.ap, pb[b_][:, :], AF.Relu),
                     deps=bk.rdeps(b) + SC1.wd())
            SC1.w(h1)
            bk.add_r(b, h1)
            h2 = p.s(lambda e, o=XRK.ap[:, et * T + ch * CH: et * T + (ch + 1) * CH]:
                     e.activation(o, SC1.ap, AF.Square), deps=SC1.rd())
            SC1.r(h2)
            XRK.w(h2)
            ka_h.append(h2)
    for c in range(CT):
        WBu = WP.next()
        fetch_w(env, WBu, wv_[:, c * 1024:(c + 1) * 1024])
        for ch in range(NCH):
            b = 2 + ch % 2
            wd = bk.wdeps(b)
            steps = [(WBu.ap[:, e_ * 128:(e_ + 1) * 128],
                      XRK.ap[:, e_ * T + ch * CH: e_ * T + (ch + 1) * CH])
                     for e_ in range(CT)]
            hmm = p.mmgrp(pb[b][:, :], steps, deps=wd + WBu.rd() + ka_h)
            WBu.r(hmm)
            XRK.r(hmm)
            bk.set_w(b, hmm)
            ldr = p.load(sb.ap[:, 0:CH], r_spill[c * 128:(c + 1) * 128,
                                                 ch * CH:(ch + 1) * CH],
                         deps=r_done + sb.wd(), stream="scb")
            sb.w(ldr)
            hmu2 = p.v(lambda e, b_=b: e.tensor_tensor(SC1.ap, pb[b_][:, :],
                                                       sb.ap[:, 0:CH], ALU.mult),
                       deps=bk.rdeps(b) + sb.rd() + SC1.wd())
            sb.r(hmu2)
            SC1.w(hmu2)
            bk.add_r(b, hmu2)
            rt = RB[c].ap[:, ch * CH:(ch + 1) * CH]
            hev = p.v(lambda e, o=rt: e.tensor_tensor(o, o, SC1.ap, ALU.add),
                      deps=SC1.rd() + RB[c].rd() + RB[c].wd())
            SC1.r(hev)
            RB[c].w(hev)


def emit_mamba(env, ph):
    p, bk, pb, arena = env["p"], env["bk"], env["pb"], env["arena"]
    RB, XLB = env["RB"], env["XLB"]
    SC1, SC2, SCB = env["SC1"], env["SC2"], env["SCB"]
    SLOT = env["SLOT"]
    ROWA, ROWB, ROWC = env["ROWA"], env["ROWB"], env["ROWC"]
    ONEC, ONER = env["ONEC"], env["ONER"]
    E16 = env["E16"]
    HC, AS = env["HC"], env["AS"]
    din, dram_scratch = env["din"], env["dram_scratch"]
    WB_t = env["WB"]
    COLS = env["COLS"]
    R_t = env["R_t"]
    DT_ = 16

    emit_ln(env)

    tag = ph["tag"]
    ipw = din(tag + "_ipw")
    xpw = din(tag + "_xpw")
    dpw = din(tag + "_dpw")
    opw = din(tag + "_opw")
    f1w = din(tag + "_f1w")
    f2w = din(tag + "_f2w")
    alog = din(tag + "_alog")
    mcols = din(tag + "_cols")
    COLS.w(p.load(COLS.ap[:], mcols, deps=COLS.wd(), stream="cols"))

    AS.w(p.load(AS.ap, alog, deps=AS.wd(), stream="as"))
    HCB = [Buf(HC.ap[:, i:i + 1]) for i in range(256)]
    AS.w(p.s(lambda e: e.activation(AS.ap, AS.ap, AF.Exp), deps=AS.rd() + AS.wd()))
    AS.w(p.v(lambda e: e.tensor_scalar(AS.ap, AS.ap, -1.0, None, ALU.mult),
             deps=AS.rd() + AS.wd()))
    hmz = p.v(lambda e: e.memset(HC.ap, 0.0), deps=HC.wd())
    HC.w(hmz)
    for b_ in HCB:
        b_.w(hmz)

    xs_spill = dram_scratch(tag + "_xs", [2048, T], BF16)
    z_spill = dram_scratch(tag + "_z", [2048, T], BF16)
    mo_spill = dram_scratch(tag + "_mo", [DM, T], BF16)
    resid_spill = dram_scratch(tag + "_res", [DM, T], F32)

    XSC = arena.bf16_at(0, 8192)
    DTC = arena.bf16_at(8192, 8192)
    BRP = arena.bf16_at(25088, 512)
    CRP = arena.bf16_at(25600, 512)
    DTW = arena.bf16_at(16384, 2048)
    DTRB = arena.bf16_at(18432, 2048)
    BT = arena.bf16_at(20480, 2048)
    CTb = arena.bf16_at(22528, 2048)
    MOS = arena.bf16_at(24576, 512)
    XPW = arena.bf16_at(8192, 1536)
    FC = arena.bf16_at(0, 8192)
    S2 = arena.f32_at(8192, 4096)
    WP4 = Pool([Buf(WB_t[:, i * 1024:(i + 1) * 1024]) for i in range(4)])
    WP2 = Pool([Buf(WB_t[:, 0:2048]), Buf(WB_t[:, 2048:4096])])

    CONVT = Buf(R_t[:, 0:2052])
    CACC = Buf(R_t[:, 2052:4100])
    Y = Buf(R_t[:, 4104:12296])

    wst = env["WSTb"][0]
    for hx in range(2):
        ld = p.load(wst.ap[0:64, 0:1024], dpw[:, hx * 1024:(hx + 1) * 1024],
                    deps=wst.wd(), stream="wst")
        wst.w(ld)
        cv = p.v(lambda e, o=DTW.ap[0:64, hx * 1024:(hx + 1) * 1024]:
                 e.tensor_copy(o, wst.ap[0:64, 0:1024]),
                 deps=wst.rd() + (DTW.wd() if hx == 0 else []))
        wst.r(cv)
        DTW.w(cv)

    sp_h = []
    for c in range(CT):
        h = p.store(resid_spill[c * 128:(c + 1) * 128, :], RB[c].ap,
                    deps=RB[c].rd(), stream="res")
        sp_h.append(h)
    for c in range(CT):
        RB[c].r(sp_h[-1])
    CONVT.w(sp_h[-1])
    CACC.w(sp_h[-1])
    Y.w(sp_h[-1])

    hz3 = CONVT.w(p.v(lambda e: e.memset(CONVT.ap[:, 0:3], 0.0), deps=CONVT.wd()))
    xs_done = []
    z_done = []
    for et in range(32):
        IPWb = WP4.next()
        fetch_w(env, IPWb, ipw[:, et * 1024:(et + 1) * 1024])
        is_xs = et < 16
        for ch in range(NCH):
            b = ch % 2
            wd = bk.wdeps(b)
            steps = [(IPWb.ap[:, c * 128:(c + 1) * 128],
                      XLB[c].ap[:, ch * CH:(ch + 1) * CH]) for c in range(CT)]
            hmm = p.mmgrp(pb[b][:, :], steps, deps=wd + IPWb.rd() + _xl_rd(env))
            IPWb.r(hmm)
            for c in range(CT):
                XLB[c].r(hmm)
            bk.set_w(b, hmm)
            if is_xs:
                hev = p.s(lambda e, o=CONVT.ap[:, 3 + ch * CH: 3 + (ch + 1) * CH], b_=b:
                          e.activation(o, pb[b_][:, :], AF.Copy),
                          deps=bk.rdeps(b) + (CONVT.wd() if ch == 0 else []))
                CONVT.w(hev)
                bk.add_r(b, hev)
            else:
                hev = p.s(lambda e, o=SCB.ap[:, ch * CH:(ch + 1) * CH], b_=b:
                          e.activation(o, pb[b_][:, :], AF.Copy),
                          deps=bk.rdeps(b) + (SCB.wd() if ch == 0 else []))
                SCB.w(hev)
                bk.add_r(b, hev)
        if is_xs:
            h1 = p.v(lambda e, w=COLS.ap[:, et * 4:et * 4 + 1]:
                     e.tensor_scalar(CACC.ap, CONVT.ap[:, 0:T], w, None, ALU.mult),
                     deps=COLS.rd() + CONVT.rd() + CACC.wd())
            COLS.r(h1)
            CONVT.r(h1)
            CACC.w(h1)
            for j in (1, 2, 3):
                hj = p.v(lambda e, j_=j, w=COLS.ap[:, et * 4 + j:et * 4 + j + 1]:
                         e.scalar_tensor_tensor(CACC.ap, CONVT.ap[:, j_:j_ + T], w,
                                                CACC.ap, ALU.mult, ALU.add),
                         deps=COLS.rd() + CONVT.rd() + CACC.rd())
                COLS.r(hj)
                CONVT.r(hj)
                CACC.w(hj)
            hsl = p.s(lambda e, bc=COLS.ap[:, 128 + et:129 + et]:
                      e.activation(SCB.ap, CACC.ap, AF.Silu, bias=bc),
                      deps=CACC.rd() + SCB.wd() + COLS.rd())
            COLS.r(hsl)
            CACC.r(hsl)
            SCB.w(hsl)
            hst = p.store(xs_spill[et * 128:(et + 1) * 128, :], SCB.ap, deps=SCB.rd(),
                          stream="xs")
            SCB.r(hst)
            xs_done.append(hst)
        else:
            hst = p.store(z_spill[(et - 16) * 128:(et - 15) * 128, :], SCB.ap,
                          deps=SCB.rd(), stream="z")
            SCB.r(hst)
            z_done.append(hst)

    fetch_w(env, XPW, xpw[:, 0:768], dst_off=0, cols=768)
    fetch_w(env, XPW, xpw[:, 768:1536], dst_off=768, cols=768)
    for ch in range(NCH):
        w0 = bk.wdeps(0)
        w1 = bk.wdeps(1)
        w2 = bk.wdeps(2)
        hdtr = hB = hC = None
        for dt in range(DT_):
            sl = SLOT[dt % 2]
            ld = p.load(sl.ap, xs_spill[dt * 128:(dt + 1) * 128, ch * CH:(ch + 1) * CH],
                        deps=xs_done + sl.wd(), stream=f"slot{dt % 2}")
            sl.w(ld)
            hdtr = p.mm(pb[0][0:64, :], XPW.ap[:, dt * 96:dt * 96 + 64], sl.ap,
                        dt == 0, dt == DT_ - 1,
                        deps=(w0 + XPW.rd() + sl.rd()) if dt == 0 else sl.rd(),
                        inc=True)
            sl.r(hdtr)
            XPW.r(hdtr)
            hB = p.mm(pb[1][0:16, :], XPW.ap[:, dt * 96 + 64:dt * 96 + 80], sl.ap,
                      dt == 0, dt == DT_ - 1,
                      deps=(w1 + sl.rd()) if dt == 0 else sl.rd(),
                      inc=True)
            sl.r(hB)
            hC = p.mm(pb[2][0:16, :], XPW.ap[:, dt * 96 + 80:dt * 96 + 96], sl.ap,
                      dt == 0, dt == DT_ - 1,
                      deps=(w2 + sl.rd()) if dt == 0 else sl.rd(),
                      inc=True)
            sl.r(hC)
        bk.set_w(0, hdtr)
        bk.set_w(1, hB)
        bk.set_w(2, hC)
        SLOT[0].r(hC)
        SLOT[1].r(hC)
        h1 = p.s(lambda e, o=DTRB.ap[0:64, ch * CH:(ch + 1) * CH]:
                 e.activation(o, pb[0][0:64, :], AF.Copy),
                 deps=bk.rdeps(0) + (DTRB.wd() if ch == 0 else []))
        DTRB.w(h1)
        bk.add_r(0, h1)
        h2 = p.v(lambda e, o=BT.ap[0:16, ch * CH:(ch + 1) * CH]:
                 e.tensor_copy(o, pb[1][0:16, :]),
                 deps=bk.rdeps(1) + (BT.wd() if ch == 0 else []))
        BT.w(h2)
        bk.add_r(1, h2)
        h3 = p.v(lambda e, o=CTb.ap[0:16, ch * CH:(ch + 1) * CH]:
                 e.tensor_copy(o, pb[2][0:16, :]),
                 deps=bk.rdeps(2) + (CTb.wd() if ch == 0 else []))
        CTb.w(h3)
        bk.add_r(2, h3)
    DTC.r(bk.st[0]["w"])
    DTC.r(bk.st[1]["w"])
    DTC.r(bk.st[2]["w"])
    DTC.users += XPW.wd()

    cross(WP2.bufs, WP4.bufs)
    mo_done = []
    for ch in range(NCH):
        for dt in range(DT_):
            h = p.load(XSC.ap[:, dt * CH:(dt + 1) * CH],
                       xs_spill[dt * 128:(dt + 1) * 128, ch * CH:(ch + 1) * CH],
                       deps=xs_done + (XSC.wd() if dt == 0 else []), stream="xsc")
            XSC.w(h)
        for dt in range(DT_):
            b = dt % 2
            wd = bk.wdeps(b)
            hdt = p.mm(pb[b][:, :], DTW.ap[0:64, dt * 128:(dt + 1) * 128],
                       DTRB.ap[0:64, ch * CH:(ch + 1) * CH], True, True,
                       deps=wd + DTW.rd() + DTRB.rd(), inc=True)
            DTW.r(hdt)
            DTRB.r(hdt)
            bk.set_w(b, hdt)
            bc = COLS.ap[:, 80 + dt:81 + dt]
            ha = p.s(lambda e, b_=b, bc_=bc: e.activation(SC1.ap, pb[b_][:, :], AF.Abs,
                                                          bias=bc_),
                     deps=bk.rdeps(b) + SC1.wd() + COLS.rd())
            COLS.r(ha)
            SC1.w(ha)
            bk.add_r(b, ha)
            hb = p.s(lambda e: e.activation(SC1.ap, SC1.ap, AF.Exp, scale=-1.0),
                     deps=SC1.rd())
            SC1.w(hb)
            hc = p.s(lambda e: e.activation(SC1.ap, SC1.ap, AF.Ln, bias=1.0),
                     deps=SC1.rd())
            SC1.w(hc)
            hd = p.s(lambda e, b_=b, bc_=bc: e.activation(SC2.ap, pb[b_][:, :], AF.Relu,
                                                          bias=bc_),
                     deps=bk.rdeps(b) + SC2.wd() + COLS.rd())
            COLS.r(hd)
            SC2.w(hd)
            bk.add_r(b, hd)
            he = p.v(lambda e, o=DTC.ap[:, dt * CH:(dt + 1) * CH]:
                     e.tensor_tensor(o, SC1.ap, SC2.ap, ALU.add),
                     deps=SC1.rd() + SC2.rd() + (DTC.wd() if dt == 0 else []))
            SC1.r(he)
            SC2.r(he)
            DTC.w(he)
            hf = p.v(lambda e, o=XSC.ap[:, dt * CH:(dt + 1) * CH],
                     i=DTC.ap[:, dt * CH:(dt + 1) * CH]:
                     e.tensor_tensor(o, i, o, ALU.mult),
                     deps=DTC.rd() + XSC.rd() + XSC.wd())
            DTC.r(hf)
            XSC.w(hf)
        hy0 = Y.w(p.v(lambda e: e.memset(Y.ap, 0.0), deps=Y.wd()))
        for n in range(16):
            w6 = bk.wdeps(6)
            hbr = p.mm(pb[6][:, :], E16.ap[0:16, n * 128:(n + 1) * 128],
                       BT.ap[0:16, ch * CH:(ch + 1) * CH], True, True,
                       deps=w6 + E16.rd() + BT.rd(), inc=True)
            E16.r(hbr)
            BT.r(hbr)
            bk.set_w(6, hbr)
            hbe = p.s(lambda e: e.activation(BRP.ap, pb[6][:, :], AF.Copy),
                      deps=bk.rdeps(6) + BRP.wd())
            BRP.w(hbe)
            bk.add_r(6, hbe)
            w7 = bk.wdeps(7)
            hcr = p.mm(pb[7][:, :], E16.ap[0:16, n * 128:(n + 1) * 128],
                       CTb.ap[0:16, ch * CH:(ch + 1) * CH], True, True,
                       deps=w7 + E16.rd() + CTb.rd(), inc=True)
            E16.r(hcr)
            CTb.r(hcr)
            bk.set_w(7, hcr)
            hce = p.s(lambda e: e.activation(CRP.ap, pb[7][:, :], AF.Copy),
                      deps=bk.rdeps(7) + CRP.wd())
            CRP.w(hce)
            bk.add_r(7, hce)
            for dt in range(DT_):
                it = n * DT_ + dt
                _set = 3 * (it % 2)
                sA, sB, sC = SLOT[_set], SLOT[_set + 1], SLOT[_set + 2]
                scan_v = (it % 2 == 0)
                hda = p.s(lambda e, i=DTC.ap[:, dt * CH:(dt + 1) * CH],
                          sc=AS.ap[:, dt * 16 + n:dt * 16 + n + 1], o=sA.ap:
                          e.activation(o, i, AF.Exp, scale=sc),
                          deps=DTC.rd() + AS.rd() + sA.wd())
                DTC.r(hda)
                AS.r(hda)
                sA.w(hda)
                hdb = p.v(lambda e, i=XSC.ap[:, dt * CH:(dt + 1) * CH], o=sB.ap:
                          e.tensor_tensor(o, i, BRP.ap, ALU.mult),
                          deps=XSC.rd() + BRP.rd() + sB.wd())
                XSC.r(hdb)
                BRP.r(hdb)
                sB.w(hdb)
                hcb = HCB[dt * 16 + n]
                hsc = p.v(lambda e, ini=hcb.ap,
                          o=sC.ap, a=sA.ap, b2=sB.ap:
                          e.tensor_tensor_scan(o, a, b2, ini, ALU.mult, ALU.add),
                          deps=sA.rd() + sB.rd() + hcb.rd() + sC.wd())
                sA.r(hsc)
                sB.r(hsc)
                hcb.r(hsc)
                sC.w(hsc)
                hcy = p.on("gpsimd", lambda e, o=hcb.ap,
                           i=sC.ap[:, 511:512]: e.tensor_copy(o, i),
                           deps=sC.rd() + hcb.wd(), inc="ge")
                sC.r(hcy)
                hcb.w(hcy)
                hym = p.v(lambda e, i=sC.ap, o=sB.ap:
                          e.tensor_tensor(o, i, CRP.ap, ALU.mult),
                          deps=sC.rd() + CRP.rd() + sB.wd())
                sC.r(hym)
                CRP.r(hym)
                sB.w(hym)
                hya = p.on("gpsimd",
                           lambda e, o=Y.ap[:, dt * CH:(dt + 1) * CH], i=sB.ap:
                           e.tensor_tensor(o, o, i, ALU.add),
                           deps=sB.rd() + Y.rd() + Y.wd(), inc="ge")
                sB.r(hya)
                Y.w(hya)
        for dt in range(DT_):
            h = p.load(XSC.ap[:, dt * CH:(dt + 1) * CH],
                       xs_spill[dt * 128:(dt + 1) * 128, ch * CH:(ch + 1) * CH],
                       deps=XSC.wd() if dt == 0 else (), stream="xsc")
            XSC.w(h)
            h = p.load(DTC.ap[:, dt * CH:(dt + 1) * CH],
                       z_spill[dt * 128:(dt + 1) * 128, ch * CH:(ch + 1) * CH],
                       deps=(z_done + DTC.wd()) if dt == 0 else (), stream="dtc")
            DTC.w(h)
        for dt in range(DT_):
            hya = p.v(lambda e, o=Y.ap[:, dt * CH:(dt + 1) * CH],
                      i=XSC.ap[:, dt * CH:(dt + 1) * CH],
                      d=COLS.ap[:, 64 + dt:65 + dt]:
                      e.scalar_tensor_tensor(o, i, d, o, ALU.mult, ALU.add),
                      deps=XSC.rd() + COLS.rd() + Y.rd() + Y.wd())
            XSC.r(hya)
            COLS.r(hya)
            Y.w(hya)
            sZ = SLOT[5]
            hsz = p.s(lambda e, i=DTC.ap[:, dt * CH:(dt + 1) * CH], o=sZ.ap:
                      e.activation(o, i, AF.Silu), deps=DTC.rd() + sZ.wd())
            DTC.r(hsz)
            sZ.w(hsz)
            hmx = p.v(lambda e, o=XSC.ap[:, dt * CH:(dt + 1) * CH],
                      y=Y.ap[:, dt * CH:(dt + 1) * CH], s=sZ.ap:
                      e.tensor_tensor(o, y, s, ALU.mult),
                      deps=Y.rd() + sZ.rd() + XSC.rd() + XSC.wd())
            sZ.r(hmx)
            Y.r(hmx)
            XSC.w(hmx)
        for c in range(CT):
            OPW = WP2.next()
            fetch_w(env, OPW, opw[:, c * 2048:c * 2048 + 1024], dst_off=0, cols=1024)
            fetch_w(env, OPW, opw[:, c * 2048 + 1024:(c + 1) * 2048],
                    dst_off=1024, cols=1024)
            b = 4 + c % 2
            wd = bk.wdeps(b)
            steps = [(OPW.ap[:, dt * 128:(dt + 1) * 128],
                      XSC.ap[:, dt * CH:(dt + 1) * CH]) for dt in range(DT_)]
            hmm = p.mmgrp(pb[b][:, :], steps, deps=wd + OPW.rd() + XSC.rd())
            OPW.r(hmm)
            XSC.r(hmm)
            bk.set_w(b, hmm)
            hev = p.s(lambda e, b_=b: e.activation(MOS.ap[:, 0:CH], pb[b_][:, :], AF.Copy),
                      deps=bk.rdeps(b) + MOS.wd())
            MOS.w(hev)
            bk.add_r(b, hev)
            hst = p.store(mo_spill[c * 128:(c + 1) * 128, ch * CH:(ch + 1) * CH],
                          MOS.ap[:, 0:CH], deps=MOS.rd(), stream="mo")
            MOS.r(hst)
            mo_done.append(hst)

    S2.users += DTC.wd()
    for ch in range(NCH):
        for c in range(CT):
            hld = p.load(MOS.ap[:, 0:CH],
                         mo_spill[c * 128:(c + 1) * 128, ch * CH:(ch + 1) * CH],
                         deps=mo_done + MOS.wd(), stream="mos")
            MOS.w(hld)
            hs = p.v(lambda e, o=S2.ap[:, c * CH:(c + 1) * CH],
                     x=XLB[c].ap[:, ch * CH:(ch + 1) * CH]:
                     e.tensor_tensor(o, x, MOS.ap[:, 0:CH], ALU.add),
                     deps=MOS.rd() + XLB[c].rd() + (S2.wd() if c == 0 else []))
            MOS.r(hs)
            XLB[c].r(hs)
            S2.w(hs)
        _affine_ln(env, ch, S2, 1e-12, 96, 104, dst="XL")
    r_free = Y.wd() + CONVT.wd() + CACC.wd()
    FC.users += XSC.wd()
    env["_RESID_SPILL"] = resid_spill
    env["_RFREE"] = r_free
    for ch in range(NCH):
        for et in range(16):
            F1W = WP2.next()
            fetch_w(env, F1W, f1w[:, et * 1024:(et + 1) * 1024])
            b = et % 2
            wd = bk.wdeps(b)
            steps = [(F1W.ap[:, c * 128:(c + 1) * 128],
                      XLB[c].ap[:, ch * CH:(ch + 1) * CH]) for c in range(CT)]
            hmm = p.mmgrp(pb[b][:, :], steps, deps=wd + F1W.rd() + _xl_rd(env) +
                          (FC.wd() if et == 0 else []))
            F1W.r(hmm)
            for c in range(CT):
                XLB[c].r(hmm)
            bk.set_w(b, hmm)
            hb1 = p.v(lambda e, b_=b, bc=COLS.ap[:, 144 + et:145 + et]:
                      e.tensor_scalar(SC1.ap, pb[b_][:, :], bc, None, ALU.add),
                      deps=bk.rdeps(b) + COLS.rd() + SC1.wd())
            COLS.r(hb1)
            SC1.w(hb1)
            bk.add_r(b, hb1)
            hlr = p.v(lambda e, o=FC.ap[:, et * CH:(et + 1) * CH]:
                      e.scalar_tensor_tensor(o, SC1.ap, 0.01, SC1.ap, ALU.mult, ALU.max),
                      deps=SC1.rd())
            SC1.r(hlr)
            FC.w(hlr)
        for c in range(CT):
            F2W = WP2.next()
            fetch_w(env, F2W, f2w[:, c * 2048:c * 2048 + 1024], dst_off=0, cols=1024)
            fetch_w(env, F2W, f2w[:, c * 2048 + 1024:(c + 1) * 2048],
                    dst_off=1024, cols=1024)
            b = 4 + c % 2
            wd = bk.wdeps(b)
            steps = [(F2W.ap[:, et * 128:(et + 1) * 128],
                      FC.ap[:, et * CH:(et + 1) * CH]) for et in range(16)]
            hmm = p.mmgrp(pb[b][:, :], steps, deps=wd + F2W.rd() + FC.rd())
            F2W.r(hmm)
            FC.r(hmm)
            bk.set_w(b, hmm)
            hb2 = p.v(lambda e, b_=b, bc=COLS.ap[:, 160 + c:161 + c]:
                      e.tensor_scalar(SC1.ap, pb[b_][:, :], bc, None, ALU.add),
                      deps=bk.rdeps(b) + SC1.wd() + COLS.rd())
            COLS.r(hb2)
            SC1.w(hb2)
            bk.add_r(b, hb2)
            hs2 = p.v(lambda e, o=S2.ap[:, c * CH:(c + 1) * CH],
                      x=XLB[c].ap[:, ch * CH:(ch + 1) * CH]:
                      e.tensor_tensor(o, SC1.ap, x, ALU.add),
                      deps=SC1.rd() + XLB[c].rd() + (S2.wd() if c == 0 else []))
            SC1.r(hs2)
            XLB[c].r(hs2)
            S2.w(hs2)
        _affine_ln(env, ch, S2, 1e-12, 112, 120, dst="RESID")


def _affine_ln(env, ch, S2, eps, w_off, b_off, dst):
    p, bk, pb = env["p"], env["bk"], env["pb"]
    XLB = env["XLB"]
    SC1, SC2 = env["SC1"], env["SC2"]
    ROWA, ROWB, ROWC = env["ROWA"], env["ROWB"], env["ROWC"]
    ONEC, ONER = env["ONEC"], env["ONER"]
    COLS = env["COLS"]
    w6 = bk.wdeps(6)
    w5 = bk.wdeps(5)
    hsum = hsq = None
    for c in range(CT):
        st = S2.ap[:, c * CH:(c + 1) * CH]
        hsum = p.mm(pb[6][0:1, :], ONEC[:], st, c == 0, c == CT - 1,
                    deps=(w6 + S2.rd()) if c == 0 else S2.rd(), inc=True)
        S2.r(hsum)
        sq_b = SC1 if c % 2 == 0 else SC2
        hsqa = p.s(lambda e, o=sq_b.ap, i=st: e.activation(o, i, AF.Square),
                   deps=sq_b.wd() + S2.rd() + (w5 if c == 0 else []))
        sq_b.w(hsqa)
        S2.r(hsqa)
        hsq = p.mm(pb[5][0:1, :], ONEC[:], sq_b.ap, c == 0, c == CT - 1,
                   deps=sq_b.rd(), inc=True)
        sq_b.r(hsq)
    bk.set_w(6, hsum)
    bk.set_w(5, hsq)
    SC1.r(hsq)
    SC2.r(hsq)
    S2.r(hsq)
    ha = p.v(lambda e: e.tensor_scalar(ROWA.ap, pb[6][0:1, :], 1.0 / DM, None, ALU.mult),
             deps=bk.rdeps(6) + ROWA.wd())
    ROWA.w(ha)
    bk.add_r(6, ha)
    hb = p.v(lambda e: e.tensor_scalar(ROWB.ap, pb[5][0:1, :], 1.0 / DM, None, ALU.mult),
             deps=bk.rdeps(5) + ROWB.wd())
    ROWB.w(hb)
    bk.add_r(5, hb)
    hc = p.v(lambda e: e.tensor_tensor(ROWC.ap, ROWA.ap, ROWA.ap, ALU.mult),
             deps=ROWA.rd() + ROWC.wd())
    ROWA.r(hc)
    ROWC.w(hc)
    hd = p.v(lambda e: e.tensor_tensor(ROWC.ap, ROWB.ap, ROWC.ap, ALU.subtract),
             deps=ROWB.rd() + ROWC.rd())
    ROWB.r(hd)
    ROWC.w(hd)
    eps_ap = env["EPS5"] if eps == 1e-5 else env["EPS12"]
    he = p.s(lambda e, e_=eps_ap: e.activation(ROWC.ap, ROWC.ap, AF.Sqrt, bias=e_),
             deps=ROWC.rd() + env["ones_h"])
    ROWC.w(he)
    hf = p.v(lambda e: e.reciprocal(ROWC.ap, ROWC.ap), deps=ROWC.rd())
    ROWC.w(hf)
    w7 = bk.wdeps(7)
    hm = p.mm(pb[7][:, :], ONER[:], ROWA.ap, True, True, deps=w7 + ROWA.rd(), inc=True)
    ROWA.r(hm)
    bk.set_w(7, hm)
    w4 = bk.wdeps(4)
    hr = p.mm(pb[4][:, :], ONER[:], ROWC.ap, True, True, deps=w4 + ROWC.rd(), inc=True)
    ROWC.r(hr)
    bk.set_w(4, hr)
    RB = env["RB"]
    for c in range(CT):
        st = S2.ap[:, c * CH:(c + 1) * CH]
        h1 = p.v(lambda e, i=st: e.tensor_tensor(SC1.ap, i, pb[7][:, :], ALU.subtract),
                 deps=bk.rdeps(7) + SC1.wd() + S2.rd())
        SC1.w(h1)
        S2.r(h1)
        bk.add_r(7, h1)
        h2 = p.v(lambda e: e.tensor_tensor(SC1.ap, SC1.ap, pb[4][:, :], ALU.mult),
                 deps=bk.rdeps(4) + SC1.rd())
        SC1.w(h2)
        bk.add_r(4, h2)
        if dst == "XL":
            out = XLB[c].ap[:, ch * CH:(ch + 1) * CH]
            h3 = p.v(lambda e, o=out, wc=COLS.ap[:, w_off + c:w_off + c + 1],
                     bc2=COLS.ap[:, b_off + c:b_off + c + 1]:
                     e.tensor_scalar(o, SC1.ap, wc, bc2, ALU.mult, ALU.add),
                     deps=SC1.rd() + COLS.rd() + XLB[c].wd())
            SC1.r(h3)
            COLS.r(h3)
            XLB[c].w(h3)
        else:
            h3 = p.v(lambda e, wc=COLS.ap[:, w_off + c:w_off + c + 1],
                     bc2=COLS.ap[:, b_off + c:b_off + c + 1]:
                     e.tensor_scalar(SC1.ap, SC1.ap, wc, bc2, ALU.mult, ALU.add),
                     deps=SC1.rd() + COLS.rd())
            COLS.r(h3)
            SC1.w(h3)
            ld = p.load(SC2.ap, env["_RESID_SPILL"][c * 128:(c + 1) * 128,
                                                    ch * CH:(ch + 1) * CH],
                        deps=SC2.wd(), stream="sc2")
            SC2.w(ld)
            rt = RB[c].ap[:, ch * CH:(ch + 1) * CH]
            h4 = p.v(lambda e, o=rt: e.tensor_tensor(o, SC1.ap, SC2.ap, ALU.add),
                     deps=SC1.rd() + SC2.rd() + env["_RFREE"] + RB[c].wd())
            SC1.r(h4)
            SC2.r(h4)
            RB[c].w(h4)


def _ct(a):
    return np.ascontiguousarray(a, dtype=np.float32)


def _tile_w(wt, blk=128):
    C, E = wt.shape
    A = C // 128
    I = E // blk
    t = wt.reshape(A, 128, I, blk).transpose(1, 2, 0, 3).reshape(128, I * A * blk)
    return np.ascontiguousarray(t, dtype=np.float32)


def _tile_w3(w3, blk=128):
    return np.stack([_tile_w(w3[e], blk) for e in range(w3.shape[0])])


def _tile_down(dt):
    t = dt.reshape(2, 11, 128, 8, 128).transpose(2, 0, 3, 1, 4).reshape(128, -1)
    return np.ascontiguousarray(t, dtype=np.float32)


def _colpack(v, k):
    return np.ascontiguousarray(np.asarray(v, np.float32).reshape(k, 128).T)


def _rope_tables_l0():
    inv = 1.0 / (10000.0 ** (np.arange(0, 64, 2, dtype=np.float64) / 64))
    t = np.arange(T, dtype=np.float64)
    fr = t[:, None] * inv[None, :]
    emb = np.concatenate([fr, fr], axis=1)
    cosT = np.cos(emb).T
    sinT = np.sin(emb).T
    cos2 = np.ascontiguousarray(np.vstack([cosT, cosT]).astype(BF))
    sin2 = np.ascontiguousarray(np.vstack([sinT, sinT]).astype(BF))
    Pm = np.zeros((64, 64), np.float32)
    for j in range(32):
        Pm[j, 2 * j + 1] = -1.0
    for j in range(32, 64):
        Pm[j, 2 * (j - 32)] = 1.0
    blk = np.zeros((128, 128), np.float32)
    blk[0:64, 0:64] = Pm
    blk[64:128, 64:128] = Pm
    perm = np.ascontiguousarray(blk.T.astype(BF))
    return cos2, sin2, perm


def _rope_tables_l1():
    inv = 1.0 / (10000.0 ** (np.arange(0, 64, 2, dtype=np.float64) / 64))
    t = np.arange(T, dtype=np.float64)
    fr = t[:, None] * inv[None, :]
    cosT = np.repeat(np.cos(fr).T, 2, axis=0)
    sinT = np.repeat(np.sin(fr).T, 2, axis=0)
    cos2 = np.ascontiguousarray(np.vstack([cosT, cosT]).astype(BF))
    sin2 = np.ascontiguousarray(np.vstack([sinT, sinT]).astype(BF))
    Pm = np.zeros((64, 64), np.float32)
    for i in range(32):
        Pm[2 * i, 2 * i + 1] = -1.0
        Pm[2 * i + 1, 2 * i] = 1.0
    blk = np.zeros((128, 128), np.float32)
    blk[0:64, 0:64] = Pm
    blk[64:128, 64:128] = Pm
    perm = np.ascontiguousarray(blk.T.astype(BF))
    return cos2, sin2, perm


def _prep_arrays(inp):
    a = {}
    e16 = np.zeros((16, 2048), np.float32)
    for n in range(16):
        e16[n, n * 128:(n + 1) * 128] = 1.0
    a["c_e16"] = e16.astype(BF)
    a["c_id128"] = np.eye(128, dtype=np.float32)

    c0, s0, p0 = _rope_tables_l0()
    a["a0a_wq"] = _tile_w(np.asarray(inp["L0_attn_a_Wq"]).T)
    a["a0a_wk"] = _tile_w(np.asarray(inp["L0_attn_a_Wk"]).T)
    a["a0a_wv"] = _tile_w(np.asarray(inp["L0_attn_a_Wv"]).T)
    a["a0a_wo"] = _tile_w(np.asarray(inp["L0_attn_a_Wo"]).T)
    a["a0a_cos"], a["a0a_sin"], a["a0a_perm"] = c0, s0, p0
    wqkv = np.asarray(inp["L0_attn_b_w_qkv"])
    a["a0b_wq"] = _tile_w(np.ascontiguousarray(wqkv[0:1024].T))
    a["a0b_wk"] = _tile_w(np.ascontiguousarray(wqkv[1024:2048].T))
    a["a0b_wv"] = _tile_w(np.ascontiguousarray(wqkv[2048:3072].T))
    a["a0b_wo"] = _tile_w(np.asarray(inp["L0_attn_b_w_out"]).T)
    a["a0b_bo"] = _colpack(inp["L0_attn_b_b_out"], 8)
    a["moe_sg"] = _tile_w(np.asarray(inp["L0_ffn_sg"]).T)
    a["moe_su"] = _tile_w(np.asarray(inp["L0_ffn_su"]).T)
    a["moe_sd"] = _tile_down(np.ascontiguousarray(np.asarray(inp["L0_ffn_sd"]).T))
    a["moe_rg"] = _tile_w3(np.asarray(inp["L0_ffn_rg"]).transpose(0, 2, 1))
    a["moe_ru"] = _tile_w3(np.asarray(inp["L0_ffn_ru"]).transpose(0, 2, 1))
    a["moe_rd"] = np.stack([_tile_down(np.ascontiguousarray(np.asarray(inp["L0_ffn_rd"])[e].T)) for e in range(7)])
    a["moe_wr"] = _tile_w(np.asarray(inp["L0_ffn_wr"]).T, blk=7)
    a["moe_rb"] = _ct(np.asarray(inp["L0_ffn_rb"]).reshape(7, 1))
    c1, s1, p1 = _rope_tables_l1()
    a["a1a_wq"] = _tile_w(np.asarray(inp["L1_attn_a_Wq"]).T)
    a["a1a_wk"] = _tile_w(np.asarray(inp["L1_attn_a_Wk"]).T)
    a["a1a_wv"] = _tile_w(np.asarray(inp["L1_attn_a_Wv"]).T)
    a["a1a_wo"] = _tile_w(np.asarray(inp["L1_attn_a_Wo"]).T)
    a["a1a_cos"], a["a1a_sin"], a["a1a_perm"] = c1, s1, p1
    a["lff_wr"] = _tile_w(np.asarray(inp["L1_ffn_W_r"]).T)
    a["lff_wk"] = _tile_w(np.asarray(inp["L1_ffn_W_k"]).T)
    a["lff_wv"] = _tile_w(np.asarray(inp["L1_ffn_W_v"]).T)
    cols = np.zeros((128, 16), np.float32)
    cols[:, 0:8] = _colpack(inp["L1_ffn_mu_r"], 8)
    cols[:, 8:16] = _colpack(inp["L1_ffn_mu_k"], 8)
    a["lff_cols"] = cols
    a["mb_ipw"] = _tile_w(np.asarray(inp["L1_attn_b_in_proj_w"]).T)
    a["mb_xpw"] = _tile_w(np.asarray(inp["L1_attn_b_x_proj_w"]).T, blk=96)
    a["mb_dpw"] = _ct(np.asarray(inp["L1_attn_b_dt_proj_w"]).T)
    a["mb_opw"] = _tile_w(np.asarray(inp["L1_attn_b_out_proj_w"]).T)
    a["mb_f1w"] = _tile_w(np.asarray(inp["L1_attn_b_ffn_w1"]).T)
    a["mb_f2w"] = _tile_w(np.asarray(inp["L1_attn_b_ffn_w2"]).T)
    alog = np.asarray(inp["L1_attn_b_A_log"], np.float32)
    a["mb_alog"] = _ct(alog.reshape(16, 128, 16).transpose(1, 0, 2).reshape(128, 256))
    mc = np.zeros((128, 192), np.float32)
    cw = np.asarray(inp["L1_attn_b_conv_w"], np.float32)
    mc[:, 0:64] = cw.reshape(16, 128, 4).transpose(1, 0, 2).reshape(128, 64)
    mc[:, 64:80] = _colpack(inp["L1_attn_b_D"], 16)
    mc[:, 80:96] = _colpack(inp["L1_attn_b_dt_proj_b"], 16)
    mc[:, 96:104] = _colpack(inp["L1_attn_b_ln1_w"], 8)
    mc[:, 104:112] = _colpack(inp["L1_attn_b_ln1_b"], 8)
    mc[:, 112:120] = _colpack(inp["L1_attn_b_ln2_w"], 8)
    mc[:, 120:128] = _colpack(inp["L1_attn_b_ln2_b"], 8)
    mc[:, 128:144] = _colpack(inp["L1_attn_b_conv_b"], 16)
    mc[:, 144:160] = _colpack(inp["L1_attn_b_ffn_b1"], 16)
    mc[:, 160:168] = _colpack(inp["L1_attn_b_ffn_b2"], 8)
    a["mb_cols"] = mc
    return a


PH_A0A = dict(kind="attn", tag="a0a", rope=True, scale=0.125)
PH_MOE = dict(kind="moe", tag="moe")
PH_A0B = dict(kind="attn", tag="a0b", rope=False, scale=1.0 / 32.0, bias=True)
PH_A1A = dict(kind="attn", tag="a1a", rope=True, scale=0.125)
PH_LFF = dict(kind="l1ffn", tag="lff")
PH_MB = dict(kind="mamba", tag="mb")
ALL_PHASES = [PH_A0A, PH_MOE, PH_A0B, PH_MOE, PH_A1A, PH_LFF, PH_MB, PH_LFF]

FUSED = True


def kernel(**inputs):
    arrays = _prep_arrays(inputs)
    x = np.asarray(inputs["L0_attn_a_x"], np.float32)
    ncore = 8
    x_cur = [np.ascontiguousarray(x[i].T) for i in range(ncore)]

    if FUSED:
        groups = [ALL_PHASES]
    else:
        groups = [[ph] for ph in ALL_PHASES]

    prog_cache = {}
    for grp in groups:
        key = tuple(id(ph) for ph in grp)
        if key not in prog_cache:
            prog_cache[key] = build_program(grp, arrays)
        nc, used = prog_cache[key]
        in_maps = []
        for i in range(ncore):
            m = {n: arrays[n] for n in used if n != "x_in"}
            m["x_in"] = x_cur[i]
            in_maps.append(m)
        res = run_bass_kernel_spmd(nc, in_maps, core_ids=list(range(ncore)))
        x_cur = [np.asarray(res.results[i]["x_out"]) for i in range(ncore)]

    out = np.stack([xc.T for xc in x_cur]).astype(np.float32)
    return np.ascontiguousarray(out)


# version 19
# speedup vs baseline: 1.0706x; 1.0420x over previous
import numpy as np
from contextlib import ExitStack
import ml_dtypes
import concourse.bass as bass
import concourse.mybir as mybir
from concourse.bass_utils import run_bass_kernel_spmd

F32 = mybir.dt.float32
BF16 = mybir.dt.bfloat16
AF = mybir.ActivationFunctionType
ALU = mybir.AluOpType
AX = mybir.AxisListType
BF = ml_dtypes.bfloat16

T = 2048
CH = 512
NCH = T // CH
CT = 8
DM = 1024

ENGINES = ("sync", "tensor", "vector", "scalar", "gpsimd")


class P:
    def __init__(self, nc, ctx):
        self.nc = nc
        self.ctx = ctx
        self.q = {e: [] for e in ENGINES}
        self.sems = {}
        self.cnt = {}
        self.lastw = {e: {} for e in ENGINES}
        for s in ("pe", "ve", "se", "ge"):
            self.sems[s] = ctx.enter_context(nc.semaphore(s))
            self.cnt[s] = 0
        self.pending_barrier = {e: None for e in ENGINES}

    def dsem(self, name):
        if name not in self.sems:
            self.sems[name] = self.ctx.enter_context(self.nc.semaphore(name))
            self.cnt[name] = 0
        return name

    def barrier(self):
        snap = [(s, v) for s, v in self.cnt.items() if v > 0]
        for e in ENGINES:
            self.pending_barrier[e] = snap

    def on(self, eng, fn, deps=(), inc=None, n=1):
        if self.pending_barrier[eng]:
            deps = list(self.pending_barrier[eng]) + list(deps)
            self.pending_barrier[eng] = None
        w = []
        for d in deps:
            if d is None:
                continue
            s, v = d
            if self.lastw[eng].get(s, 0) >= v:
                continue
            self.lastw[eng][s] = v
            w.append((s, v))
        h = None
        if inc is not None:
            self.cnt[inc] += n
            h = (inc, self.cnt[inc])
        sems = self.sems

        def run(eo, w=tuple(w), fn=fn, inc=inc, n=n):
            for (s, v) in w:
                eo.wait_ge(sems[s], v)
            r = fn(eo)
            if inc is not None:
                r.then_inc(sems[inc], n)

        self.q[eng].append(run)
        return h

    def load(self, dst, src, deps=(), stream="ld"):
        return self.on("sync", lambda e, d=dst, s=src: e.dma_start(out=d, in_=s),
                       deps=deps, inc=self.dsem("L_" + stream), n=16)

    def store(self, dst, src, deps=(), stream="st"):
        return self.on("gpsimd", lambda e, d=dst, s=src: e.dma_start(out=d, in_=s),
                       deps=deps, inc=self.dsem("S_" + stream), n=16)

    def mm(self, out, lhsT, rhs, start, stop, deps=(), inc=False):
        return self.on(
            "tensor",
            lambda e, o=out, l=lhsT, r=rhs, st=start, sp=stop:
                e.matmul(o, l, r, start=st, stop=sp),
            deps=deps, inc="pe" if inc else None)

    def mmgrp(self, out, steps, deps=()):
        n = len(steps)
        h = None
        for i, (l, r) in enumerate(steps):
            h = self.mm(out, l, r, i == 0, i == n - 1,
                        deps=deps if i == 0 else (), inc=(i == n - 1))
        return h

    def v(self, fn, deps=()):
        return self.on("vector", fn, deps=deps, inc="ve")

    def s(self, fn, deps=()):
        return self.on("scalar", fn, deps=deps, inc="se")


class Buf:
    def __init__(self, ap):
        self.ap = ap
        self.writer = None
        self.users = []

    def wd(self):
        d = list(self.users)
        if self.writer is not None:
            d.append(self.writer)
        self.users = []
        return d

    def rd(self):
        return [self.writer] if self.writer is not None else []

    def w(self, h):
        self.writer = h
        return h

    def r(self, h):
        if h is not None:
            self.users.append(h)
        return h


class Banks:
    def __init__(self, tensors):
        self.t = tensors
        self.st = [{"w": None, "r": []} for _ in tensors]

    def wdeps(self, i):
        d = list(self.st[i]["r"])
        if self.st[i]["w"] is not None:
            d.append(self.st[i]["w"])
        self.st[i]["r"] = []
        return d

    def set_w(self, i, h):
        self.st[i]["w"] = h

    def rdeps(self, i):
        return [self.st[i]["w"]] if self.st[i]["w"] is not None else []

    def add_r(self, i, h):
        if h is not None:
            self.st[i]["r"].append(h)


class Arena:
    def __init__(self, tens, cols):
        self.t = tens
        self.cols = cols
        self.off = 0

    def reset(self):
        self.off = 0

    def bf16(self, n):
        o = self.off
        assert o + n <= self.cols, f"arena overflow {o}+{n}>{self.cols}"
        self.off += n
        return Buf(self.t[:, o:o + n])

    def f32(self, n):
        if self.off % 2:
            self.off += 1
        o = self.off
        assert o + 2 * n <= self.cols, "arena overflow f32"
        self.off += 2 * n
        return Buf(self.t[:, o:o + 2 * n].bitcast(F32))

    def bf16_at(self, o, n):
        assert o + n <= self.cols
        return Buf(self.t[:, o:o + n])

    def f32_at(self, o, n):
        assert o % 2 == 0 and o + 2 * n <= self.cols
        return Buf(self.t[:, o:o + 2 * n].bitcast(F32))


def fetch_w(env, dst_buf, src_ap, dst_off=0, cols=None):
    p = env["p"]
    i = env["WST_i"][0]
    env["WST_i"][0] = 1 - i
    wst = env["WSTb"][i]
    if cols is None:
        n = 1
        for d in src_ap.shape:
            n *= d
        cols = n // 128
    assert cols <= 1024, cols
    stg = wst.ap[:, 0:cols]
    if len(src_ap.shape) == 3:
        stg = stg.rearrange("p (a n) -> p a n", a=src_ap.shape[1])
    ld = p.load(stg, src_ap, deps=wst.wd(), stream=f"wst{i}")
    wst.w(ld)
    dst = dst_buf.ap[:, dst_off:dst_off + cols]
    cv = p.on("gpsimd", lambda e, d=dst, s=wst.ap[:, 0:cols]:
              e.tensor_copy(d, s), deps=wst.rd() + dst_buf.wd(), inc="ge")
    wst.r(cv)
    dst_buf.w(cv)
    return cv


class Pool:
    def __init__(self, bufs):
        self.bufs = bufs
        self.i = 0

    def next(self):
        b = self.bufs[self.i]
        self.i = (self.i + 1) % len(self.bufs)
        return b


def cross(tos, froms):
    pend = []
    for f in froms:
        pend += f.wd()
    for t in tos:
        t.users += pend


def build_program(phases, arrays):
    nc = bass.Bass("TRN2", target_bir_lowering=False)
    ctx = ExitStack()
    used = []
    dcache = {}

    def din(name):
        if name in dcache:
            return dcache[name]
        a = arrays[name]
        dt = BF16 if a.dtype == BF else F32
        used.append(name)
        ap = nc.dram_tensor(name, list(a.shape), dt, kind="ExternalInput").ap()
        dcache[name] = ap
        return ap

    x_in = nc.dram_tensor("x_in", [DM, T], F32, kind="ExternalInput").ap()
    used.append("x_in")
    x_out = nc.dram_tensor("x_out", [DM, T], F32, kind="ExternalOutput").ap()

    dsc = {}

    def dram_scratch(name, shape, dt):
        if name not in dsc:
            dsc[name] = nc.dram_tensor(name, list(shape), dt, kind="Internal").ap()
        return dsc[name]

    with ctx:
        p = P(nc, ctx)
        E = ctx.enter_context
        R_t = E(nc.sbuf_tensor("Rt", [128, DM * T // 128], F32))
        XL_t = E(nc.sbuf_tensor("XLt", [128, DM * T // 128], BF16))
        AR_t = E(nc.sbuf_tensor("ARt", [128, 26752], BF16))
        WST_t = E(nc.sbuf_tensor("WSTt", [128, 2048], F32))
        WB_t = E(nc.sbuf_tensor("WBt", [128, 4096], BF16))
        SC1_t = E(nc.sbuf_tensor("SC1t", [128, 512], F32))
        SC2_t = E(nc.sbuf_tensor("SC2t", [128, 512], F32))
        SCB_t = E(nc.sbuf_tensor("SCBt", [128, 2048], BF16))
        SLOT_t = E(nc.sbuf_tensor("SLOTt", [128, 3072], BF16))
        ROWA_t = E(nc.sbuf_tensor("ROWAt", [1, 512], F32))
        ROWB_t = E(nc.sbuf_tensor("ROWBt", [1, 512], F32))
        ROWC_t = E(nc.sbuf_tensor("ROWCt", [1, 512], F32))
        ROWD_t = E(nc.sbuf_tensor("ROWDt", [1, 512], F32))
        ROWE_t = E(nc.sbuf_tensor("ROWEt", [1, 512], F32))
        ROWF_t = E(nc.sbuf_tensor("ROWFt", [1, 512], F32))
        ONECB_t = E(nc.sbuf_tensor("ONECBt", [128, 1], BF16))
        COLS_t = E(nc.sbuf_tensor("COLSt", [128, 192], F32))
        E16_t = E(nc.sbuf_tensor("E16t", [16, 2048], BF16))
        IDT_t = E(nc.sbuf_tensor("IDTt", [128, 128], F32))
        HC_t = E(nc.sbuf_tensor("HCt", [128, 256], F32))
        AS_t = E(nc.sbuf_tensor("ASt", [128, 256], F32))
        ONEC_t = E(nc.sbuf_tensor("ONECt", [128, 1], F32))
        ONER_t = E(nc.sbuf_tensor("ONERt", [1, 128], F32))
        EPS_t = E(nc.sbuf_tensor("EPSt", [1, 2], F32))
        N10_t = E(nc.sbuf_tensor("N10t", [128, 1], F32))
        CM1_t = E(nc.sbuf_tensor("CM1t", [128, 1], F32))
        pb = [E(nc.psum_tensor(f"pb{i}", [128, 512], F32)) for i in range(8)]

        env = dict(nc=nc, p=p, bk=Banks(pb), pb=pb, arena=Arena(AR_t, 26752),
                   R_t=R_t,
                   RB=[Buf(R_t[:, c * T:(c + 1) * T]) for c in range(CT)],
                   XLB=[Buf(XL_t[:, c * T:(c + 1) * T]) for c in range(CT)],
                   WSTb=[Buf(WST_t[:, 0:1024]), Buf(WST_t[:, 1024:2048])],
                   WST_i=[0], WB=WB_t,
                   SC1=Buf(SC1_t[:]), SC2=Buf(SC2_t[:]), SCB=Buf(SCB_t[:]),
                   SLOT=[Buf(SLOT_t[:, i * 512:(i + 1) * 512]) for i in range(6)],
                   ROWA=Buf(ROWA_t[:]), ROWB=Buf(ROWB_t[:]), ROWC=Buf(ROWC_t[:]),
                   ROWD=Buf(ROWD_t[:]), ROWE=Buf(ROWE_t[:]), ROWF=Buf(ROWF_t[:]),
                   ONECB=ONECB_t,
                   COLS=Buf(COLS_t[:]), E16=Buf(E16_t[:]), IDT=Buf(IDT_t[:]),
                   HC=Buf(HC_t[:]), AS=Buf(AS_t[:]),
                   ONEC=ONEC_t, ONER=ONER_t,
                   din=din, dram_scratch=dram_scratch)
        bk, arena = env["bk"], env["arena"]

        h1 = p.v(lambda e: e.memset(ONEC_t[:], 1.0))
        h2 = p.v(lambda e: e.memset(ONER_t[:], 1.0))
        h3 = p.v(lambda e: e.memset(EPS_t[0:1, 0:1], 1e-5))
        h4 = p.v(lambda e: e.memset(EPS_t[0:1, 1:2], 1e-12))
        h5 = p.v(lambda e: e.memset(N10_t[:], -10.0))
        h6 = p.v(lambda e: e.memset(CM1_t[:], -1.0))
        h7 = p.v(lambda e: e.memset(ONECB_t[:], 1.0))
        nc.const_aps.register(F32, -1.0, CM1_t[:])
        env["ones_h"] = [h1, h2, h3, h4, h5, h6, h7]
        env["N10"] = N10_t[:]
        env["EPS5"] = EPS_t[0:1, 0:1]
        env["EPS12"] = EPS_t[0:1, 1:2]
        env["E16"].w(p.load(E16_t[:], din("c_e16"), stream="e16"))
        env["IDT"].w(p.load(IDT_t[:], din("c_id128"), stream="idt"))

        rl = []
        for c in range(CT):
            rl.append(p.load(env["RB"][c].ap, x_in[c * 128:(c + 1) * 128, :],
                             stream="rb"))
        for c in range(CT):
            env["RB"][c].w(rl[-1])

        for ph in phases:
            p.barrier()
            arena.reset()
            {"attn": emit_attn, "moe": emit_moe,
             "l1ffn": emit_l1ffn, "mamba": emit_mamba}[ph["kind"]](env, ph)

        p.barrier()
        sh = []
        for c in range(CT):
            sh.append(p.store(x_out[c * 128:(c + 1) * 128, :], env["RB"][c].ap,
                              deps=env["RB"][c].rd(), stream="xout"))
        p.on("gpsimd", lambda e: e.nop(), deps=[sh[-1]])

        with nc.Block() as block:
            @block.sync
            def _(eo):
                for f in p.q["sync"]:
                    f(eo)

            @block.tensor
            def _(eo):
                for f in p.q["tensor"]:
                    f(eo)

            @block.vector
            def _(eo):
                for f in p.q["vector"]:
                    f(eo)

            @block.scalar
            def _(eo):
                for f in p.q["scalar"]:
                    f(eo)

            @block.gpsimd
            def _(eo):
                for f in p.q["gpsimd"]:
                    f(eo)

    return nc, used


def emit_ln(env):
    p, bk, pb = env["p"], env["bk"], env["pb"]
    RB, XLB = env["RB"], env["XLB"]
    SC1, SC2 = env["SC1"], env["SC2"]
    SLOT = env["SLOT"]
    ROWS = [(env["ROWA"], env["ROWB"], env["ROWC"]),
            (env["ROWD"], env["ROWE"], env["ROWF"])]
    ONEC, ONER, ONECB = env["ONEC"], env["ONER"], env["ONECB"]
    base = list(env["ones_h"])
    for ch in range(NCH):
        par = ch % 2
        b_sum, b_sq, b_m, b_r = (6, 5, 7, 4) if par == 0 else (1, 0, 2, 3)
        rA, rB, rC = ROWS[par]
        w6 = bk.wdeps(b_sum)
        w5 = bk.wdeps(b_sq)
        hsum = hsq = None
        for c in range(CT):
            rt = RB[c].ap[:, ch * CH:(ch + 1) * CH]
            hsum = p.mm(pb[b_sum][0:1, :], ONEC[:], rt, c == 0, c == CT - 1,
                        deps=(base + w6 + RB[c].rd()) if c == 0 else RB[c].rd(),
                        inc=True)
            RB[c].r(hsum)
            sq_b = SLOT[(ch * 2 + c % 2) % 6]
            hsqa = p.s(lambda e, o=sq_b.ap, i=rt: e.activation(o, i, AF.Square),
                       deps=sq_b.wd() + RB[c].rd() + (w5 if c == 0 else []))
            sq_b.w(hsqa)
            RB[c].r(hsqa)
            hsq = p.mm(pb[b_sq][0:1, :], ONECB[:], sq_b.ap, c == 0, c == CT - 1,
                       deps=sq_b.rd(), inc=True)
            sq_b.r(hsq)
        bk.set_w(b_sum, hsum)
        bk.set_w(b_sq, hsq)
        for c in range(CT):
            RB[c].r(hsq)
        ha = p.v(lambda e, o=rA.ap, b_=b_sum:
                 e.tensor_scalar(o, pb[b_][0:1, :], 1.0 / DM, None, ALU.mult),
                 deps=bk.rdeps(b_sum) + rA.wd())
        rA.w(ha)
        bk.add_r(b_sum, ha)
        hb = p.v(lambda e, o=rB.ap, b_=b_sq:
                 e.tensor_scalar(o, pb[b_][0:1, :], 1.0 / DM, None, ALU.mult),
                 deps=bk.rdeps(b_sq) + rB.wd())
        rB.w(hb)
        bk.add_r(b_sq, hb)
        hc = p.v(lambda e, o=rC.ap, a=rA.ap: e.tensor_tensor(o, a, a, ALU.mult),
                 deps=rA.rd() + rC.wd())
        rA.r(hc)
        rC.w(hc)
        hd = p.v(lambda e, o=rC.ap, b2=rB.ap: e.tensor_tensor(o, b2, o, ALU.subtract),
                 deps=rB.rd() + rC.rd())
        rB.r(hd)
        rC.w(hd)
        he = p.s(lambda e, o=rC.ap: e.activation(o, o, AF.Sqrt, bias=env["EPS5"]),
                 deps=rC.rd() + env["ones_h"])
        rC.w(he)
        hf = p.v(lambda e, o=rC.ap: e.reciprocal(o, o), deps=rC.rd())
        rC.w(hf)
        w7 = bk.wdeps(b_m)
        hm = p.mm(pb[b_m][:, :], ONER[:], rA.ap, True, True,
                  deps=w7 + rA.rd(), inc=True)
        rA.r(hm)
        bk.set_w(b_m, hm)
        w4 = bk.wdeps(b_r)
        hr = p.mm(pb[b_r][:, :], ONER[:], rC.ap, True, True,
                  deps=w4 + rC.rd(), inc=True)
        rC.r(hr)
        bk.set_w(b_r, hr)
        for c in range(CT):
            rt = RB[c].ap[:, ch * CH:(ch + 1) * CH]
            xt = XLB[c].ap[:, ch * CH:(ch + 1) * CH]
            sb = SC1 if c % 2 == 0 else SC2
            h1_ = p.v(lambda e, o=sb.ap, i=rt, b_=b_m:
                      e.tensor_tensor(o, i, pb[b_][:, :], ALU.subtract),
                      deps=bk.rdeps(b_m) + sb.wd() + RB[c].rd())
            sb.w(h1_)
            RB[c].r(h1_)
            bk.add_r(b_m, h1_)
            h2_ = p.v(lambda e, o=xt, i=sb.ap, b_=b_r:
                      e.tensor_tensor(o, i, pb[b_][:, :], ALU.mult),
                      deps=bk.rdeps(b_r) + sb.rd() + XLB[c].wd())
            sb.r(h2_)
            bk.add_r(b_r, h2_)
            XLB[c].w(h2_)


def _xl_rd(env):
    return [h for c in range(CT) for h in env["XLB"][c].rd()]


def emit_attn(env, ph):
    p, bk, pb, arena = env["p"], env["bk"], env["pb"], env["arena"]
    RB, XLB = env["RB"], env["XLB"]
    SC1, SC2, SCB = env["SC1"], env["SC2"], env["SCB"]
    ROWA, ROWB, ONER = env["ROWA"], env["ROWB"], env["ONER"]
    din, dram_scratch = env["din"], env["dram_scratch"]
    WB_t = env["WB"]
    COLS = env["COLS"]

    emit_ln(env)

    tag = ph["tag"]
    wq = din(tag + "_wq"); wk = din(tag + "_wk")
    wv = din(tag + "_wv"); wo = din(tag + "_wo")
    rope = ph.get("rope", False)
    scale = ph["scale"]
    has_bias = ph.get("bias", False)
    if has_bias:
        COLS.w(p.load(COLS.ap[:, 0:CT], din(tag + "_bo"), deps=COLS.wd(), stream="cols"))

    QT = arena.bf16(T)
    KT = arena.bf16(T)
    VB = arena.bf16(16 * 132)
    PRB2 = [arena.bf16(16 * CH), arena.bf16(16 * CH)]
    AOS = Buf(arena.t[:, 0:4096])
    WP = Pool([Buf(WB_t[:, i * 1024:(i + 1) * 1024]) for i in range(4)])
    if rope:
        COST = arena.bf16(T)
        SINT = arena.bf16(T)
        PERM = Buf(env["COLS"].ap[:, 64:128].bitcast(BF16))
        COST.w(p.load(COST.ap, din(tag + "_cos"), stream="cost"))
        SINT.w(p.load(SINT.ap, din(tag + "_sin"), stream="sint"))
        PERM.w(p.load(PERM.ap, din(tag + "_perm"), deps=PERM.wd(), stream="perm"))

    ao = dram_scratch(tag + "_ao", [DM, T], BF16)
    vview = VB.ap.rearrange("p (t h c) -> p t h c", t=16, h=2)
    VB.w(p.v(lambda e: e.memset(vview[:, :, :, 64:65], 1.0), deps=VB.wd()))

    ao_done = []
    for hg in range(CT):
        for (W, DST) in ((wq, QT), (wk, KT)):
            WQB = WP.next()
            fetch_w(env, WQB, W[:, hg * 1024:(hg + 1) * 1024])
            for ch in range(NCH):
                b = ch % 2
                wd = bk.wdeps(b)
                steps = [(WQB.ap[:, c * 128:(c + 1) * 128],
                          XLB[c].ap[:, ch * CH:(ch + 1) * CH]) for c in range(CT)]
                hmm = p.mmgrp(pb[b][:, :], steps,
                              deps=wd + WQB.rd() + _xl_rd(env) + DST.wd())
                WQB.r(hmm)
                for c in range(CT):
                    XLB[c].r(hmm)
                bk.set_w(b, hmm)
                hev = p.v(lambda e, o=DST.ap[:, ch * CH:(ch + 1) * CH], b_=b:
                          e.tensor_copy(o, pb[b_][:, :]), deps=bk.rdeps(b))
                bk.add_r(b, hev)
                DST.w(hev)
        if rope:
            SLOT = env["SLOT"]
            ru_ = 0
            for DST in (QT, KT):
                for ch in range(NCH):
                    b = 2 + ch % 2
                    wd = bk.wdeps(b)
                    sl = DST.ap[:, ch * CH:(ch + 1) * CH]
                    s1 = SLOT[(2 * ru_) % 6]
                    s2 = SLOT[(2 * ru_ + 1) % 6]
                    ru_ += 1
                    hrot = p.mm(pb[b][:, :], PERM.ap, sl, True, True,
                                deps=wd + PERM.rd() + DST.rd(), inc=True)
                    PERM.r(hrot)
                    DST.r(hrot)
                    bk.set_w(b, hrot)
                    ha = p.v(lambda e, i=sl, c=COST.ap[:, ch * CH:(ch + 1) * CH],
                             o=s1.ap: e.tensor_tensor(o, i, c, ALU.mult),
                             deps=COST.rd() + DST.rd() + s1.wd())
                    COST.r(ha)
                    DST.r(ha)
                    s1.w(ha)
                    hb = p.v(lambda e, s=SINT.ap[:, ch * CH:(ch + 1) * CH], b_=b,
                             o=s2.ap: e.tensor_tensor(o, pb[b_][:, :], s, ALU.mult),
                             deps=bk.rdeps(b) + SINT.rd() + s2.wd())
                    SINT.r(hb)
                    bk.add_r(b, hb)
                    s2.w(hb)
                    hc = p.v(lambda e, o=sl, a_=s1.ap, b2=s2.ap:
                             e.tensor_tensor(o, a_, b2, ALU.add),
                             deps=s1.rd() + s2.rd() + DST.wd())
                    s1.r(hc)
                    s2.r(hc)
                    DST.w(hc)
        WVB = WP.next()
        fetch_w(env, WVB, wv[:, hg * 1024:(hg + 1) * 1024])
        for tb in range(16):
            b = 4 + tb % 2
            wd = bk.wdeps(b)
            steps = [(XLB[c].ap[:, tb * 128:(tb + 1) * 128],
                      WVB.ap[:, c * 128:(c + 1) * 128]) for c in range(CT)]
            hv = p.mmgrp(pb[b][:, 0:128], steps,
                         deps=wd + WVB.rd() + _xl_rd(env) + VB.wd())
            WVB.r(hv)
            for c in range(CT):
                XLB[c].r(hv)
            bk.set_w(b, hv)
            vdst = VB.ap[:, tb * 132: tb * 132 + 132].rearrange(
                "p (h c) -> p h c", h=2)[:, :, 0:64]
            vsrc = pb[b][:, 0:128].rearrange("p (h c) -> p h c", h=2)
            he = p.v(lambda e, o=vdst, i=vsrc: e.tensor_copy(o, i),
                     deps=bk.rdeps(b) + VB.wd())
            bk.add_r(b, he)
            VB.w(he)
        for qc in range(NCH):
            prb_h = [[], []]
            for kt in range(16):
                for a in range(2):
                    hr = slice(64 * a, 64 * a + 64)
                    PRB = PRB2[a]
                    b = 2 * a + kt % 2
                    wd = bk.wdeps(b)
                    hs = p.mm(pb[b][:, :], KT.ap[hr, kt * 128:(kt + 1) * 128],
                              QT.ap[hr, qc * CH:(qc + 1) * CH], True, True,
                              deps=wd + KT.rd() + QT.rd(), inc=True)
                    KT.r(hs)
                    QT.r(hs)
                    bk.set_w(b, hs)
                    he = p.s(lambda e, o=PRB.ap[:, kt * CH:(kt + 1) * CH], b_=b:
                             e.activation(o, pb[b_][:, :], AF.Exp, bias=env["N10"],
                                          scale=scale),
                             deps=bk.rdeps(b) + env["ones_h"] +
                             (PRB.wd() if kt == 0 else []))
                    bk.add_r(b, he)
                    PRB.w(he)
                    prb_h[a].append(he)
            for a in range(2):
                PRB = PRB2[a]
                bav = 4 + a
                wd = bk.wdeps(bav)
                steps = [(VB.ap[:, kt * 132 + a * 66: kt * 132 + a * 66 + 65],
                          PRB.ap[:, kt * CH:(kt + 1) * CH]) for kt in range(16)]
                hav = p.mmgrp(pb[bav][0:65, :], steps, deps=wd + prb_h[a] + VB.rd())
                VB.r(hav)
                PRB.r(hav)
                bk.set_w(bav, hav)
                rw = ROWA if a == 0 else ROWB
                hl1 = p.s(lambda e, b_=bav, o=rw.ap: e.activation(o, pb[b_][64:65, :], AF.Ln),
                          deps=bk.rdeps(bav) + rw.wd())
                rw.w(hl1)
                bk.add_r(bav, hl1)
                hrc = p.s(lambda e, o=rw.ap: e.activation(o, o, AF.Exp, scale=-1.0),
                          deps=rw.rd() + rw.wd())
                rw.w(hrc)
                bbc = 6 + a
                wd3 = bk.wdeps(bbc)
                hbc = p.mm(pb[bbc][0:64, :], ONER[0:1, 0:64], rw.ap, True, True,
                           deps=wd3 + rw.rd(), inc=True)
                rw.r(hbc)
                bk.set_w(bbc, hbc)
                sx = SC1 if a == 0 else SC2
                hcp = p.v(lambda e, b_=bbc, o=sx.ap: e.tensor_copy(o[0:64, :], pb[b_][0:64, :]),
                          deps=bk.rdeps(bbc) + sx.wd())
                sx.w(hcp)
                bk.add_r(bbc, hcp)
                hnm = p.v(lambda e, b_=bav, i=sx.ap, o=SCB.ap[0:64, a * CH:a * CH + CH]:
                          e.tensor_tensor(o, pb[b_][0:64, :], i[0:64, :], ALU.mult),
                          deps=bk.rdeps(bav) + sx.rd() + SCB.wd())
                sx.r(hnm)
                SCB.w(hnm)
                bk.add_r(bav, hnm)
                hst = p.store(ao[hg * 128 + a * 64: hg * 128 + a * 64 + 64,
                                 qc * CH:(qc + 1) * CH], SCB.ap[0:64, a * CH:a * CH + CH],
                              deps=SCB.rd(), stream="ao")
                SCB.r(hst)
                ao_done.append(hst)

    WOF = Buf(PRB2[0].ap)
    WOF.users += PRB2[0].wd()
    AOS.users += QT.wd() + KT.wd()
    for c in range(CT):
        fetch_w(env, WOF, wo[:, c * 1024:(c + 1) * 1024], dst_off=c * 1024)
    for ch in range(NCH):
        lds = []
        for i in range(CT):
            h = p.load(AOS.ap[:, i * CH:(i + 1) * CH],
                       ao[i * 128:(i + 1) * 128, ch * CH:(ch + 1) * CH],
                       deps=ao_done + AOS.wd(), stream="aos")
            AOS.w(h)
            lds.append(h)
        for c in range(CT):
            WOB = WOF
            b = 6 + c % 2
            wd = bk.wdeps(b)
            steps = [(WOB.ap[:, c * 1024 + i * 128:c * 1024 + (i + 1) * 128],
                      AOS.ap[:, i * CH:(i + 1) * CH]) for i in range(CT)]
            hmm = p.mmgrp(pb[b][:, :], steps, deps=wd + WOB.rd() + lds)
            WOB.r(hmm)
            AOS.r(hmm)
            bk.set_w(b, hmm)
            rt = RB[c].ap[:, ch * CH:(ch + 1) * CH]
            if has_bias:
                hev = p.v(lambda e, o=rt, b_=b, cc=c:
                          e.scalar_tensor_tensor(o, pb[b_][:, :], COLS.ap[:, cc:cc + 1], o,
                                                 ALU.add, ALU.add),
                          deps=bk.rdeps(b) + COLS.rd() + RB[c].rd() + RB[c].wd())
                COLS.r(hev)
            else:
                hev = p.v(lambda e, o=rt, b_=b: e.tensor_tensor(o, o, pb[b_][:, :], ALU.add),
                          deps=bk.rdeps(b) + RB[c].rd() + RB[c].wd())
            bk.add_r(b, hev)
            RB[c].w(hev)


def emit_moe(env, ph):
    p, bk, pb, arena = env["p"], env["bk"], env["pb"], env["arena"]
    RB, XLB = env["RB"], env["XLB"]
    SC1, SC2 = env["SC1"], env["SC2"]
    E16, IDT = env["E16"], env["IDT"]
    din, dram_scratch = env["din"], env["dram_scratch"]
    WB_t = env["WB"]
    COLS = env["COLS"]
    HT = 11

    emit_ln(env)

    tag = ph["tag"]
    sg = din(tag + "_sg"); su = din(tag + "_su"); sd = din(tag + "_sd")
    rg = din(tag + "_rg"); ru = din(tag + "_ru"); rd_ = din(tag + "_rd")
    wr = din(tag + "_wr"); rb = din(tag + "_rb")

    AH = arena.bf16(HT * T)
    W7 = arena.bf16(T)
    LGT = arena.f32(CH)
    WTK = arena.f32(16)
    WRP = arena.f32(CH)
    WRB = arena.bf16(64)
    GU = Pool([Buf(WB_t[:, i * 1024:(i + 1) * 1024]) for i in range(4)])
    DBs = [Buf(WB_t[:, 0:1408]), Buf(WB_t[:, 1408:2816])]
    DBp = Pool(DBs)

    fetch_w(env, WRB, wr[:, 0:56], cols=56)
    COLS.w(p.load(COLS.ap[0:7, 8:9], rb, deps=COLS.wd(), stream="cols"))

    resid_spill = dram_scratch("moe_resid", [DM, T], F32)

    for ch in range(NCH):
        wd = bk.wdeps(6)
        steps = [(WRB.ap[:, c * 7:(c + 1) * 7],
                  XLB[c].ap[:, ch * CH:(ch + 1) * CH]) for c in range(CT)]
        hl = p.mmgrp(pb[6][0:7, :], steps, deps=wd + WRB.rd() + _xl_rd(env))
        WRB.r(hl)
        for c in range(CT):
            XLB[c].r(hl)
        bk.set_w(6, hl)
        hev = p.v(lambda e: e.tensor_scalar(LGT.ap[0:7, :], pb[6][0:7, :],
                                            COLS.ap[0:7, 8:9], None, ALU.add),
                  deps=bk.rdeps(6) + COLS.rd() + LGT.wd())
        COLS.r(hev)
        LGT.w(hev)
        bk.add_r(6, hev)
        for tb in range(4):
            t0 = ch * 4 + tb
            wd7 = bk.wdeps(7)
            htr = p.on("tensor",
                       lambda e, s=LGT.ap[0:7, tb * 128:(tb + 1) * 128]:
                       e.transpose(pb[7][:, 0:7], s, IDT.ap[0:7, 0:7]),
                       deps=wd7 + LGT.rd() + IDT.rd(), inc="pe")
            LGT.r(htr)
            IDT.r(htr)
            bk.set_w(7, htr)
            hm = p.v(lambda e: e.reduce_max(WTK.ap[:, 0:1], pb[7][:, 0:7], axis=AX.X),
                     deps=bk.rdeps(7) + WTK.wd())
            WTK.w(hm)
            bk.add_r(7, hm)
            hng = p.v(lambda e: e.tensor_scalar(WTK.ap[:, 1:2], WTK.ap[:, 0:1],
                                                -1.0, None, ALU.mult), deps=WTK.rd())
            WTK.w(hng)
            hex_ = p.s(lambda e: e.activation(SC1.ap[:, 0:7], pb[7][:, 0:7], AF.Exp,
                                              bias=WTK.ap[:, 1:2]),
                       deps=bk.rdeps(7) + WTK.rd() + SC1.wd())
            WTK.r(hex_)
            SC1.w(hex_)
            bk.add_r(7, hex_)
            hs1 = p.s(lambda e: e.activation(SC2.ap[:, 0:7], SC1.ap[:, 0:7], AF.Sign,
                                             bias=-1.0),
                      deps=SC1.rd() + SC2.wd() + env["ones_h"])
            SC1.r(hs1)
            SC2.w(hs1)
            hm1 = p.v(lambda e: e.tensor_scalar(SC2.ap[:, 0:7], SC2.ap[:, 0:7],
                                                1.0, None, ALU.add), deps=SC2.rd())
            SC2.w(hm1)
            hpm = p.v(lambda e: e.scalar_tensor_tensor(SC1.ap[:, 8:15], SC2.ap[:, 0:7],
                                                       -1e30, SC1.ap[:, 0:7],
                                                       ALU.mult, ALU.add),
                      deps=SC2.rd() + SC1.rd())
            SC2.r(hpm)
            SC1.w(hpm)
            hm2 = p.v(lambda e: e.reduce_max(WTK.ap[:, 2:3], SC1.ap[:, 8:15], axis=AX.X),
                      deps=SC1.rd() + WTK.wd())
            WTK.w(hm2)
            hng2 = p.v(lambda e: e.tensor_scalar(WTK.ap[:, 3:4], WTK.ap[:, 2:3],
                                                 -1.0, None, ALU.mult), deps=WTK.rd())
            WTK.w(hng2)
            hs2 = p.s(lambda e: e.activation(SC1.ap[:, 8:15], SC1.ap[:, 8:15], AF.Sign,
                                             bias=WTK.ap[:, 3:4]),
                      deps=WTK.rd() + SC1.rd())
            WTK.r(hs2)
            SC1.w(hs2)
            hm2b = p.v(lambda e: e.tensor_scalar(SC1.ap[:, 8:15], SC1.ap[:, 8:15],
                                                 1.0, None, ALU.add), deps=SC1.rd())
            SC1.w(hm2b)
            hdn = p.v(lambda e: e.tensor_scalar(WTK.ap[:, 4:5], WTK.ap[:, 2:3],
                                                1.0, None, ALU.add), deps=WTK.rd())
            WTK.w(hdn)
            hrdn = p.v(lambda e: e.reciprocal(WTK.ap[:, 4:5], WTK.ap[:, 4:5]),
                       deps=WTK.rd())
            WTK.w(hrdn)
            hw1 = p.v(lambda e: e.scalar_tensor_tensor(SC2.ap[:, 0:7], SC1.ap[:, 8:15],
                                                       WTK.ap[:, 2:3], SC2.ap[:, 0:7],
                                                       ALU.mult, ALU.add),
                      deps=SC1.rd() + WTK.rd() + SC2.rd())
            SC1.r(hw1)
            WTK.r(hw1)
            SC2.w(hw1)
            hw2 = p.v(lambda e: e.tensor_scalar(SC2.ap[:, 0:7], SC2.ap[:, 0:7],
                                                WTK.ap[:, 4:5], None, ALU.mult),
                      deps=SC2.rd() + WTK.rd())
            WTK.r(hw2)
            SC2.w(hw2)
            wd7b = bk.wdeps(7)
            htb = p.on("tensor", lambda e: e.transpose(pb[7][0:7, 0:128], SC2.ap[:, 0:7],
                                                       IDT.ap[:, :]),
                       deps=wd7b + SC2.rd() + IDT.rd(), inc="pe")
            SC2.r(htb)
            IDT.r(htb)
            bk.set_w(7, htb)
            hwv = p.v(lambda e, o=W7.ap[0:7, t0 * 128:(t0 + 1) * 128]:
                      e.tensor_copy(o, pb[7][0:7, 0:128]),
                      deps=bk.rdeps(7) + (W7.wd() if t0 == 0 else []))
            bk.add_r(7, hwv)
            W7.w(hwv)

    if ph.get("dbg_xl"):
        dbx = env["nc"].dram_tensor("dbg_xl", [DM, T], BF16, kind="ExternalOutput").ap()
        for c in range(CT):
            h = p.store(dbx[c * 128:(c + 1) * 128, :], XLB[c].ap, deps=XLB[c].rd(),
                        stream="dbg")
            XLB[c].r(h)
    if ph.get("dbg"):
        dbg = env["nc"].dram_tensor("dbg_w", [16, T], F32, kind="ExternalOutput").ap()
        dbw = p.v(lambda e: e.tensor_copy(SC1.ap[0:16, 0:7], SC1.ap[0:16, 0:7]),
                  deps=[])
        hst = p.store(dbg[0:7, :], W7.ap[0:7, :], deps=[W7.writer], stream="dbg")
        W7.r(hst)

    sp_hs = []
    for c in range(CT):
        sp_hs.append(p.store(resid_spill[c * 128:(c + 1) * 128, :], RB[c].ap,
                             deps=RB[c].rd(), stream="res"))
    for c in range(CT):
        RB[c].r(sp_hs[-1])

    n_ex = 1 if ph.get("only_shared") else 8
    for ex in range(n_ex):
        if ex == 0:
            gT, uT, dT = sg, su, sd
        else:
            gT, uT, dT = rg[ex - 1], ru[ex - 1], rd_[ex - 1]
        for half in range(2):
            cross(GU.bufs, DBs)
            a_done = []
            for ib in range(HT):
                i0 = half * HT + ib
                WGB = GU.next()
                WUB = GU.next()
                fetch_w(env, WGB, gT[:, i0 * 1024:(i0 + 1) * 1024])
                fetch_w(env, WUB, uT[:, i0 * 1024:(i0 + 1) * 1024])
                for ch in range(NCH):
                    bg = ch % 2
                    bu = 2 + ch % 2
                    wdg = bk.wdeps(bg)
                    steps = [(WGB.ap[:, c * 128:(c + 1) * 128],
                              XLB[c].ap[:, ch * CH:(ch + 1) * CH]) for c in range(CT)]
                    hgm = p.mmgrp(pb[bg][:, :], steps,
                                  deps=wdg + WGB.rd() + _xl_rd(env))
                    WGB.r(hgm)
                    for c in range(CT):
                        XLB[c].r(hgm)
                    bk.set_w(bg, hgm)
                    wdu = bk.wdeps(bu)
                    steps = [(WUB.ap[:, c * 128:(c + 1) * 128],
                              XLB[c].ap[:, ch * CH:(ch + 1) * CH]) for c in range(CT)]
                    hum = p.mmgrp(pb[bu][:, :], steps, deps=wdu + WUB.rd())
                    WUB.r(hum)
                    for c in range(CT):
                        XLB[c].r(hum)
                    bk.set_w(bu, hum)
                    hsi = p.s(lambda e, b_=bg: e.activation(SC1.ap, pb[b_][:, :], AF.Silu),
                              deps=bk.rdeps(bg) + SC1.wd())
                    SC1.w(hsi)
                    bk.add_r(bg, hsi)
                    ha = p.v(lambda e, o=AH.ap[:, ib * T + ch * CH: ib * T + (ch + 1) * CH],
                             b_=bu: e.tensor_tensor(o, SC1.ap, pb[b_][:, :], ALU.mult),
                             deps=bk.rdeps(bu) + SC1.rd() +
                             (AH.wd() if (ib == 0 and ch == 0) else []))
                    SC1.r(ha)
                    bk.add_r(bu, ha)
                    AH.w(ha)
                    a_done.append(ha)
            cross(DBs, GU.bufs)
            for c in range(CT):
                DB = DBp.next()
                base_ = (half * 8 + c) * 1408
                fetch_w(env, DB, dT[:, base_:base_ + 768], dst_off=0, cols=768)
                fetch_w(env, DB, dT[:, base_ + 768:base_ + 1408],
                        dst_off=768, cols=640)
                for ch in range(NCH):
                    b = 4 + (ch % 2)
                    wd = bk.wdeps(b)
                    steps = [(DB.ap[:, ib * 128:(ib + 1) * 128],
                              AH.ap[:, ib * T + ch * CH: ib * T + (ch + 1) * CH])
                             for ib in range(HT)]
                    hmm = p.mmgrp(pb[b][:, :], steps, deps=wd + DB.rd() + a_done)
                    DB.r(hmm)
                    AH.r(hmm)
                    bk.set_w(b, hmm)
                    rt = RB[c].ap[:, ch * CH:(ch + 1) * CH]
                    if ex == 0:
                        if half == 0:
                            hev = p.s(lambda e, o=rt, b_=b:
                                      e.activation(o, pb[b_][:, :], AF.Copy),
                                      deps=bk.rdeps(b) + RB[c].wd())
                        else:
                            hev = p.v(lambda e, o=rt, b_=b:
                                      e.tensor_tensor(o, o, pb[b_][:, :], ALU.add),
                                      deps=bk.rdeps(b) + RB[c].rd())
                        bk.add_r(b, hev)
                        RB[c].w(hev)
                    else:
                        wd6 = bk.wdeps(6)
                        hwre = p.mm(pb[6][:, :], E16.ap[0:7, (ex - 1) * 128:ex * 128],
                                    W7.ap[0:7, ch * CH:(ch + 1) * CH], True, True,
                                    deps=wd6 + E16.rd() + W7.rd(), inc=True)
                        E16.r(hwre)
                        W7.r(hwre)
                        bk.set_w(6, hwre)
                        hcp = p.s(lambda e: e.activation(WRP.ap, pb[6][:, :], AF.Copy),
                                  deps=bk.rdeps(6) + WRP.wd())
                        WRP.w(hcp)
                        bk.add_r(6, hcp)
                        hmu = p.v(lambda e, b_=b: e.tensor_tensor(SC2.ap, pb[b_][:, :],
                                                                  WRP.ap, ALU.mult),
                                  deps=bk.rdeps(b) + WRP.rd() + SC2.wd())
                        WRP.r(hmu)
                        SC2.w(hmu)
                        bk.add_r(b, hmu)
                        hev = p.v(lambda e, o=rt: e.tensor_tensor(o, o, SC2.ap, ALU.add),
                                  deps=SC2.rd() + RB[c].rd())
                        SC2.r(hev)
                        RB[c].w(hev)

    if ph.get("only_shared"):
        return
    for c in range(CT):
        for ch in range(NCH):
            sb = SC1 if ch % 2 == 0 else SC2
            ld = p.load(sb.ap, resid_spill[c * 128:(c + 1) * 128, ch * CH:(ch + 1) * CH],
                        deps=sb.wd(), stream="sc1" if ch % 2 == 0 else "sc2")
            sb.w(ld)
            rt = RB[c].ap[:, ch * CH:(ch + 1) * CH]
            ha = p.v(lambda e, o=rt, i=sb.ap: e.tensor_tensor(o, o, i, ALU.add),
                     deps=sb.rd() + RB[c].rd())
            sb.r(ha)
            RB[c].w(ha)


def emit_l1ffn(env, ph):
    p, bk, pb, arena = env["p"], env["bk"], env["pb"], env["arena"]
    RB, XLB = env["RB"], env["XLB"]
    SC1, SC2 = env["SC1"], env["SC2"]
    din, dram_scratch = env["din"], env["dram_scratch"]
    WB_t = env["WB"]
    COLS = env["COLS"]

    emit_ln(env)

    tag = ph["tag"]
    wr_ = din(tag + "_wr"); wk_ = din(tag + "_wk"); wv_ = din(tag + "_wv")
    hmu = COLS.w(p.load(COLS.ap[:, 0:16], din(tag + "_cols"), deps=COLS.wd(), stream="cols"))
    homu = COLS.w(p.v(lambda e: e.tensor_scalar(COLS.ap[:, 16:32], COLS.ap[:, 0:16],
                                                -1.0, 1.0, ALU.mult, ALU.add),
                      deps=COLS.rd() + COLS.wd()))

    XRK = arena.bf16(T * CT)
    TMP = arena.bf16(T)
    WP = Pool([Buf(WB_t[:, i * 1024:(i + 1) * 1024]) for i in range(4)])
    r_spill = dram_scratch(tag + "_r", [DM, T], BF16)

    def mix(dst_tile, mu_off, omu_off, c, dst_buf):
        xt = XLB[c].ap
        h1 = p.v(lambda e: e.tensor_scalar(TMP.ap[:, 1:T], xt[:, 0:T - 1],
                                           COLS.ap[:, omu_off + c:omu_off + c + 1],
                                           None, ALU.mult),
                 deps=COLS.rd() + XLB[c].rd() + TMP.wd())
        COLS.r(h1)
        XLB[c].r(h1)
        TMP.w(h1)
        h2 = p.v(lambda e: e.memset(TMP.ap[:, 0:1], 0.0), deps=[])
        TMP.w(h2)
        h3 = p.v(lambda e, o=dst_tile: e.scalar_tensor_tensor(
            o, xt, COLS.ap[:, mu_off + c:mu_off + c + 1], TMP.ap, ALU.mult, ALU.add),
            deps=TMP.rd() + COLS.rd() + XLB[c].rd() + dst_buf.wd())
        COLS.r(h3)
        XLB[c].r(h3)
        TMP.r(h3)
        dst_buf.w(h3)
        return h3

    xr_h = []
    for c in range(CT):
        xr_h.append(mix(XRK.ap[:, c * T:(c + 1) * T], 0, 16, c, XRK))
    r_done = []
    sb = env["SCB"]
    for et in range(CT):
        WBu = WP.next()
        fetch_w(env, WBu, wr_[:, et * 1024:(et + 1) * 1024])
        for ch in range(NCH):
            b = ch % 2
            wd = bk.wdeps(b)
            steps = [(WBu.ap[:, c * 128:(c + 1) * 128],
                      XRK.ap[:, c * T + ch * CH: c * T + (ch + 1) * CH]) for c in range(CT)]
            hmm = p.mmgrp(pb[b][:, :], steps, deps=wd + WBu.rd() + xr_h)
            WBu.r(hmm)
            XRK.r(hmm)
            bk.set_w(b, hmm)
            hev = p.s(lambda e, o=sb.ap[:, 0:CH], b_=b:
                      e.activation(o, pb[b_][:, :], AF.Sigmoid),
                      deps=bk.rdeps(b) + sb.wd())
            sb.w(hev)
            bk.add_r(b, hev)
            hst = p.store(r_spill[et * 128:(et + 1) * 128, ch * CH:(ch + 1) * CH],
                          sb.ap[:, 0:CH], deps=sb.rd(), stream="rsp")
            sb.r(hst)
            r_done.append(hst)
    xk_h = []
    for c in range(CT):
        xk_h.append(mix(XLB[c].ap, 8, 24, c, XLB[c]))
    ka_h = []
    for et in range(CT):
        WBu = WP.next()
        fetch_w(env, WBu, wk_[:, et * 1024:(et + 1) * 1024])
        for ch in range(NCH):
            b = ch % 2
            wd = bk.wdeps(b)
            steps = [(WBu.ap[:, c * 128:(c + 1) * 128],
                      XLB[c].ap[:, ch * CH:(ch + 1) * CH]) for c in range(CT)]
            hmm = p.mmgrp(pb[b][:, :], steps, deps=wd + WBu.rd() + xk_h + XRK.wd())
            WBu.r(hmm)
            for c in range(CT):
                XLB[c].r(hmm)
            bk.set_w(b, hmm)
            h1 = p.s(lambda e, b_=b: e.activation(SC1.ap, pb[b_][:, :], AF.Relu),
                     deps=bk.rdeps(b) + SC1.wd())
            SC1.w(h1)
            bk.add_r(b, h1)
            h2 = p.s(lambda e, o=XRK.ap[:, et * T + ch * CH: et * T + (ch + 1) * CH]:
                     e.activation(o, SC1.ap, AF.Square), deps=SC1.rd())
            SC1.r(h2)
            XRK.w(h2)
            ka_h.append(h2)
    for c in range(CT):
        WBu = WP.next()
        fetch_w(env, WBu, wv_[:, c * 1024:(c + 1) * 1024])
        for ch in range(NCH):
            b = 2 + ch % 2
            wd = bk.wdeps(b)
            steps = [(WBu.ap[:, e_ * 128:(e_ + 1) * 128],
                      XRK.ap[:, e_ * T + ch * CH: e_ * T + (ch + 1) * CH])
                     for e_ in range(CT)]
            hmm = p.mmgrp(pb[b][:, :], steps, deps=wd + WBu.rd() + ka_h)
            WBu.r(hmm)
            XRK.r(hmm)
            bk.set_w(b, hmm)
            ldr = p.load(sb.ap[:, 0:CH], r_spill[c * 128:(c + 1) * 128,
                                                 ch * CH:(ch + 1) * CH],
                         deps=r_done + sb.wd(), stream="scb")
            sb.w(ldr)
            hmu2 = p.v(lambda e, b_=b: e.tensor_tensor(SC1.ap, pb[b_][:, :],
                                                       sb.ap[:, 0:CH], ALU.mult),
                       deps=bk.rdeps(b) + sb.rd() + SC1.wd())
            sb.r(hmu2)
            SC1.w(hmu2)
            bk.add_r(b, hmu2)
            rt = RB[c].ap[:, ch * CH:(ch + 1) * CH]
            hev = p.v(lambda e, o=rt: e.tensor_tensor(o, o, SC1.ap, ALU.add),
                      deps=SC1.rd() + RB[c].rd() + RB[c].wd())
            SC1.r(hev)
            RB[c].w(hev)


def emit_mamba(env, ph):
    p, bk, pb, arena = env["p"], env["bk"], env["pb"], env["arena"]
    RB, XLB = env["RB"], env["XLB"]
    SC1, SC2, SCB = env["SC1"], env["SC2"], env["SCB"]
    SLOT = env["SLOT"]
    ROWA, ROWB, ROWC = env["ROWA"], env["ROWB"], env["ROWC"]
    ONEC, ONER = env["ONEC"], env["ONER"]
    E16 = env["E16"]
    HC, AS = env["HC"], env["AS"]
    din, dram_scratch = env["din"], env["dram_scratch"]
    WB_t = env["WB"]
    COLS = env["COLS"]
    R_t = env["R_t"]
    DT_ = 16

    emit_ln(env)

    tag = ph["tag"]
    ipw = din(tag + "_ipw")
    xpw = din(tag + "_xpw")
    dpw = din(tag + "_dpw")
    opw = din(tag + "_opw")
    f1w = din(tag + "_f1w")
    f2w = din(tag + "_f2w")
    alog = din(tag + "_alog")
    mcols = din(tag + "_cols")
    COLS.w(p.load(COLS.ap[:], mcols, deps=COLS.wd(), stream="cols"))

    AS.w(p.load(AS.ap, alog, deps=AS.wd(), stream="as"))
    HCB = [Buf(HC.ap[:, i:i + 1]) for i in range(256)]
    AS.w(p.s(lambda e: e.activation(AS.ap, AS.ap, AF.Exp), deps=AS.rd() + AS.wd()))
    AS.w(p.v(lambda e: e.tensor_scalar(AS.ap, AS.ap, -1.0, None, ALU.mult),
             deps=AS.rd() + AS.wd()))
    hmz = p.v(lambda e: e.memset(HC.ap, 0.0), deps=HC.wd())
    HC.w(hmz)
    for b_ in HCB:
        b_.w(hmz)

    xs_spill = dram_scratch(tag + "_xs", [2048, T], BF16)
    z_spill = dram_scratch(tag + "_z", [2048, T], BF16)
    mo_spill = dram_scratch(tag + "_mo", [DM, T], BF16)
    resid_spill = dram_scratch(tag + "_res", [DM, T], F32)

    XSC = arena.bf16_at(0, 8192)
    DTC = arena.bf16_at(8192, 8192)
    BRP = arena.bf16_at(25088, 512)
    CRP = arena.bf16_at(25600, 512)
    DTW = arena.bf16_at(16384, 2048)
    DTRB = arena.bf16_at(18432, 2048)
    BT = arena.bf16_at(20480, 2048)
    CTb = arena.bf16_at(22528, 2048)
    MOS = arena.bf16_at(24576, 512)
    XPW = arena.bf16_at(8192, 1536)
    FC = arena.bf16_at(0, 8192)
    S2 = arena.f32_at(8192, 4096)
    WP4 = Pool([Buf(WB_t[:, i * 1024:(i + 1) * 1024]) for i in range(4)])
    WP2 = Pool([Buf(WB_t[:, 0:2048]), Buf(WB_t[:, 2048:4096])])

    CONVT = Buf(R_t[:, 0:2052])
    CACC = Buf(R_t[:, 2052:4100])
    Y = Buf(R_t[:, 4104:12296])

    wst = env["WSTb"][0]
    for hx in range(2):
        ld = p.load(wst.ap[0:64, 0:1024], dpw[:, hx * 1024:(hx + 1) * 1024],
                    deps=wst.wd(), stream="wst")
        wst.w(ld)
        cv = p.v(lambda e, o=DTW.ap[0:64, hx * 1024:(hx + 1) * 1024]:
                 e.tensor_copy(o, wst.ap[0:64, 0:1024]),
                 deps=wst.rd() + (DTW.wd() if hx == 0 else []))
        wst.r(cv)
        DTW.w(cv)

    sp_h = []
    for c in range(CT):
        h = p.store(resid_spill[c * 128:(c + 1) * 128, :], RB[c].ap,
                    deps=RB[c].rd(), stream="res")
        sp_h.append(h)
    for c in range(CT):
        RB[c].r(sp_h[-1])
    CONVT.w(sp_h[-1])
    CACC.w(sp_h[-1])
    Y.w(sp_h[-1])

    hz3 = CONVT.w(p.v(lambda e: e.memset(CONVT.ap[:, 0:3], 0.0), deps=CONVT.wd()))
    xs_done = []
    z_done = []
    for et in range(32):
        IPWb = WP4.next()
        fetch_w(env, IPWb, ipw[:, et * 1024:(et + 1) * 1024])
        is_xs = et < 16
        for ch in range(NCH):
            b = ch % 2
            wd = bk.wdeps(b)
            steps = [(IPWb.ap[:, c * 128:(c + 1) * 128],
                      XLB[c].ap[:, ch * CH:(ch + 1) * CH]) for c in range(CT)]
            hmm = p.mmgrp(pb[b][:, :], steps, deps=wd + IPWb.rd() + _xl_rd(env))
            IPWb.r(hmm)
            for c in range(CT):
                XLB[c].r(hmm)
            bk.set_w(b, hmm)
            if is_xs:
                hev = p.s(lambda e, o=CONVT.ap[:, 3 + ch * CH: 3 + (ch + 1) * CH], b_=b:
                          e.activation(o, pb[b_][:, :], AF.Copy),
                          deps=bk.rdeps(b) + (CONVT.wd() if ch == 0 else []))
                CONVT.w(hev)
                bk.add_r(b, hev)
            else:
                hev = p.s(lambda e, o=SCB.ap[:, ch * CH:(ch + 1) * CH], b_=b:
                          e.activation(o, pb[b_][:, :], AF.Copy),
                          deps=bk.rdeps(b) + (SCB.wd() if ch == 0 else []))
                SCB.w(hev)
                bk.add_r(b, hev)
        if is_xs:
            h1 = p.v(lambda e, w=COLS.ap[:, et * 4:et * 4 + 1]:
                     e.tensor_scalar(CACC.ap, CONVT.ap[:, 0:T], w, None, ALU.mult),
                     deps=COLS.rd() + CONVT.rd() + CACC.wd())
            COLS.r(h1)
            CONVT.r(h1)
            CACC.w(h1)
            for j in (1, 2, 3):
                hj = p.v(lambda e, j_=j, w=COLS.ap[:, et * 4 + j:et * 4 + j + 1]:
                         e.scalar_tensor_tensor(CACC.ap, CONVT.ap[:, j_:j_ + T], w,
                                                CACC.ap, ALU.mult, ALU.add),
                         deps=COLS.rd() + CONVT.rd() + CACC.rd())
                COLS.r(hj)
                CONVT.r(hj)
                CACC.w(hj)
            hsl = p.s(lambda e, bc=COLS.ap[:, 128 + et:129 + et]:
                      e.activation(SCB.ap, CACC.ap, AF.Silu, bias=bc),
                      deps=CACC.rd() + SCB.wd() + COLS.rd())
            COLS.r(hsl)
            CACC.r(hsl)
            SCB.w(hsl)
            hst = p.store(xs_spill[et * 128:(et + 1) * 128, :], SCB.ap, deps=SCB.rd(),
                          stream="xs")
            SCB.r(hst)
            xs_done.append(hst)
        else:
            hst = p.store(z_spill[(et - 16) * 128:(et - 15) * 128, :], SCB.ap,
                          deps=SCB.rd(), stream="z")
            SCB.r(hst)
            z_done.append(hst)

    fetch_w(env, XPW, xpw[:, 0:768], dst_off=0, cols=768)
    fetch_w(env, XPW, xpw[:, 768:1536], dst_off=768, cols=768)
    for ch in range(NCH):
        w0 = bk.wdeps(0)
        w1 = bk.wdeps(1)
        w2 = bk.wdeps(2)
        hdtr = hB = hC = None
        for dt in range(DT_):
            sl = SLOT[dt % 2]
            ld = p.load(sl.ap, xs_spill[dt * 128:(dt + 1) * 128, ch * CH:(ch + 1) * CH],
                        deps=xs_done + sl.wd(), stream=f"slot{dt % 2}")
            sl.w(ld)
            hdtr = p.mm(pb[0][0:64, :], XPW.ap[:, dt * 96:dt * 96 + 64], sl.ap,
                        dt == 0, dt == DT_ - 1,
                        deps=(w0 + XPW.rd() + sl.rd()) if dt == 0 else sl.rd(),
                        inc=True)
            sl.r(hdtr)
            XPW.r(hdtr)
            hB = p.mm(pb[1][0:16, :], XPW.ap[:, dt * 96 + 64:dt * 96 + 80], sl.ap,
                      dt == 0, dt == DT_ - 1,
                      deps=(w1 + sl.rd()) if dt == 0 else sl.rd(),
                      inc=True)
            sl.r(hB)
            hC = p.mm(pb[2][0:16, :], XPW.ap[:, dt * 96 + 80:dt * 96 + 96], sl.ap,
                      dt == 0, dt == DT_ - 1,
                      deps=(w2 + sl.rd()) if dt == 0 else sl.rd(),
                      inc=True)
            sl.r(hC)
        bk.set_w(0, hdtr)
        bk.set_w(1, hB)
        bk.set_w(2, hC)
        SLOT[0].r(hC)
        SLOT[1].r(hC)
        h1 = p.s(lambda e, o=DTRB.ap[0:64, ch * CH:(ch + 1) * CH]:
                 e.activation(o, pb[0][0:64, :], AF.Copy),
                 deps=bk.rdeps(0) + (DTRB.wd() if ch == 0 else []))
        DTRB.w(h1)
        bk.add_r(0, h1)
        h2 = p.v(lambda e, o=BT.ap[0:16, ch * CH:(ch + 1) * CH]:
                 e.tensor_copy(o, pb[1][0:16, :]),
                 deps=bk.rdeps(1) + (BT.wd() if ch == 0 else []))
        BT.w(h2)
        bk.add_r(1, h2)
        h3 = p.v(lambda e, o=CTb.ap[0:16, ch * CH:(ch + 1) * CH]:
                 e.tensor_copy(o, pb[2][0:16, :]),
                 deps=bk.rdeps(2) + (CTb.wd() if ch == 0 else []))
        CTb.w(h3)
        bk.add_r(2, h3)
    DTC.r(bk.st[0]["w"])
    DTC.r(bk.st[1]["w"])
    DTC.r(bk.st[2]["w"])
    DTC.users += XPW.wd()

    cross(WP2.bufs, WP4.bufs)
    mo_done = []
    for ch in range(NCH):
        for dt in range(DT_):
            h = p.load(XSC.ap[:, dt * CH:(dt + 1) * CH],
                       xs_spill[dt * 128:(dt + 1) * 128, ch * CH:(ch + 1) * CH],
                       deps=xs_done + (XSC.wd() if dt == 0 else []), stream="xsc")
            XSC.w(h)
        for dt in range(DT_):
            b = dt % 2
            wd = bk.wdeps(b)
            hdt = p.mm(pb[b][:, :], DTW.ap[0:64, dt * 128:(dt + 1) * 128],
                       DTRB.ap[0:64, ch * CH:(ch + 1) * CH], True, True,
                       deps=wd + DTW.rd() + DTRB.rd(), inc=True)
            DTW.r(hdt)
            DTRB.r(hdt)
            bk.set_w(b, hdt)
            bc = COLS.ap[:, 80 + dt:81 + dt]
            ha = p.s(lambda e, b_=b, bc_=bc: e.activation(SC1.ap, pb[b_][:, :], AF.Abs,
                                                          bias=bc_),
                     deps=bk.rdeps(b) + SC1.wd() + COLS.rd())
            COLS.r(ha)
            SC1.w(ha)
            bk.add_r(b, ha)
            hb = p.s(lambda e: e.activation(SC1.ap, SC1.ap, AF.Exp, scale=-1.0),
                     deps=SC1.rd())
            SC1.w(hb)
            hc = p.s(lambda e: e.activation(SC1.ap, SC1.ap, AF.Ln, bias=1.0),
                     deps=SC1.rd())
            SC1.w(hc)
            hd = p.s(lambda e, b_=b, bc_=bc: e.activation(SC2.ap, pb[b_][:, :], AF.Relu,
                                                          bias=bc_),
                     deps=bk.rdeps(b) + SC2.wd() + COLS.rd())
            COLS.r(hd)
            SC2.w(hd)
            bk.add_r(b, hd)
            he = p.v(lambda e, o=DTC.ap[:, dt * CH:(dt + 1) * CH]:
                     e.tensor_tensor(o, SC1.ap, SC2.ap, ALU.add),
                     deps=SC1.rd() + SC2.rd() + (DTC.wd() if dt == 0 else []))
            SC1.r(he)
            SC2.r(he)
            DTC.w(he)
            hf = p.v(lambda e, o=XSC.ap[:, dt * CH:(dt + 1) * CH],
                     i=DTC.ap[:, dt * CH:(dt + 1) * CH]:
                     e.tensor_tensor(o, i, o, ALU.mult),
                     deps=DTC.rd() + XSC.rd() + XSC.wd())
            DTC.r(hf)
            XSC.w(hf)
        hy0 = Y.w(p.v(lambda e: e.memset(Y.ap, 0.0), deps=Y.wd()))
        for n in range(16):
            w6 = bk.wdeps(6)
            hbr = p.mm(pb[6][:, :], E16.ap[0:16, n * 128:(n + 1) * 128],
                       BT.ap[0:16, ch * CH:(ch + 1) * CH], True, True,
                       deps=w6 + E16.rd() + BT.rd(), inc=True)
            E16.r(hbr)
            BT.r(hbr)
            bk.set_w(6, hbr)
            hbe = p.s(lambda e: e.activation(BRP.ap, pb[6][:, :], AF.Copy),
                      deps=bk.rdeps(6) + BRP.wd())
            BRP.w(hbe)
            bk.add_r(6, hbe)
            w7 = bk.wdeps(7)
            hcr = p.mm(pb[7][:, :], E16.ap[0:16, n * 128:(n + 1) * 128],
                       CTb.ap[0:16, ch * CH:(ch + 1) * CH], True, True,
                       deps=w7 + E16.rd() + CTb.rd(), inc=True)
            E16.r(hcr)
            CTb.r(hcr)
            bk.set_w(7, hcr)
            hce = p.s(lambda e: e.activation(CRP.ap, pb[7][:, :], AF.Copy),
                      deps=bk.rdeps(7) + CRP.wd())
            CRP.w(hce)
            bk.add_r(7, hce)
            for dt in range(DT_):
                it = n * DT_ + dt
                _set = 3 * (it % 2)
                sA, sB, sC = SLOT[_set], SLOT[_set + 1], SLOT[_set + 2]
                scan_v = (it % 2 == 0)
                hda = p.s(lambda e, i=DTC.ap[:, dt * CH:(dt + 1) * CH],
                          sc=AS.ap[:, dt * 16 + n:dt * 16 + n + 1], o=sA.ap:
                          e.activation(o, i, AF.Exp, scale=sc),
                          deps=DTC.rd() + AS.rd() + sA.wd())
                DTC.r(hda)
                AS.r(hda)
                sA.w(hda)
                hdb = p.v(lambda e, i=XSC.ap[:, dt * CH:(dt + 1) * CH], o=sB.ap:
                          e.tensor_tensor(o, i, BRP.ap, ALU.mult),
                          deps=XSC.rd() + BRP.rd() + sB.wd())
                XSC.r(hdb)
                BRP.r(hdb)
                sB.w(hdb)
                hcb = HCB[dt * 16 + n]
                hsc = p.v(lambda e, ini=hcb.ap,
                          o=sC.ap, a=sA.ap, b2=sB.ap:
                          e.tensor_tensor_scan(o, a, b2, ini, ALU.mult, ALU.add),
                          deps=sA.rd() + sB.rd() + hcb.rd() + sC.wd())
                sA.r(hsc)
                sB.r(hsc)
                hcb.r(hsc)
                sC.w(hsc)
                hcy = p.on("gpsimd", lambda e, o=hcb.ap,
                           i=sC.ap[:, 511:512]: e.tensor_copy(o, i),
                           deps=sC.rd() + hcb.wd(), inc="ge")
                sC.r(hcy)
                hcb.w(hcy)
                hym = p.v(lambda e, i=sC.ap, o=sB.ap:
                          e.tensor_tensor(o, i, CRP.ap, ALU.mult),
                          deps=sC.rd() + CRP.rd() + sB.wd())
                sC.r(hym)
                CRP.r(hym)
                sB.w(hym)
                hya = p.on("gpsimd",
                           lambda e, o=Y.ap[:, dt * CH:(dt + 1) * CH], i=sB.ap:
                           e.tensor_tensor(o, o, i, ALU.add),
                           deps=sB.rd() + Y.rd() + Y.wd(), inc="ge")
                sB.r(hya)
                Y.w(hya)
        for dt in range(DT_):
            h = p.load(XSC.ap[:, dt * CH:(dt + 1) * CH],
                       xs_spill[dt * 128:(dt + 1) * 128, ch * CH:(ch + 1) * CH],
                       deps=XSC.wd() if dt == 0 else (), stream="xsc")
            XSC.w(h)
            h = p.load(DTC.ap[:, dt * CH:(dt + 1) * CH],
                       z_spill[dt * 128:(dt + 1) * 128, ch * CH:(ch + 1) * CH],
                       deps=(z_done + DTC.wd()) if dt == 0 else (), stream="dtc")
            DTC.w(h)
        for dt in range(DT_):
            hya = p.v(lambda e, o=Y.ap[:, dt * CH:(dt + 1) * CH],
                      i=XSC.ap[:, dt * CH:(dt + 1) * CH],
                      d=COLS.ap[:, 64 + dt:65 + dt]:
                      e.scalar_tensor_tensor(o, i, d, o, ALU.mult, ALU.add),
                      deps=XSC.rd() + COLS.rd() + Y.rd() + Y.wd())
            XSC.r(hya)
            COLS.r(hya)
            Y.w(hya)
            sZ = SLOT[5]
            hsz = p.s(lambda e, i=DTC.ap[:, dt * CH:(dt + 1) * CH], o=sZ.ap:
                      e.activation(o, i, AF.Silu), deps=DTC.rd() + sZ.wd())
            DTC.r(hsz)
            sZ.w(hsz)
            hmx = p.v(lambda e, o=XSC.ap[:, dt * CH:(dt + 1) * CH],
                      y=Y.ap[:, dt * CH:(dt + 1) * CH], s=sZ.ap:
                      e.tensor_tensor(o, y, s, ALU.mult),
                      deps=Y.rd() + sZ.rd() + XSC.rd() + XSC.wd())
            sZ.r(hmx)
            Y.r(hmx)
            XSC.w(hmx)
        for c in range(CT):
            OPW = WP2.next()
            fetch_w(env, OPW, opw[:, c * 2048:c * 2048 + 1024], dst_off=0, cols=1024)
            fetch_w(env, OPW, opw[:, c * 2048 + 1024:(c + 1) * 2048],
                    dst_off=1024, cols=1024)
            b = 4 + c % 2
            wd = bk.wdeps(b)
            steps = [(OPW.ap[:, dt * 128:(dt + 1) * 128],
                      XSC.ap[:, dt * CH:(dt + 1) * CH]) for dt in range(DT_)]
            hmm = p.mmgrp(pb[b][:, :], steps, deps=wd + OPW.rd() + XSC.rd())
            OPW.r(hmm)
            XSC.r(hmm)
            bk.set_w(b, hmm)
            hev = p.s(lambda e, b_=b: e.activation(MOS.ap[:, 0:CH], pb[b_][:, :], AF.Copy),
                      deps=bk.rdeps(b) + MOS.wd())
            MOS.w(hev)
            bk.add_r(b, hev)
            hst = p.store(mo_spill[c * 128:(c + 1) * 128, ch * CH:(ch + 1) * CH],
                          MOS.ap[:, 0:CH], deps=MOS.rd(), stream="mo")
            MOS.r(hst)
            mo_done.append(hst)

    S2.users += DTC.wd()
    for ch in range(NCH):
        for c in range(CT):
            hld = p.load(MOS.ap[:, 0:CH],
                         mo_spill[c * 128:(c + 1) * 128, ch * CH:(ch + 1) * CH],
                         deps=mo_done + MOS.wd(), stream="mos")
            MOS.w(hld)
            hs = p.v(lambda e, o=S2.ap[:, c * CH:(c + 1) * CH],
                     x=XLB[c].ap[:, ch * CH:(ch + 1) * CH]:
                     e.tensor_tensor(o, x, MOS.ap[:, 0:CH], ALU.add),
                     deps=MOS.rd() + XLB[c].rd() + (S2.wd() if c == 0 else []))
            MOS.r(hs)
            XLB[c].r(hs)
            S2.w(hs)
        _affine_ln(env, ch, S2, 1e-12, 96, 104, dst="XL")
    r_free = Y.wd() + CONVT.wd() + CACC.wd()
    FC.users += XSC.wd()
    env["_RESID_SPILL"] = resid_spill
    env["_RFREE"] = r_free
    for ch in range(NCH):
        for et in range(16):
            F1W = WP2.next()
            fetch_w(env, F1W, f1w[:, et * 1024:(et + 1) * 1024])
            b = et % 2
            wd = bk.wdeps(b)
            steps = [(F1W.ap[:, c * 128:(c + 1) * 128],
                      XLB[c].ap[:, ch * CH:(ch + 1) * CH]) for c in range(CT)]
            hmm = p.mmgrp(pb[b][:, :], steps, deps=wd + F1W.rd() + _xl_rd(env) +
                          (FC.wd() if et == 0 else []))
            F1W.r(hmm)
            for c in range(CT):
                XLB[c].r(hmm)
            bk.set_w(b, hmm)
            hb1 = p.v(lambda e, b_=b, bc=COLS.ap[:, 144 + et:145 + et]:
                      e.tensor_scalar(SC1.ap, pb[b_][:, :], bc, None, ALU.add),
                      deps=bk.rdeps(b) + COLS.rd() + SC1.wd())
            COLS.r(hb1)
            SC1.w(hb1)
            bk.add_r(b, hb1)
            hlr = p.v(lambda e, o=FC.ap[:, et * CH:(et + 1) * CH]:
                      e.scalar_tensor_tensor(o, SC1.ap, 0.01, SC1.ap, ALU.mult, ALU.max),
                      deps=SC1.rd())
            SC1.r(hlr)
            FC.w(hlr)
        for c in range(CT):
            F2W = WP2.next()
            fetch_w(env, F2W, f2w[:, c * 2048:c * 2048 + 1024], dst_off=0, cols=1024)
            fetch_w(env, F2W, f2w[:, c * 2048 + 1024:(c + 1) * 2048],
                    dst_off=1024, cols=1024)
            b = 4 + c % 2
            wd = bk.wdeps(b)
            steps = [(F2W.ap[:, et * 128:(et + 1) * 128],
                      FC.ap[:, et * CH:(et + 1) * CH]) for et in range(16)]
            hmm = p.mmgrp(pb[b][:, :], steps, deps=wd + F2W.rd() + FC.rd())
            F2W.r(hmm)
            FC.r(hmm)
            bk.set_w(b, hmm)
            hb2 = p.v(lambda e, b_=b, bc=COLS.ap[:, 160 + c:161 + c]:
                      e.tensor_scalar(SC1.ap, pb[b_][:, :], bc, None, ALU.add),
                      deps=bk.rdeps(b) + SC1.wd() + COLS.rd())
            COLS.r(hb2)
            SC1.w(hb2)
            bk.add_r(b, hb2)
            hs2 = p.v(lambda e, o=S2.ap[:, c * CH:(c + 1) * CH],
                      x=XLB[c].ap[:, ch * CH:(ch + 1) * CH]:
                      e.tensor_tensor(o, SC1.ap, x, ALU.add),
                      deps=SC1.rd() + XLB[c].rd() + (S2.wd() if c == 0 else []))
            SC1.r(hs2)
            XLB[c].r(hs2)
            S2.w(hs2)
        _affine_ln(env, ch, S2, 1e-12, 112, 120, dst="RESID")


def _affine_ln(env, ch, S2, eps, w_off, b_off, dst):
    p, bk, pb = env["p"], env["bk"], env["pb"]
    XLB = env["XLB"]
    SC1, SC2 = env["SC1"], env["SC2"]
    SLOT = env["SLOT"]
    ROWS = [(env["ROWA"], env["ROWB"], env["ROWC"]),
            (env["ROWD"], env["ROWE"], env["ROWF"])]
    ONEC, ONER, ONECB = env["ONEC"], env["ONER"], env["ONECB"]
    COLS = env["COLS"]
    par = ch % 2
    b_sum, b_sq, b_m, b_r = (6, 5, 7, 4) if par == 0 else (1, 0, 2, 3)
    rA, rB, rC = ROWS[par]
    w6 = bk.wdeps(b_sum)
    w5 = bk.wdeps(b_sq)
    hsum = hsq = None
    for c in range(CT):
        st = S2.ap[:, c * CH:(c + 1) * CH]
        hsum = p.mm(pb[b_sum][0:1, :], ONEC[:], st, c == 0, c == CT - 1,
                    deps=(w6 + S2.rd()) if c == 0 else S2.rd(), inc=True)
        S2.r(hsum)
        sq_b = SLOT[(ch * 2 + c % 2) % 6]
        hsqa = p.s(lambda e, o=sq_b.ap, i=st: e.activation(o, i, AF.Square),
                   deps=sq_b.wd() + S2.rd() + (w5 if c == 0 else []))
        sq_b.w(hsqa)
        S2.r(hsqa)
        hsq = p.mm(pb[b_sq][0:1, :], ONECB[:], sq_b.ap, c == 0, c == CT - 1,
                   deps=sq_b.rd(), inc=True)
        sq_b.r(hsq)
    bk.set_w(b_sum, hsum)
    bk.set_w(b_sq, hsq)
    S2.r(hsq)
    ha = p.v(lambda e, o=rA.ap, b_=b_sum:
             e.tensor_scalar(o, pb[b_][0:1, :], 1.0 / DM, None, ALU.mult),
             deps=bk.rdeps(b_sum) + rA.wd())
    rA.w(ha)
    bk.add_r(b_sum, ha)
    hb = p.v(lambda e, o=rB.ap, b_=b_sq:
             e.tensor_scalar(o, pb[b_][0:1, :], 1.0 / DM, None, ALU.mult),
             deps=bk.rdeps(b_sq) + rB.wd())
    rB.w(hb)
    bk.add_r(b_sq, hb)
    hc = p.v(lambda e, o=rC.ap, a=rA.ap: e.tensor_tensor(o, a, a, ALU.mult),
             deps=rA.rd() + rC.wd())
    rA.r(hc)
    rC.w(hc)
    hd = p.v(lambda e, o=rC.ap, b2=rB.ap: e.tensor_tensor(o, b2, o, ALU.subtract),
             deps=rB.rd() + rC.rd())
    rB.r(hd)
    rC.w(hd)
    eps_ap = env["EPS5"] if eps == 1e-5 else env["EPS12"]
    he = p.s(lambda e, e_=eps_ap, o=rC.ap: e.activation(o, o, AF.Sqrt, bias=e_),
             deps=rC.rd() + env["ones_h"])
    rC.w(he)
    hf = p.v(lambda e, o=rC.ap: e.reciprocal(o, o), deps=rC.rd())
    rC.w(hf)
    w7 = bk.wdeps(b_m)
    hm = p.mm(pb[b_m][:, :], ONER[:], rA.ap, True, True, deps=w7 + rA.rd(), inc=True)
    rA.r(hm)
    bk.set_w(b_m, hm)
    w4 = bk.wdeps(b_r)
    hr = p.mm(pb[b_r][:, :], ONER[:], rC.ap, True, True, deps=w4 + rC.rd(), inc=True)
    rC.r(hr)
    bk.set_w(b_r, hr)
    RB = env["RB"]
    for c in range(CT):
        st = S2.ap[:, c * CH:(c + 1) * CH]
        h1 = p.v(lambda e, i=st, b_=b_m: e.tensor_tensor(SC1.ap, i, pb[b_][:, :],
                                                         ALU.subtract),
                 deps=bk.rdeps(b_m) + SC1.wd() + S2.rd())
        SC1.w(h1)
        S2.r(h1)
        bk.add_r(b_m, h1)
        h2 = p.v(lambda e, b_=b_r: e.tensor_tensor(SC1.ap, SC1.ap, pb[b_][:, :],
                                                   ALU.mult),
                 deps=bk.rdeps(b_r) + SC1.rd())
        SC1.w(h2)
        bk.add_r(b_r, h2)
        if dst == "XL":
            out = XLB[c].ap[:, ch * CH:(ch + 1) * CH]
            h3 = p.v(lambda e, o=out, wc=COLS.ap[:, w_off + c:w_off + c + 1],
                     bc2=COLS.ap[:, b_off + c:b_off + c + 1]:
                     e.tensor_scalar(o, SC1.ap, wc, bc2, ALU.mult, ALU.add),
                     deps=SC1.rd() + COLS.rd() + XLB[c].wd())
            SC1.r(h3)
            COLS.r(h3)
            XLB[c].w(h3)
        else:
            h3 = p.v(lambda e, wc=COLS.ap[:, w_off + c:w_off + c + 1],
                     bc2=COLS.ap[:, b_off + c:b_off + c + 1]:
                     e.tensor_scalar(SC1.ap, SC1.ap, wc, bc2, ALU.mult, ALU.add),
                     deps=SC1.rd() + COLS.rd())
            COLS.r(h3)
            SC1.w(h3)
            ld = p.load(SC2.ap, env["_RESID_SPILL"][c * 128:(c + 1) * 128,
                                                    ch * CH:(ch + 1) * CH],
                        deps=SC2.wd(), stream="sc2")
            SC2.w(ld)
            rt = RB[c].ap[:, ch * CH:(ch + 1) * CH]
            h4 = p.v(lambda e, o=rt: e.tensor_tensor(o, SC1.ap, SC2.ap, ALU.add),
                     deps=SC1.rd() + SC2.rd() + env["_RFREE"] + RB[c].wd())
            SC1.r(h4)
            SC2.r(h4)
            RB[c].w(h4)

def _ct(a):
    return np.ascontiguousarray(a, dtype=np.float32)


def _tile_w(wt, blk=128):
    C, E = wt.shape
    A = C // 128
    I = E // blk
    t = wt.reshape(A, 128, I, blk).transpose(1, 2, 0, 3).reshape(128, I * A * blk)
    return np.ascontiguousarray(t, dtype=np.float32)


def _tile_w3(w3, blk=128):
    return np.stack([_tile_w(w3[e], blk) for e in range(w3.shape[0])])


def _tile_down(dt):
    t = dt.reshape(2, 11, 128, 8, 128).transpose(2, 0, 3, 1, 4).reshape(128, -1)
    return np.ascontiguousarray(t, dtype=np.float32)


def _colpack(v, k):
    return np.ascontiguousarray(np.asarray(v, np.float32).reshape(k, 128).T)


def _rope_tables_l0():
    inv = 1.0 / (10000.0 ** (np.arange(0, 64, 2, dtype=np.float64) / 64))
    t = np.arange(T, dtype=np.float64)
    fr = t[:, None] * inv[None, :]
    emb = np.concatenate([fr, fr], axis=1)
    cosT = np.cos(emb).T
    sinT = np.sin(emb).T
    cos2 = np.ascontiguousarray(np.vstack([cosT, cosT]).astype(BF))
    sin2 = np.ascontiguousarray(np.vstack([sinT, sinT]).astype(BF))
    Pm = np.zeros((64, 64), np.float32)
    for j in range(32):
        Pm[j, 2 * j + 1] = -1.0
    for j in range(32, 64):
        Pm[j, 2 * (j - 32)] = 1.0
    blk = np.zeros((128, 128), np.float32)
    blk[0:64, 0:64] = Pm
    blk[64:128, 64:128] = Pm
    perm = np.ascontiguousarray(blk.T.astype(BF))
    return cos2, sin2, perm


def _rope_tables_l1():
    inv = 1.0 / (10000.0 ** (np.arange(0, 64, 2, dtype=np.float64) / 64))
    t = np.arange(T, dtype=np.float64)
    fr = t[:, None] * inv[None, :]
    cosT = np.repeat(np.cos(fr).T, 2, axis=0)
    sinT = np.repeat(np.sin(fr).T, 2, axis=0)
    cos2 = np.ascontiguousarray(np.vstack([cosT, cosT]).astype(BF))
    sin2 = np.ascontiguousarray(np.vstack([sinT, sinT]).astype(BF))
    Pm = np.zeros((64, 64), np.float32)
    for i in range(32):
        Pm[2 * i, 2 * i + 1] = -1.0
        Pm[2 * i + 1, 2 * i] = 1.0
    blk = np.zeros((128, 128), np.float32)
    blk[0:64, 0:64] = Pm
    blk[64:128, 64:128] = Pm
    perm = np.ascontiguousarray(blk.T.astype(BF))
    return cos2, sin2, perm


def _prep_arrays(inp):
    a = {}
    e16 = np.zeros((16, 2048), np.float32)
    for n in range(16):
        e16[n, n * 128:(n + 1) * 128] = 1.0
    a["c_e16"] = e16.astype(BF)
    a["c_id128"] = np.eye(128, dtype=np.float32)

    c0, s0, p0 = _rope_tables_l0()
    a["a0a_wq"] = _tile_w(np.asarray(inp["L0_attn_a_Wq"]).T)
    a["a0a_wk"] = _tile_w(np.asarray(inp["L0_attn_a_Wk"]).T)
    a["a0a_wv"] = _tile_w(np.asarray(inp["L0_attn_a_Wv"]).T)
    a["a0a_wo"] = _tile_w(np.asarray(inp["L0_attn_a_Wo"]).T)
    a["a0a_cos"], a["a0a_sin"], a["a0a_perm"] = c0, s0, p0
    wqkv = np.asarray(inp["L0_attn_b_w_qkv"])
    a["a0b_wq"] = _tile_w(np.ascontiguousarray(wqkv[0:1024].T))
    a["a0b_wk"] = _tile_w(np.ascontiguousarray(wqkv[1024:2048].T))
    a["a0b_wv"] = _tile_w(np.ascontiguousarray(wqkv[2048:3072].T))
    a["a0b_wo"] = _tile_w(np.asarray(inp["L0_attn_b_w_out"]).T)
    a["a0b_bo"] = _colpack(inp["L0_attn_b_b_out"], 8)
    a["moe_sg"] = _tile_w(np.asarray(inp["L0_ffn_sg"]).T)
    a["moe_su"] = _tile_w(np.asarray(inp["L0_ffn_su"]).T)
    a["moe_sd"] = _tile_down(np.ascontiguousarray(np.asarray(inp["L0_ffn_sd"]).T))
    a["moe_rg"] = _tile_w3(np.asarray(inp["L0_ffn_rg"]).transpose(0, 2, 1))
    a["moe_ru"] = _tile_w3(np.asarray(inp["L0_ffn_ru"]).transpose(0, 2, 1))
    a["moe_rd"] = np.stack([_tile_down(np.ascontiguousarray(np.asarray(inp["L0_ffn_rd"])[e].T)) for e in range(7)])
    a["moe_wr"] = _tile_w(np.asarray(inp["L0_ffn_wr"]).T, blk=7)
    a["moe_rb"] = _ct(np.asarray(inp["L0_ffn_rb"]).reshape(7, 1))
    c1, s1, p1 = _rope_tables_l1()
    a["a1a_wq"] = _tile_w(np.asarray(inp["L1_attn_a_Wq"]).T)
    a["a1a_wk"] = _tile_w(np.asarray(inp["L1_attn_a_Wk"]).T)
    a["a1a_wv"] = _tile_w(np.asarray(inp["L1_attn_a_Wv"]).T)
    a["a1a_wo"] = _tile_w(np.asarray(inp["L1_attn_a_Wo"]).T)
    a["a1a_cos"], a["a1a_sin"], a["a1a_perm"] = c1, s1, p1
    a["lff_wr"] = _tile_w(np.asarray(inp["L1_ffn_W_r"]).T)
    a["lff_wk"] = _tile_w(np.asarray(inp["L1_ffn_W_k"]).T)
    a["lff_wv"] = _tile_w(np.asarray(inp["L1_ffn_W_v"]).T)
    cols = np.zeros((128, 16), np.float32)
    cols[:, 0:8] = _colpack(inp["L1_ffn_mu_r"], 8)
    cols[:, 8:16] = _colpack(inp["L1_ffn_mu_k"], 8)
    a["lff_cols"] = cols
    a["mb_ipw"] = _tile_w(np.asarray(inp["L1_attn_b_in_proj_w"]).T)
    a["mb_xpw"] = _tile_w(np.asarray(inp["L1_attn_b_x_proj_w"]).T, blk=96)
    a["mb_dpw"] = _ct(np.asarray(inp["L1_attn_b_dt_proj_w"]).T)
    a["mb_opw"] = _tile_w(np.asarray(inp["L1_attn_b_out_proj_w"]).T)
    a["mb_f1w"] = _tile_w(np.asarray(inp["L1_attn_b_ffn_w1"]).T)
    a["mb_f2w"] = _tile_w(np.asarray(inp["L1_attn_b_ffn_w2"]).T)
    alog = np.asarray(inp["L1_attn_b_A_log"], np.float32)
    a["mb_alog"] = _ct(alog.reshape(16, 128, 16).transpose(1, 0, 2).reshape(128, 256))
    mc = np.zeros((128, 192), np.float32)
    cw = np.asarray(inp["L1_attn_b_conv_w"], np.float32)
    mc[:, 0:64] = cw.reshape(16, 128, 4).transpose(1, 0, 2).reshape(128, 64)
    mc[:, 64:80] = _colpack(inp["L1_attn_b_D"], 16)
    mc[:, 80:96] = _colpack(inp["L1_attn_b_dt_proj_b"], 16)
    mc[:, 96:104] = _colpack(inp["L1_attn_b_ln1_w"], 8)
    mc[:, 104:112] = _colpack(inp["L1_attn_b_ln1_b"], 8)
    mc[:, 112:120] = _colpack(inp["L1_attn_b_ln2_w"], 8)
    mc[:, 120:128] = _colpack(inp["L1_attn_b_ln2_b"], 8)
    mc[:, 128:144] = _colpack(inp["L1_attn_b_conv_b"], 16)
    mc[:, 144:160] = _colpack(inp["L1_attn_b_ffn_b1"], 16)
    mc[:, 160:168] = _colpack(inp["L1_attn_b_ffn_b2"], 8)
    a["mb_cols"] = mc
    return a


PH_A0A = dict(kind="attn", tag="a0a", rope=True, scale=0.125)
PH_MOE = dict(kind="moe", tag="moe")
PH_A0B = dict(kind="attn", tag="a0b", rope=False, scale=1.0 / 32.0, bias=True)
PH_A1A = dict(kind="attn", tag="a1a", rope=True, scale=0.125)
PH_LFF = dict(kind="l1ffn", tag="lff")
PH_MB = dict(kind="mamba", tag="mb")
ALL_PHASES = [PH_A0A, PH_MOE, PH_A0B, PH_MOE, PH_A1A, PH_LFF, PH_MB, PH_LFF]

FUSED = True


def kernel(**inputs):
    arrays = _prep_arrays(inputs)
    x = np.asarray(inputs["L0_attn_a_x"], np.float32)
    ncore = 8
    x_cur = [np.ascontiguousarray(x[i].T) for i in range(ncore)]

    if FUSED:
        groups = [ALL_PHASES]
    else:
        groups = [[ph] for ph in ALL_PHASES]

    prog_cache = {}
    for grp in groups:
        key = tuple(id(ph) for ph in grp)
        if key not in prog_cache:
            prog_cache[key] = build_program(grp, arrays)
        nc, used = prog_cache[key]
        in_maps = []
        for i in range(ncore):
            m = {n: arrays[n] for n in used if n != "x_in"}
            m["x_in"] = x_cur[i]
            in_maps.append(m)
        res = run_bass_kernel_spmd(nc, in_maps, core_ids=list(range(ncore)))
        x_cur = [np.asarray(res.results[i]["x_out"]) for i in range(ncore)]

    out = np.stack([xc.T for xc in x_cur]).astype(np.float32)
    return np.ascontiguousarray(out)
